# version 08
# baseline (speedup 1.0000x reference)
import numpy as np
import ml_dtypes
import concourse.bass as bass
import concourse.mybir as mybir
from concourse.bass_utils import run_bass_kernel_spmd

F32 = mybir.dt.float32
BF16 = mybir.dt.bfloat16
U8 = mybir.dt.uint8
I16 = mybir.dt.int16
U32 = mybir.dt.uint32
from concourse import library_config
AF = mybir.ActivationFunctionType
ALU = mybir.AluOpType
AX = mybir.AxisListType
BF = ml_dtypes.bfloat16

D = 1024
T = 2048
SIM_SERIAL = False



def _host_constants():
    c = {}
    p = np.arange(128)
    seld = np.zeros((128, 512), np.float32)
    for v in range(4):
        seld[32 * v + (p % 32), 128 * v + p] = 1.0
    c["c_seld"] = seld.astype(BF)
    sdy = np.zeros((128, 32), np.float32)
    sdy[p, p % 32] = 1.0
    c["c_seldy"] = sdy.astype(BF)
    selb = np.zeros((32, 1024), np.float32)
    for nb in range(8):
        selb[4 * nb + p // 32, 128 * nb + p] = 1.0
    c["c_selb"] = selb.astype(BF)
    c["c_onec"] = np.ones((128, 1), np.float32).astype(BF)
    c["c_oner"] = np.ones((1, 128), np.float32).astype(BF)
    c["c_idn"] = np.eye(128, dtype=np.float32).astype(BF)
    c["c_i64"] = np.eye(64, dtype=np.float32).astype(BF)
    ohc = np.zeros((16, 16 * 128), np.float32)
    for r in range(16):
        ohc[r, 128 * r:128 * (r + 1)] = 1.0
    c["c_ohc"] = ohc.astype(BF)
    hsA = np.zeros((128, 8 * 16), np.float32)
    for j in range(8):
        for k in range(128):
            hsA[k, 16 * j + 2 * j + k // 64] = 1.0
    c["c_hselA"] = hsA.astype(BF)
    hsB = np.zeros((16, 8 * 128), np.float32)
    for j in range(8):
        for q in range(128):
            hsB[2 * j + q // 64, 128 * j + q] = 1.0
    c["c_hselB"] = hsB.astype(BF)
    tk = np.arange(128)[:, None]
    tq = np.arange(512)[None, :]
    gm = np.concatenate([np.where(tq >= tk + d, 0.0, -1e30).astype(np.float32)
                         for d in (0, 128, 256, 384)], axis=1)
    c["c_gmask"] = gm.astype(BF)
    deltas = (-256, -128, 0, 128, 256, 384)
    lm = np.concatenate([np.where((tq - tk - d >= 0) & (tq - tk - d < 256),
                                  0.0, -1e30).astype(np.float32)
                         for d in deltas], axis=1)
    c["c_lmask"] = lm.astype(BF)
    p = np.arange(128)
    i_p = (p % 64) // 2
    for base, tag in ((1e6, "g"), (1e4, "l")):
        inv = 1.0 / base ** (np.arange(0, 64, 2) / 64.0)
        fr = np.arange(T)[:, None] * inv[None, :]
        cosv = np.cos(fr).astype(np.float32)
        sinv = np.sin(fr).astype(np.float32)
        ct = cosv[:, i_p].T.copy()
        stt = sinv[:, i_p].T.copy()
        sgn = np.where(p % 2 == 0, -1.0, 1.0)[:, None]
        c[f"c_cos{tag}"] = ct.astype(BF)
        c[f"c_sin{tag}"] = (stt * sgn).astype(BF)
    return c


_CONSTS = _host_constants()


def prep_l0attn(i, b):
    d = {}
    d["xt"] = np.ascontiguousarray(i["L0_attn_x"][b].T).astype(np.float32)
    d["dtwT"] = np.ascontiguousarray(i["L0_attn_dt_w"].T)
    d["BwT"] = np.ascontiguousarray(i["L0_attn_B_w"].T)
    d["CwT"] = np.ascontiguousarray(i["L0_attn_C_w"].T)
    d["dtb"] = np.ascontiguousarray(i["L0_attn_dt_b"].reshape(8, 128).T)
    d["Bb"] = np.ascontiguousarray(i["L0_attn_B_b"].reshape(32, 1))
    d["Cb"] = np.ascontiguousarray(i["L0_attn_C_b"].reshape(32, 1))
    d["Dsk"] = np.ascontiguousarray(i["L0_attn_D_skip"].reshape(8, 128).T)
    logA = i["L0_attn_log_A"]
    pp = np.arange(128)
    acc = np.empty((128, 256), np.float32)
    for g in range(32):
        for nb in range(8):
            acc[:, 8 * g + nb] = logA[32 * g + (pp % 32), 4 * nb + pp // 32]
    d["logAcol"] = acc
    d.update(_CONSTS)
    return d


def prep_l0ffn(i, b):
    d = {}
    d["wblk"] = np.ascontiguousarray(i["L0_ffn_w_block"])
    d["wr8"] = np.ascontiguousarray(i["L0_ffn_w_router"].reshape(8, 128).T)
    d.update(_CONSTS)
    return d


def prep_l1attn(i, b):
    d = {}
    d["n1w"] = np.ascontiguousarray(i["L1_attn_norm1_w"].reshape(8, 128).T)
    d["n2w"] = np.ascontiguousarray(i["L1_attn_norm2_w"].reshape(8, 128).T)
    d["gwq"] = np.ascontiguousarray(i["L1_attn_g_wq"])
    d["gkvd"] = np.ascontiguousarray(i["L1_attn_g_kv_down"])
    d["gkup"] = np.ascontiguousarray(i["L1_attn_g_k_up"])
    d["gvup"] = np.ascontiguousarray(i["L1_attn_g_v_up"])
    d["gwo"] = np.ascontiguousarray(i["L1_attn_g_wo"])
    d["lwq"] = np.ascontiguousarray(i["L1_attn_l_wq"])
    d["lwk"] = np.ascontiguousarray(i["L1_attn_l_wk"])
    d["lwv"] = np.ascontiguousarray(i["L1_attn_l_wv"])
    d["lwo"] = np.ascontiguousarray(i["L1_attn_l_wo"])
    d.update(_CONSTS)
    d.pop("l_deltas", None)
    return d


def prep_l1ffn(i, b):
    d = {}
    d["rwT"] = np.ascontiguousarray(i["L1_ffn_router_w"])
    d["rbc"] = np.ascontiguousarray(i["L1_ffn_router_b"].reshape(64, 1))
    d["w1"] = np.ascontiguousarray(i["L1_ffn_w1"])
    d["w2"] = np.ascontiguousarray(i["L1_ffn_w2"])
    b1 = i["L1_ffn_b1"]
    b1t = np.empty((128, 128), np.float32)
    for m in range(2):
        for e in range(64):
            b1t[:, 64 * m + e] = b1[e, 128 * m:128 * (m + 1)]
    d["b1t"] = b1t
    d["b2c"] = np.ascontiguousarray(i["L1_ffn_b2"])
    d["sw1"] = np.ascontiguousarray(i["L1_ffn_sw1"])
    d["sw2"] = np.ascontiguousarray(i["L1_ffn_sw2"])
    sb1 = i["L1_ffn_sb1"]
    sb1t = np.empty((2, 128, 16), np.float32)
    for s in range(2):
        for pe in range(8):
            for m in range(2):
                sb1t[s, :, 2 * pe + m] = sb1[s, 256 * pe + 128 * m:
                                             256 * pe + 128 * (m + 1)]
    d["sb1t0"] = np.ascontiguousarray(sb1t[0])
    d["sb1t1"] = np.ascontiguousarray(sb1t[1])
    d["idnf"] = np.eye(128, dtype=np.float32)
    d["sb2c"] = np.ascontiguousarray(i["L1_ffn_sb2"])
    d["iotam"] = np.ascontiguousarray(np.broadcast_to(
        (np.arange(2048, dtype=np.int32) - 2048).astype(np.int16), (64, 2048)))
    d["iotan"] = np.ascontiguousarray(np.broadcast_to(
        np.arange(384, dtype=np.int16), (64, 384)))
    d.update(_CONSTS)
    return d




class Ctx:
    def __init__(self, nc):
        self.nc = nc
        self.sem = {}
        self.cnt = {}
        self._cm = []
        for name in ("dma", "dw", "dmo", "t", "v", "s", "g", "gt", "gu", "gv", "gw", "sc", "sd"):
            cm = nc.semaphore(name)
            self.sem[name] = cm.__enter__()
            self._cm.append(cm)
            self.cnt[name] = 0
        self.sb_off = 18048
        self.sb_limit = 228992
        self.marks = []
        self.offsets = {}
        self.uid = 0

    def stamp(self, inst, key, inc=1):
        inst.then_inc(self.sem[key], inc)
        self.cnt[key] += inc
        return self.cnt[key]

    def dma(self, q, key, dst, src):
        if self.cnt[key] > 0:
            self.wait(q, key, self.cnt[key])
        return self.stamp(q.dma_start(out=dst, in_=src), key, 16)

    def wait(self, eng, key, val=None):
        if val is None:
            val = self.cnt[key]
        eng.wait_ge(self.sem[key], val)

    def alloc(self, name, shape, dtype):
        bytesz = {F32: 4, BF16: 2, U8: 1, I16: 2, U32: 4}[dtype]
        fb = int(np.prod(shape[1:])) * bytesz
        fb = (fb + 31) // 32 * 32
        off = self.sb_off
        assert off + fb <= self.sb_limit, f"SBUF overflow {name}: {off}+{fb}>{self.sb_limit}"
        self.sb_off = off + fb
        self.uid += 1
        self.offsets[name] = off
        return self.nc.alloc_sbuf_tensor_at(f"{name}_{self.uid}", list(shape), dtype,
                                            offset=off).ap()

    def alias(self, name, shape, dtype, at_name):
        self.uid += 1
        return self.nc.alloc_sbuf_tensor_at(f"{name}_{self.uid}", list(shape), dtype,
                                            offset=self.offsets[at_name]).ap()

    def register_consts(self, block, values):
        cpool = self.alloc("cpool", [128, len(values)], F32)
        ctx = self

        @block.vector
        def _(v):
            for i, val in enumerate(values):
                sl = cpool[:, i:i + 1]
                ctx.stamp(v.memset(sl, float(val)), "v")
                ctx.nc.const_aps.register(F32, float(val), sl)

    def mark(self):
        self.marks.append(self.sb_off)

    def release(self):
        self.sb_off = self.marks.pop()


class _Rec:
    __slots__ = ("name", "args", "kwargs", "incs")

    def __init__(self, name, args, kwargs):
        self.name = name
        self.args = args
        self.kwargs = kwargs
        self.incs = []

    def then_inc(self, sem, n):
        self.incs.append((sem, n))
        return self


class _EngProxy:
    def __init__(self, stream):
        self._stream = stream

    def __getattr__(self, name):
        stream = self._stream

        def call(*a, **kw):
            r = _Rec(name, a, kw)
            stream.append(r)
            return r
        return call


class DeferBlock:

    ENGINES = ("tensor", "vector", "scalar", "sync", "gpsimd")

    def __init__(self):
        self.streams = {k: [] for k in self.ENGINES}

    def _dec(self, key):
        def deco(f):
            f(_EngProxy(self.streams[key]))
        return deco

    @property
    def tensor(self):
        return self._dec("tensor")

    @property
    def vector(self):
        return self._dec("vector")

    @property
    def scalar(self):
        return self._dec("scalar")

    @property
    def sync(self):
        return self._dec("sync")

    @property
    def gpsimd(self):
        return self._dec("gpsimd")

    def replay(self, nc, chunk=1200):
        with nc.Block() as block:
            for key in self.ENGINES:
                stream = self.streams[key]
                for lo in range(0, max(len(stream), 1), chunk):
                    seg = stream[lo:lo + chunk]

                    def emit(eng, seg=seg):
                        for r in seg:
                            inst = getattr(eng, r.name)(*r.args, **r.kwargs)
                            for sem, n in r.incs:
                                inst.then_inc(sem, n)
                    getattr(block, key)(emit)


def c512(ap, c):
    return ap[:, 512 * c:512 * (c + 1)]




def emit_ln(ctx, block, zfull, A, P, Y, coef, out_dram):
    ctx.mark()
    zb = [ctx.alloc(f"zb{i}", [128, T], BF16) for i in range(8)]
    z2 = [ctx.alloc(f"z2{i}", [128, T], BF16) for i in range(2)]
    onec = ctx.alloc("ln_onec", [128, 1], BF16)
    oner = ctx.alloc("ln_oner", [1, 128], BF16)
    mrow = ctx.alloc("mrow", [1, T], F32)
    vrow = ctx.alloc("vrow", [1, T], F32)
    msq = ctx.alloc("msq", [1, T], F32)
    rrow = ctx.alloc("rrow", [1, T], F32)
    rrow_b = ctx.alloc("rrow_b", [1, T], BF16)
    mrs_b = ctx.alloc("mrs_b", [1, T], BF16)

    @block.vector
    def _(v):
        ctx.stamp(v.memset(onec, 1.0), "v")
        ctx.stamp(v.memset(oner, 1.0), "v")

    S1 = Y[0:1, :]
    S2 = Y[64:65, :]

    for blk in range(8):
        @block.vector
        def _(v, blk=blk):
            ctx.wait(v, "v")
            ctx.wait(v, "t")
            if A is not None:
                ctx.stamp(v.scalar_tensor_tensor(
                    zfull[blk], zfull[blk], float(coef), A[blk],
                    ALU.mult, ALU.add), "v")
                ctx.wait(v, "v")
            elif coef != 1.0:
                ctx.stamp(v.tensor_scalar_mul(zfull[blk], zfull[blk], float(coef)), "v")
                ctx.wait(v, "v")
            ctx.stamp(v.tensor_copy(zb[blk], zfull[blk]), "v")
            ctx.wait(v, "v")
            ctx.stamp(v.tensor_tensor(z2[blk % 2], zb[blk], zb[blk], ALU.mult), "v")

        @block.tensor
        def _(t, blk=blk):
            ctx.wait(t, "v")
            for c in range(4):
                ctx.stamp(t.matmul(c512(S1, c), onec, c512(zb[blk], c),
                                   start=(blk == 0), stop=(blk == 7)), "t")
                ctx.stamp(t.matmul(c512(S2, c), onec, c512(z2[blk % 2], c),
                                   start=(blk == 0), stop=(blk == 7)), "t")

    @block.vector
    def _(v):
        ctx.wait(v, "t")
        ctx.stamp(v.tensor_scalar_mul(mrow, S1, 1.0 / D), "v")
        ctx.stamp(v.tensor_scalar_mul(vrow, S2, 1.0 / D), "v")
        ctx.wait(v, "v")
        ctx.stamp(v.tensor_tensor(msq, mrow, mrow, ALU.mult), "v")
        ctx.wait(v, "v")
        ctx.stamp(v.tensor_tensor(vrow, vrow, msq, ALU.subtract), "v")

    @block.scalar
    def _(s):
        ctx.wait(s, "v")
        ctx.stamp(s.activation(rrow, vrow, AF.Ln, bias=1e-5), "s")
        ctx.wait(s, "s")
        ctx.stamp(s.activation(rrow, rrow, AF.Exp, scale=-0.5), "s")

    @block.vector
    def _(v):
        ctx.wait(v, "s")
        ctx.stamp(v.tensor_copy(rrow_b, rrow), "v")
        ctx.stamp(v.tensor_tensor(msq, mrow, rrow, ALU.mult), "v")
        ctx.wait(v, "v")
        ctx.stamp(v.tensor_copy(mrs_b, msq), "v")

    for c in range(4):
        @block.tensor
        def _(t, c=c):
            ctx.wait(t, "v")
            ctx.stamp(t.matmul(P[:, 0:512], oner, rrow_b[0:1, 512 * c:512 * (c + 1)],
                               start=True, stop=True), "t")
            ctx.stamp(t.matmul(P[:, 512:1024], oner, mrs_b[0:1, 512 * c:512 * (c + 1)],
                               start=True, stop=True), "t")
        for blk in range(8):
            @block.vector
            def _(v, c=c, blk=blk):
                ctx.wait(v, "t")
                sl = slice(512 * c, 512 * (c + 1))
                ctx.stamp(v.tensor_tensor(zfull[blk][:, sl], zfull[blk][:, sl],
                                          P[:, 0:512], ALU.mult), "v")
                ctx.wait(v, "v")
                ctx.stamp(v.tensor_tensor(zfull[blk][:, sl], zfull[blk][:, sl],
                                          P[:, 512:1024], ALU.subtract), "v")

    if out_dram is not None:
        @block.gpsimd
        def _(g):
            ctx.wait(g, "v")
            for blk in range(8):
                ctx.dma(g, "dmo", out_dram[128 * blk:128 * (blk + 1), :], zfull[blk])
    ctx.release()




def emit_l0_attn_core(ctx, block, io, A, P, Y):
    nc = ctx.nc
    ctx.mark()
    dtT = [ctx.alloc(f"dtT{i}", [128, T], BF16) for i in range(8)]
    dtxT = [ctx.alloc(f"dtxT{i}", [128, T], BF16) for i in range(8)]
    bT = ctx.alloc("bT", [32, T], BF16)
    cT = ctx.alloc("cT", [32, T], BF16)
    nacol = ctx.alloc("nacol", [128, 256], F32)
    dsk_sb = ctx.alloc("dsk_sb", [128, 8], F32)
    seld = ctx.alloc("seld", [128, 512], BF16)
    seldy = ctx.alloc("seldy", [128, 32], BF16)
    selb = ctx.alloc("selb", [32, 1024], BF16)

    ctx.mark()
    xTb = [ctx.alloc(f"xTb{i}", [128, T], BF16) for i in range(8)]
    wbufb = [ctx.alloc(f"dtwb{i}", [128, D], BF16) for i in range(8)]
    bcwb = ctx.alloc("bcwb", [128, 512], BF16)
    sp = [[ctx.alloc(f"sp{i}_{p}", [128, 512], F32) for i in range(2)]
          for p in range(2)]
    dtb_sb = ctx.alloc("dtb_sb", [128, 8], F32)
    bb_sb = ctx.alloc("bb_sb", [32, 1], F32)
    cb_sb = ctx.alloc("cb_sb", [32, 1], F32)
    acol = ctx.alloc("acol", [128, 256], F32)

    @block.sync
    def _(sync):
        for nm, dst in (("dtb", dtb_sb), ("Bb", bb_sb), ("Cb", cb_sb),
                        ("Dsk", dsk_sb), ("logAcol", acol),
                        ("c_seld", seld), ("c_seldy", seldy), ("c_selb", selb)):
            ctx.dma(sync, "dma", dst, io[nm])
    ld0 = ctx.cnt["dma"]

    @block.scalar
    def _(s):
        ctx.wait(s, "dma", ld0)
        ctx.stamp(s.activation(nacol, acol, AF.Exp), "s")

    @block.vector
    def _(v):
        ctx.wait(v, "s")
        ctx.stamp(v.tensor_scalar_mul(nacol, nacol, -1.0), "v")

    @block.gpsimd
    def _(g):
        for i in range(8):
            ctx.dma(g, "dw", xTb[i], io["xt"][128 * i:128 * (i + 1), :])
        ctx.dma(g, "dw", bcwb[:, 0:256].rearrange("p (k n) -> p k n", n=32),
                io["BwT"].rearrange("(k p) n -> p k n", p=128))
        ctx.dma(g, "dw", bcwb[:, 256:512].rearrange("p (k n) -> p k n", n=32),
                io["CwT"].rearrange("(k p) n -> p k n", p=128))
        for k in range(8):
            ctx.dma(g, "dw", wbufb[k], io["dtwT"][128 * k:128 * (k + 1), :])
    ldw = ctx.cnt["dw"]

    pviews = [c512(P, c) for c in range(4)] + [c512(Y, c) for c in range(4)]
    pfree = []
    dt_add = []
    rot = 0
    for ec in range(8):
        for tc in range(4):
            pv = pviews[rot % 8]

            @block.tensor
            def _(t, ec=ec, tc=tc, pv=pv, rot=rot):
                ctx.wait(t, "dw", ldw)
                ctx.wait(t, "v")
                if rot >= 8:
                    ctx.wait(t, "s", pfree[rot - 8])
                for k in range(8):
                    ctx.stamp(t.matmul(pv, wbufb[k][:, 128 * ec:128 * (ec + 1)],
                                       c512(xTb[k], tc),
                                       start=(k == 0), stop=(k == 7)), "t")

            spp = sp[rot % 2]

            @block.scalar
            def _(s, ec=ec, pv=pv, spp=spp, dtv=(dt_add[rot - 2] if rot >= 2 else 0)):
                ctx.wait(s, "t")
                if dtv:
                    ctx.wait(s, "v", dtv)
                bias = dtb_sb[:, ec:ec + 1]
                ctx.stamp(s.activation(spp[0], pv, AF.Relu, bias=bias), "s")
                pfree.append(ctx.stamp(s.activation(spp[1], pv, AF.Abs, bias=bias), "s"))
                ctx.wait(s, "s")
                ctx.stamp(s.activation(spp[1], spp[1], AF.Exp, scale=-1.0), "s")
                ctx.wait(s, "s")
                ctx.stamp(s.activation(spp[1], spp[1], AF.Ln, bias=1.0), "s")

            @block.vector
            def _(v, ec=ec, tc=tc, spp=spp):
                ctx.wait(v, "s")
                ctx.stamp(v.tensor_tensor(c512(dtT[ec], tc), spp[0], spp[1],
                                          ALU.add), "v")
            dt_add.append(ctx.cnt["v"])
            rot += 1

    for wcol0, bias_sb, dst in ((0, bb_sb, bT), (256, cb_sb, cT)):
        for tc in range(4):
            pv = pviews[rot % 8][0:32, :]

            @block.tensor
            def _(t, wcol0=wcol0, tc=tc, pv=pv, rot=rot):
                if rot >= 8:
                    ctx.wait(t, "s", pfree[rot - 8])
                for k in range(8):
                    ctx.stamp(t.matmul(pv, bcwb[:, wcol0 + 32 * k:wcol0 + 32 * (k + 1)],
                                       c512(xTb[k], tc),
                                       start=(k == 0), stop=(k == 7)), "t")

            @block.scalar
            def _(s, tc=tc, pv=pv, bias_sb=bias_sb, dst=dst):
                ctx.wait(s, "t")
                pfree.append(ctx.stamp(
                    s.activation(c512(dst, tc), pv, AF.Identity, bias=bias_sb), "s"))
            rot += 1

    for i in range(8):
        @block.vector
        def _(v, i=i):
            ctx.wait(v, "s")
            ctx.wait(v, "v")
            ctx.stamp(v.tensor_tensor(dtxT[i], dtT[i], xTb[i], ALU.mult), "v")

    ctx.release()
    ctx.mark()
    breps = [ctx.alloc(f"brep{i}", [128, T], BF16) for i in range(8)]
    creps = [ctx.alloc(f"crep{i}", [128, T], BF16) for i in range(8)]
    abar = [ctx.alloc(f"abar{i}", [128, T], BF16) for i in range(2)]
    ubuf = ctx.alloc("ubuf", [128, T], BF16)
    hbuf = ctx.alloc("hbuf", [128, T], BF16)
    ytbuf2 = [ctx.alloc(f"ytbuf{i}", [128, T], BF16) for i in range(2)]
    xflush = ctx.alloc("xflush", [128, T], F32)

    @block.tensor
    def _(t):
        ctx.wait(t, "s")
        ctx.wait(t, "v")
    for nb in range(8):
        for src_, dst in ((bT, breps), (cT, creps)):
            @block.tensor
            def _(t, nb=nb, src_=src_):
                ctx.wait(t, "v")
                for c in range(4):
                    ctx.stamp(t.matmul(c512(P, c), selb[:, 128 * nb:128 * (nb + 1)],
                                       c512(src_, c), start=True, stop=True), "t")

            @block.vector
            def _(v, nb=nb, dst=dst):
                ctx.wait(v, "t")
                for c in range(4):
                    ctx.stamp(v.tensor_copy(c512(dst[nb], c), c512(P, c)), "v")

    exp_done = {0: ctx.cnt["s"], 1: ctx.cnt["s"]}
    u_done = {0: ctx.cnt["v"], 1: ctx.cnt["v"]}
    scan_mark = [ctx.cnt["v"]]
    pending = [None]
    flush_mark = [ctx.cnt["v"]]

    ymm_guard = {0: 0, 1: 0}

    def emit_pending_ymm():
        if pending[0] is None:
            return
        g4p, nbp, ypar, vmark, fm = pending[0]
        pending[0] = None

        @block.tensor
        def _(t, g4=g4p, nb=nbp, ypar=ypar, vmark=vmark, fm=fm):
            ctx.wait(t, "g", vmark)
            if nb == 0:
                ctx.wait(t, "v", fm)
            tp = (0, 96) if g4 == 3 else None
            for c in range(4):
                ctx.stamp(t.matmul(
                    Y[32 * g4:32 * (g4 + 1), 512 * c:512 * (c + 1)],
                    seldy, c512(ytbuf2[ypar], c),
                    start=(nb == 0), stop=(nb == 7), tile_position=tp), "t")
        ymm_guard[ypar] = ctx.cnt["t"]
    tseq = 0
    for gb in range(8):
        for g4 in range(4):
            for nb in range(8):
                par = (g4 * 8 + nb) % 2
                ypar = tseq % 2
                tile_idx = 8 * (4 * gb + g4) + nb
                for th in range(2):
                    PA = P[:, 0:1024] if th == 0 else P[:, 1024:2048]
                    PB = P[:, 1024:2048] if th == 0 else P[:, 0:1024]

                    @block.tensor
                    def _(t, gb=gb, g4=g4, th=th, PA=PA, PB=PB,
                          ed=exp_done[th], ud=u_done[th],
                          ed2=exp_done[1 - th], ud2=u_done[1 - th]):
                        ctx.wait(t, "s", ed)
                        ctx.wait(t, "v", ud2)
                        for c in range(2):
                            ctx.stamp(t.matmul(
                                c512(PA, c), seld[:, 128 * g4:128 * (g4 + 1)],
                                dtT[gb][:, 1024 * th + 512 * c:1024 * th + 512 * (c + 1)],
                                start=True, stop=True), "t")
                        ctx.wait(t, "s", ed2)
                        ctx.wait(t, "v", ud)
                        for c in range(2):
                            ctx.stamp(t.matmul(
                                c512(PB, c), seld[:, 128 * g4:128 * (g4 + 1)],
                                dtxT[gb][:, 1024 * th + 512 * c:1024 * th + 512 * (c + 1)],
                                start=True, stop=True), "t")
                    tmark = ctx.cnt["t"]

                    @block.scalar
                    def _(s, th=th, par=par, tile_idx=tile_idx, tmark=tmark, PA=PA):
                        ctx.wait(s, "t", tmark - 2)
                        for c in range(2):
                            ctx.stamp(s.activation(
                                abar[par][:, 1024 * th + 512 * c:1024 * th + 512 * (c + 1)],
                                c512(PA, c), AF.Exp,
                                scale=nacol[:, tile_idx:tile_idx + 1]), "s")
                    exp_done[th] = ctx.cnt["s"]

                    @block.vector
                    def _(v, th=th, nb=nb, tmark=tmark, PB=PB, scm=scan_mark[0]):
                        ctx.wait(v, "t", tmark)
                        if th == 0:
                            ctx.wait(v, "v", scm)
                        ctx.stamp(v.tensor_tensor(
                            ubuf[:, 1024 * th:1024 * (th + 1)], PB,
                            breps[nb][:, 1024 * th:1024 * (th + 1)],
                            ALU.mult), "v")
                    u_done[th] = ctx.cnt["v"]

                emit_pending_ymm()

                @block.vector
                def _(v, par=par, nb=nb, gg=ctx.cnt["g"]):
                    ctx.wait(v, "s", ctx.cnt["s"])
                    ctx.wait(v, "v")
                    ctx.wait(v, "g", gg)
                    ctx.stamp(v.tensor_tensor_scan(hbuf, abar[par], ubuf,
                                                   0.0, ALU.mult, ALU.add), "v")
                scan_mark[0] = ctx.cnt["v"]
                smark = ctx.cnt["v"]

                @block.gpsimd
                def _(g, nb=nb, ypar=ypar, yg=ymm_guard[tseq % 2], sm=smark):
                    ctx.wait(g, "v", sm)
                    ctx.wait(g, "t", yg)
                    ctx.stamp(g.tensor_tensor(ytbuf2[ypar], hbuf, creps[nb],
                                              ALU.mult), "g")
                vmark = ctx.cnt["g"]
                pending[0] = (g4, nb, ypar, vmark, flush_mark[0])
                tseq += 1
        emit_pending_ymm()

        @block.sync
        def _(sync, gb=gb):
            if gb > 0:
                ctx.wait(sync, "v")
            ctx.dma(sync, "dma", xflush, io["xt"][128 * gb:128 * (gb + 1), :])

        @block.vector
        def _(v, gb=gb):
            ctx.wait(v, "dma")
            ctx.wait(v, "t")
            for c in range(4):
                ctx.stamp(v.scalar_tensor_tensor(
                    c512(A[gb], c), c512(xflush, c), dsk_sb[:, gb:gb + 1],
                    c512(Y, c), ALU.mult, ALU.add), "v")
        flush_v = ctx.cnt["v"]
        flush_mark[0] = flush_v

        @block.tensor
        def _(t, fv=flush_v):
            ctx.wait(t, "v", fv)

    ctx.release()
    ctx.release()




def emit_l0_ffn(ctx, block, io, A, zfull, P, Y):
    K_TOP = 1536
    ctx.mark()
    xTb = [ctx.alloc(f"fx{i}", [128, T], BF16) for i in range(8)]
    wblkb = [ctx.alloc(f"fw{i}", [128, D], BF16) for i in range(8)]
    wr8 = ctx.alloc("wr8", [128, 8], F32)
    idnb = ctx.alloc("idnb", [128, 128], BF16)
    ohc = ctx.alloc("ohc", [16, T], BF16)
    onescb = ctx.alloc("onescb", [128, 1], BF16)
    onesrf = ctx.alloc("onesrf", [1, 128], F32)
    snat = ctx.alloc("snat", [128, 16], F32)
    ssq = ctx.alloc("ssq", [128, 8], F32)
    mT = ctx.alloc("mT", [128, 16], BF16)
    mt16 = ctx.alloc("mt16", [16, 128], BF16)
    geb = ctx.alloc("geb", [128, 16], BF16)
    gered = ctx.alloc("gered", [128, 1], BF16)
    wncol = ctx.alloc("wncol", [128, 1], BF16)
    lo = ctx.alloc("lo", [1, 1], F32)
    hi = ctx.alloc("hi", [1, 1], F32)
    mid = ctx.alloc("mid", [1, 1], F32)
    tnum = ctx.alloc("tnum", [1, 1], F32)
    c01 = ctx.alloc("c01", [1, 1], F32)
    cntf = ctx.alloc("cntf", [128, 1], F32)
    sq = ctx.alloc("fsq", [128, 512], F32)
    cub = ctx.alloc("fcub", [128, 512], F32)
    tnh = ctx.alloc("ftnh", [128, 512], F32)
    gmx = ctx.alloc("gmx", [128, 512], F32)

    @block.sync
    def _(sync):
        ctx.dma(sync, "dma", wr8, io["wr8"])
        ctx.dma(sync, "dma", idnb, io["c_idn"])
        ctx.dma(sync, "dma", ohc, io["c_ohc"])
    ldc = ctx.cnt["dma"]

    @block.gpsimd
    def _(g):
        for i in range(8):
            ctx.dma(g, "dw", xTb[i], io["xt"][128 * i:128 * (i + 1), :])
        for k in range(8):
            ctx.dma(g, "dw", wblkb[k], io["wblk"][128 * k:128 * (k + 1), :])
    ldw = ctx.cnt["dw"]

    @block.vector
    def _(v):
        ctx.stamp(v.memset(onescb, 1.0), "v")
        ctx.stamp(v.memset(onesrf, 1.0), "v")

    zready = ctx.cnt["v"]

    Pv16 = P[:, 0:16]

    @block.tensor
    def _(t):
        ctx.wait(t, "dma", ldc)
        ctx.wait(t, "v", zready)
        for c in range(16):
            for k in range(8):
                ctx.stamp(t.matmul(Pv16[:, c:c + 1],
                                   zfull[k][:, 128 * c:128 * (c + 1)],
                                   wr8[:, k:k + 1],
                                   start=(k == 0), stop=(k == 7)), "t")

    @block.vector
    def _(v):
        ctx.wait(v, "t")
        ctx.stamp(v.tensor_copy(snat, Pv16), "v")
        ctx.stamp(v.tensor_tensor(ssq, wr8, wr8, ALU.mult), "v")
        ctx.wait(v, "v")
        ctx.stamp(v.reduce_sum(cntf, ssq, axis=AX.X), "v")
        ctx.wait(v, "v")
        ctx.stamp(v.tensor_copy(wncol, cntf), "v")

    @block.tensor
    def _(t):
        ctx.wait(t, "v")
        ctx.stamp(t.matmul(Y[0:1, 0:1], wncol, onescb, start=True, stop=True), "t")

    @block.scalar
    def _(s):
        ctx.wait(s, "t")
        ctx.stamp(s.activation(hi, Y[0:1, 0:1], AF.Sqrt, scale=1024.0), "s")

    @block.vector
    def _(v):
        ctx.wait(v, "s")
        ctx.stamp(v.tensor_scalar_add(hi, hi, 1.0), "v")
        ctx.wait(v, "v")
        ctx.stamp(v.tensor_scalar_mul(lo, hi, -1.0), "v")


    for it in range(34):
        @block.tensor
        def _(t, it=it):
            ctx.wait(t, "v")
        @block.vector
        def _(v, it=it):
            ctx.wait(v, "v")
            ctx.wait(v, "t")
            ctx.stamp(v.tensor_tensor(tnum, lo, hi, ALU.add), "v")
            ctx.wait(v, "v")
            ctx.stamp(v.tensor_scalar_mul(mid, tnum, 0.5), "v")

        @block.tensor
        def _(t, it=it):
            ctx.wait(t, "v")
            ctx.stamp(t.matmul(Y[0:128, 16:17], onesrf, mid, start=True, stop=True), "t")

        @block.vector
        def _(v, it=it):
            ctx.wait(v, "t")
            ctx.stamp(v.tensor_scalar(geb, snat, Y[0:128, 16:17], None, ALU.is_ge), "v")
            ctx.wait(v, "v")
            ctx.stamp(v.reduce_sum(cntf, geb, axis=AX.X), "v")
            ctx.wait(v, "v")
            ctx.stamp(v.tensor_copy(gered, cntf), "v")

        @block.tensor
        def _(t, it=it):
            ctx.wait(t, "v")
            ctx.stamp(t.matmul(Y[0:1, 17:18], gered, onescb, start=True, stop=True), "t")

        @block.vector
        def _(v, it=it):
            ctx.wait(v, "t")
            ctx.stamp(v.tensor_scalar(c01, Y[0:1, 17:18], float(K_TOP) - 0.5, None,
                                      ALU.is_ge), "v")
            ctx.wait(v, "v")
            ctx.stamp(v.tensor_tensor(tnum, mid, lo, ALU.subtract), "v")
            ctx.wait(v, "v")
            ctx.stamp(v.tensor_tensor(tnum, c01, tnum, ALU.mult), "v")
            ctx.wait(v, "v")
            ctx.stamp(v.tensor_tensor(lo, lo, tnum, ALU.add), "v")
            ctx.wait(v, "v")
            ctx.stamp(v.tensor_tensor(tnum, mid, hi, ALU.subtract), "v")
            ctx.wait(v, "v")
            ctx.stamp(v.tensor_tensor(c01, c01, tnum, ALU.mult), "v")
            ctx.wait(v, "v")
            ctx.stamp(v.tensor_tensor(tnum, tnum, c01, ALU.subtract), "v")
            ctx.wait(v, "v")
            ctx.stamp(v.tensor_tensor(hi, hi, tnum, ALU.add), "v")

    @block.tensor
    def _(t):
        ctx.wait(t, "v")
        ctx.stamp(t.matmul(Y[0:128, 16:17], onesrf, lo, start=True, stop=True), "t")

    @block.vector
    def _(v):
        ctx.wait(v, "t")
        ctx.stamp(v.tensor_scalar(mT, snat, Y[0:128, 16:17], None, ALU.is_ge), "v")

    @block.tensor
    def _(t):
        ctx.wait(t, "v")
        ctx.stamp(t.transpose(P[0:16, 0:64].bitcast(BF16), mT, idnb), "t")

    @block.vector
    def _(v):
        ctx.wait(v, "t")
        ctx.stamp(v.tensor_copy(mt16, P[0:16, 0:64].bitcast(BF16)), "v")
    mready_v = ctx.cnt["v"]

    @block.tensor
    def _(t):
        ctx.wait(t, "v", mready_v)
        for r in range(16):
            ctx.stamp(t.matmul(Y[:, 128 * r:128 * (r + 1)],
                               ohc[:, 128 * r:128 * (r + 1)], mt16,
                               start=True, stop=True), "t")
    mbc_t = ctx.cnt["t"]

    pviews = [c512(P, c) for c in range(4)]
    pfree = []
    rot = 0
    for ec in range(8):
        for tc in range(4):
            pv = pviews[rot % 4]

            @block.tensor
            def _(t, ec=ec, tc=tc, pv=pv, rot=rot):
                ctx.wait(t, "dw", ldw)
                if rot >= 4:
                    ctx.wait(t, "v", pfree[rot - 4])
                for k in range(8):
                    ctx.stamp(t.matmul(pv, wblkb[k][:, 128 * ec:128 * (ec + 1)],
                                       c512(xTb[k], tc),
                                       start=(k == 0), stop=(k == 7)), "t")
            mm_t = ctx.cnt["t"]

            @block.scalar
            def _(s, pv=pv, mm_t=mm_t):
                ctx.wait(s, "t", mm_t)
                ctx.stamp(s.activation(sq, pv, AF.Square), "s")
            sq_s = ctx.cnt["s"]

            @block.vector
            def _(v, ec=ec, tc=tc, pv=pv, sq_s=sq_s):
                ctx.wait(v, "s", sq_s)
                ctx.stamp(v.tensor_tensor(cub, sq, pv, ALU.mult), "v")
                ctx.wait(v, "v")
                ctx.stamp(v.scalar_tensor_tensor(cub, cub, 0.044715, pv,
                                                 ALU.mult, ALU.add), "v")

            @block.scalar
            def _(s, pv=pv):
                ctx.wait(s, "v")
                ctx.stamp(s.activation(tnh, cub, AF.Tanh, scale=0.7978845608028654), "s")

            @block.vector
            def _(v, ec=ec, tc=tc, pv=pv):
                ctx.wait(v, "s")
                ctx.stamp(v.tensor_scalar_add(tnh, tnh, 1.0), "v")
                ctx.wait(v, "v")
                ctx.stamp(v.tensor_tensor(gmx, tnh, pv, ALU.mult), "v")
                ctx.wait(v, "v")
                ctx.stamp(v.scalar_tensor_tensor(
                    gmx, gmx, 0.5, zfull[ec][:, 512 * tc:512 * (tc + 1)],
                    ALU.mult, ALU.subtract), "v")
                ctx.wait(v, "v")
                pfree.append(ctx.cnt["v"] + 1)
                ctx.stamp(v.tensor_tensor(gmx, gmx, c512(Y, tc), ALU.mult), "v")
                ctx.wait(v, "v")
                ctx.stamp(v.tensor_tensor(c512(A[ec], tc),
                                          zfull[ec][:, 512 * tc:512 * (tc + 1)],
                                          gmx, ALU.add), "v")
            rot += 1

    ctx.release()




L_DELTAS = (-256, -128, 0, 128, 256, 384)


def emit_l1_attn(ctx, block, io, A, P, Y):
    nc = ctx.nc
    ctx.mark()
    xnT = [ctx.alloc(f"axn{i}", [128, T], BF16) for i in range(8)]
    qT = [ctx.alloc(f"aq{i}", [128, T], BF16) for i in range(8)]
    kT = [ctx.alloc(f"ak{i}", [128, T], BF16) for i in range(2)]
    vnat = [ctx.alloc(f"av{i}", [128, 260], BF16) for i in range(16)]
    cosb = ctx.alloc("cosb", [128, T], BF16)
    sinb = ctx.alloc("sinb", [128, T], BF16)
    gmask = ctx.alloc("gmask", [128, 4 * 512], BF16)
    lmask = ctx.alloc("lmask", [128, 6 * 512], BF16)
    hselA = ctx.alloc("hselA", [128, 128], BF16)
    hselB = ctx.alloc("hselB", [16, 1024], BF16)
    n1w = ctx.alloc("n1w", [128, 8], F32)
    n2w = ctx.alloc("n2w", [128, 8], F32)
    idnb2 = ctx.alloc("idnb2", [128, 128], BF16)
    srow = ctx.alloc("a_srow", [1, T], F32)
    srowb = ctx.alloc("a_srowb", [1, T], BF16)
    rstdq = ctx.alloc("rstdq", [16, T], BF16)
    rstdk = ctx.alloc("rstdk", [16, T], BF16)
    ptile = [ctx.alloc(f"apt{i}", [128, 512], BF16) for i in range(4)]
    sqb = ctx.alloc("asqb", [128, T], BF16)
    swp = ctx.alias("aswp", [128, T], BF16, "asqb")
    rt1 = ctx.alloc("art1", [128, T], BF16)
    kboth = ctx.alloc("akb", [128, T], BF16)
    rzall = ctx.alloc("arz", [16, T], BF16)
    xsc = ctx.alloc("axsc", [128, T], F32)
    rstdqf = xsc[0:16, :]
    zcoll = xsc[32:33, :].bitcast(BF16)[:, 0:T]
    xsc2 = ctx.alias("axsc2", [128, T], F32, "art1")
    onescb = ctx.alloc("a_onec", [128, 1], BF16)
    onerb = ctx.alloc("a_oner", [1, 128], BF16)

    zdram = nc.dram_tensor(f"zd{ctx.uid}", [16, T], BF16, kind="Internal").ap()

    ctx.mark()
    wb = [ctx.alloc(f"awb{i}", [128, D], BF16) for i in range(8)]
    wb2 = [ctx.alloc(f"aw2{i}", [128, 256], BF16) for i in range(8)]
    kupb = [ctx.alloc(f"aku{i}", [128, 256], BF16) for i in range(2)]
    vupb = [ctx.alloc(f"avu{i}", [128, 256], BF16) for i in range(2)]
    latT = [ctx.alias(f"alt{i}", [128, T], BF16, f"awb{4 * i}") for i in range(2)]
    OtT = [ctx.alias(f"aot{i}", [128, T], BF16, f"axn{i}") for i in range(8)]

    SHUF = []
    for i in range(16):
        SHUF += [2 * i + 1, 2 * i]

    @block.sync
    def _(sync):
        for nm, dst in (("c_hselA", hselA), ("c_hselB", hselB),
                        ("c_gmask", gmask), ("c_lmask", lmask),
                        ("n1w", n1w), ("n2w", n2w), ("c_idn", idnb2)):
            ctx.dma(sync, "dma", dst, io[nm])

    @block.vector
    def _(v):
        ctx.stamp(v.memset(onescb, 1.0), "v")
        ctx.stamp(v.memset(onerb, 1.0), "v")
        for i in range(16):
            for h4 in range(4):
                ctx.stamp(v.memset(vnat[i][:, 65 * h4 + 64:65 * h4 + 65], 1.0), "v")

    def rms_norm(nw, add_A):
        xb2 = [xsc, xsc2]
        vm = {}
        sq2 = [sqb[:, 0:1024], sqb[:, 1024:2048]]
        for blk in range(8):
            @block.sync
            def _(sync, blk=blk, w=vm.get(blk - 2, 0)):
                if w:
                    ctx.wait(sync, "v", w)
                ctx.dma(sync, "dma", xb2[blk % 2], io["xt"][128 * blk:128 * (blk + 1), :])
            dm = ctx.cnt["dma"]

            @block.vector
            def _(v, blk=blk, dm=dm, tm=(vm.get(blk - 2, 0))):
                ctx.wait(v, "dma", dm)
                ctx.wait(v, "t")
                if add_A:
                    ctx.stamp(v.tensor_tensor(xb2[blk % 2], xb2[blk % 2],
                                              A[blk], ALU.add), "v")
                    ctx.wait(v, "v")
                ctx.stamp(v.tensor_copy(sqb, xb2[blk % 2]), "v")
                ctx.wait(v, "v")
                ctx.stamp(v.tensor_tensor(sqb, sqb, sqb, ALU.mult), "v")
            vm[blk] = ctx.cnt["v"]

            @block.tensor
            def _(t, blk=blk, w=vm[blk]):
                ctx.wait(t, "v", w)
                for c in range(4):
                    ctx.stamp(t.matmul(c512(Y[0:1, :], c), onescb, c512(sqb, c),
                                       start=(blk == 0), stop=(blk == 7)), "t")

        @block.vector
        def _(v):
            ctx.wait(v, "t")
            ctx.stamp(v.tensor_scalar_mul(srow, Y[0:1, :], 1.0 / D), "v")

        @block.scalar
        def _(s):
            ctx.wait(s, "v")
            ctx.stamp(s.activation(srow, srow, AF.Ln, bias=1e-6), "s")
            ctx.wait(s, "s")
            ctx.stamp(s.activation(srow, srow, AF.Exp, scale=-0.5), "s")

        @block.vector
        def _(v):
            ctx.wait(v, "s")
            ctx.stamp(v.tensor_copy(srowb, srow), "v")
        rdy = ctx.cnt["v"]

        @block.tensor
        def _(t, rdy=rdy):
            ctx.wait(t, "v", rdy)
            ctx.wait(t, "v")
            ctx.wait(t, "s")
            for c in range(4):
                ctx.stamp(t.matmul(c512(P, c), onerb, c512(srowb, c),
                                   start=True, stop=True), "t")
        bmark = ctx.cnt["t"]

        vm2 = {-2: vm[6], -1: vm[7]}
        for blk in range(8):
            @block.sync
            def _(sync, blk=blk, w=vm2.get(blk - 2, 0)):
                if w:
                    ctx.wait(sync, "v", w)
                ctx.dma(sync, "dma", xb2[blk % 2], io["xt"][128 * blk:128 * (blk + 1), :])
            dm = ctx.cnt["dma"]

            @block.vector
            def _(v, blk=blk, nw=nw, dm=dm, bmark=bmark):
                ctx.wait(v, "dma", dm)
                ctx.wait(v, "t", bmark)
                if add_A:
                    ctx.stamp(v.tensor_tensor(xb2[blk % 2], xb2[blk % 2],
                                              A[blk], ALU.add), "v")
                    ctx.wait(v, "v")
                for c in range(4):
                    ctx.stamp(v.scalar_tensor_tensor(
                        c512(xnT[blk], c), c512(xb2[blk % 2], c), nw[:, blk:blk + 1],
                        c512(P, c), ALU.mult, ALU.mult), "v")
            vm2[blk] = ctx.cnt["v"]
        @block.tensor
        def _(t):
            ctx.wait(t, "v")

    def project(dstT, w_io, wbuf, wwidth, kchunks, src_tiles):
        @block.gpsimd
        def _(g):
            ctx.wait(g, "t")
            for k in range(kchunks):
                ctx.dma(g, "dw", wbuf[k][:, 0:wwidth],
                        w_io[128 * k:128 * (k + 1), :])
        ldw = ctx.cnt["dw"]
        rot = 0
        pfree = []
        ncol = wwidth // 128
        for m in range(ncol):
            for tc in range(4):
                pv = c512(P, rot % 4)

                @block.tensor
                def _(t, m=m, tc=tc, pv=pv, rot=rot, ldw=ldw):
                    ctx.wait(t, "dw", ldw)
                    ctx.wait(t, "v")
                    if rot >= 4:
                        ctx.wait(t, "v", pfree[rot - 4])
                    for k in range(kchunks):
                        ctx.stamp(t.matmul(pv, wbuf[k][:, 128 * m:128 * (m + 1)],
                                           c512(src_tiles[k], tc),
                                           start=(k == 0), stop=(k == kchunks - 1)), "t")

                @block.vector
                def _(v, m=m, tc=tc, pv=pv):
                    ctx.wait(v, "t")
                    pfree.append(ctx.stamp(
                        v.tensor_copy(c512(dstT[m], tc), pv), "v"))
                rot += 1

    def build_vnat(wbuf, kchunks, src_tiles):
        vp = Y[0:128, 512:1024]
        for tt in range(16):
            for h4 in range(4):
                @block.tensor
                def _(t, tt=tt, h4=h4):
                    ctx.wait(t, "v")
                    for k in range(kchunks):
                        ctx.stamp(t.matmul(vp[:, 0:64],
                                           src_tiles[k][:, 128 * tt:128 * (tt + 1)],
                                           wbuf[k][:, 64 * h4:64 * (h4 + 1)],
                                           start=(k == 0), stop=(k == kchunks - 1)), "t")

                @block.vector
                def _(v, tt=tt, h4=h4):
                    ctx.wait(v, "t")
                    ctx.stamp(v.tensor_copy(vnat[tt][:, 65 * h4:65 * h4 + 64],
                                            vp[:, 0:64]), "v")

    def rope_qknorm(tiles, js, rstd_dst, nrows):
        for idx, tl in enumerate(tiles):
            @block.vector
            def _(v, tl=tl):
                ctx.wait(v, "v")
                ctx.wait(v, "t")
                ctx.stamp(v.tensor_tensor(sqb, tl, tl, ALU.mult), "v")

            @block.tensor
            def _(t, idx=idx, j=js[idx], last=(idx == len(tiles) - 1)):
                ctx.wait(t, "v")
                for c in range(4):
                    ctx.stamp(t.matmul(c512(Y[0:16, :], c),
                                       hselA[:, 16 * j:16 * (j + 1)],
                                       c512(sqb, c),
                                       start=(idx == 0), stop=last), "t")

        @block.vector
        def _(v):
            ctx.wait(v, "t")
            ctx.stamp(v.tensor_scalar_mul(rstdqf[0:nrows, :], Y[0:nrows, :],
                                          1.0 / 64.0), "v")

        @block.scalar
        def _(s):
            ctx.wait(s, "v")
            ctx.stamp(s.activation(rstdqf[0:nrows, :], rstdqf[0:nrows, :],
                                   AF.Ln, bias=1e-6), "s")
            ctx.wait(s, "s")
            ctx.stamp(s.activation(rstdqf[0:nrows, :], rstdqf[0:nrows, :],
                                   AF.Exp, scale=-0.5), "s")

        @block.vector
        def _(v):
            ctx.wait(v, "s")
            ctx.stamp(v.tensor_copy(rstd_dst[0:nrows, :], rstdqf[0:nrows, :]), "v")
        rdy = ctx.cnt["v"]

        for idx, tl in enumerate(tiles):
            j = js[idx]

            @block.vector
            def _(v, tl=tl, rdy=rdy):
                ctx.wait(v, "v", rdy)
                ctx.wait(v, "v")
                ctx.stamp(v.stream_shuffle(swp, tl, SHUF), "v")
                ctx.wait(v, "v")
                ctx.stamp(v.tensor_tensor(rt1, tl, cosb, ALU.mult), "v")
                ctx.wait(v, "v")
                ctx.stamp(v.tensor_tensor(swp, swp, sinb, ALU.mult), "v")
                ctx.wait(v, "v")
                ctx.stamp(v.tensor_tensor(rt1, rt1, swp, ALU.add), "v")
            rr = ctx.cnt["v"]

            @block.tensor
            def _(t, j=j, rr=rr):
                ctx.wait(t, "v", rr)
                for c in range(4):
                    ctx.stamp(t.matmul(c512(P, c),
                                       hselB[0:nrows, 128 * j:128 * (j + 1)],
                                       c512(rstd_dst[0:nrows, :], c),
                                       start=True, stop=True), "t")

            @block.vector
            def _(v, tl=tl):
                ctx.wait(v, "t")
                for c in range(4):
                    ctx.stamp(v.tensor_tensor(c512(tl, c), c512(rt1, c),
                                              c512(P, c), ALU.mult), "v")

    def attn_scores_wo(kind, wo_io, out_mode):
        causal = (kind == "g")
        OTp2 = [Y[0:65, 0:512], Y[0:65, 512:1024]]
        ev_marks = {0: (0, 0), 1: (0, 0)}

        @block.tensor
        def _(t):
            ctx.wait(t, "v")
            ctx.wait(t, "s")
        rot = [0]
        pm_exp = {}
        pm_mask = {}
        pm_ot = {}
        cur_kvh = [-1]
        kb_mark = [0]
        for h in range(16):
            kvh = h // 4
            if kvh != cur_kvh[0]:
                cur_kvh[0] = kvh

                @block.vector
                def _(v, kvh=kvh):
                    ctx.wait(v, "v")
                    ctx.wait(v, "t")
                    src_sl = kT[kvh // 2][64 * (kvh % 2):64 * (kvh % 2) + 64, :]
                    ctx.stamp(v.tensor_copy(kboth[0:64, :], src_sl), "v")
                    ctx.stamp(v.tensor_copy(kboth[64:128, :], src_sl), "v")
                kb_mark[0] = ctx.cnt["v"]
            qtile = qT[h // 2]
            qrow0 = 64 * (h % 2)
            for cq in range(4):
                gpar = (4 * h + cq) % 2
                OTp = OTp2[gpar]
                iks = list(range(0, 4 * cq + 4)) if causal else \
                    list(range(max(0, 4 * cq - 2), 4 * cq + 4))
                for n_i, ik in enumerate(iks):
                    rr = rot[0]
                    r = rr % 4
                    pr = rr % 4
                    pv = c512(P, r)
                    dlt = 128 * ik - 512 * cq
                    if causal:
                        masked = dlt >= 0
                        mvar = dlt // 128 if masked else None
                        msrc = gmask
                    else:
                        masked = True
                        mvar = L_DELTAS.index(dlt)
                        msrc = lmask

                    @block.tensor
                    def _(t, pv=pv, ik=ik, cq=cq, qtile=qtile, qrow0=qrow0, rr=rr,
                          kbm=kb_mark[0], masked=masked, mvar=mvar, msrc=msrc):
                        ctx.wait(t, "v", kbm)
                        if rr - 4 in pm_exp:
                            ctx.wait(t, "s", pm_exp[rr - 4])
                        ctx.stamp(t.matmul(
                            pv, kboth[qrow0:qrow0 + 64, 128 * ik:128 * (ik + 1)],
                            qtile[qrow0:qrow0 + 64, 512 * cq:512 * (cq + 1)],
                            start=True, stop=not masked), "t")
                        if masked:
                            ctx.stamp(t.matmul(
                                pv, idnb2,
                                msrc[:, 512 * mvar:512 * (mvar + 1)],
                                start=False, stop=True), "t")
                    mmt = ctx.cnt["t"]

                    @block.scalar
                    def _(s, pv=pv, pr=pr, mmt=mmt, rr=rr):
                        ctx.wait(s, "t", mmt)
                        if rr - 4 in pm_ot:
                            ctx.wait(s, "t", pm_ot[rr - 4])
                        ctx.stamp(s.activation(ptile[pr], pv, AF.Exp, scale=0.125), "s")
                    pm_exp[rr] = ctx.cnt["s"]

                    @block.tensor
                    def _(t, pr=pr, ik=ik, kvh=kvh, n_i=n_i, OTp=OTp,
                          last=(n_i == len(iks) - 1), rr=rr,
                          evm=(ev_marks[gpar][1] if n_i == 0 else 0)):
                        if n_i == 0 and evm:
                            ctx.wait(t, "v", evm)
                        ctx.wait(t, "s", pm_exp[rr])
                        ctx.stamp(t.matmul(
                            OTp, vnat[ik][:, 65 * kvh:65 * (kvh + 1)], ptile[pr],
                            start=(n_i == 0), stop=last), "t")
                    pm_ot[rr] = ctx.cnt["t"]
                    rot[0] += 1

                otm = ctx.cnt["t"]

                @block.vector
                def _(v, h=h, cq=cq, otm=otm, OTp=OTp, zdm=ctx.cnt["dmo"]):
                    ctx.wait(v, "t", otm)
                    ctx.wait(v, "dmo", zdm)
                    ctx.stamp(v.tensor_copy(
                        OtT[h // 2][64 * (h % 2):64 * (h % 2) + 64,
                                    512 * cq:512 * (cq + 1)],
                        OTp[0:64, :]), "v")
                    ctx.stamp(v.tensor_copy(zcoll[0:1, 512 * cq:512 * (cq + 1)],
                                            OTp[64:65, :]), "v")
                ev_marks[gpar] = (ctx.cnt["s"], ctx.cnt["v"])

            @block.gpsimd
            def _(g, h=h):
                ctx.wait(g, "v")
                ctx.dma(g, "dmo", zdram[h:h + 1, :], zcoll)

        @block.sync
        def _(sync):
            ctx.wait(sync, "dmo", ctx.cnt["dmo"])
            ctx.dma(sync, "dma", rzall, zdram)

        @block.vector
        def _(v):
            ctx.wait(v, "dma")
            ctx.stamp(v.reciprocal(rstdqf, rzall), "v")
            ctx.wait(v, "v")
            ctx.stamp(v.tensor_copy(rzall, rstdqf), "v")
        rzrdy = ctx.cnt["v"]

        for j in range(8):
            @block.tensor
            def _(t, j=j, rzrdy=rzrdy):
                ctx.wait(t, "v", rzrdy)
                ctx.wait(t, "v")
                for c in range(4):
                    ctx.stamp(t.matmul(c512(P, c), hselB[:, 128 * j:128 * (j + 1)],
                                       c512(rzall, c), start=True, stop=True), "t")

            @block.vector
            def _(v, j=j):
                ctx.wait(v, "t")
                for c in range(4):
                    ctx.stamp(v.tensor_tensor(c512(OtT[j], c), c512(OtT[j], c),
                                              c512(P, c), ALU.mult), "v")

        @block.gpsimd
        def _(g):
            ctx.wait(g, "t")
            for k in range(8):
                ctx.dma(g, "dw", wb[k], wo_io[128 * k:128 * (k + 1), :])
        ldwo = ctx.cnt["dw"]
        rot2 = 0
        pfree2 = []
        for blk in range(8):
            for tc in range(4):
                pv = c512(P, rot2 % 4)

                @block.tensor
                def _(t, blk=blk, tc=tc, pv=pv, rot2=rot2, ldwo=ldwo):
                    ctx.wait(t, "dw", ldwo)
                    ctx.wait(t, "v")
                    if rot2 >= 4:
                        ctx.wait(t, "v", pfree2[rot2 - 4])
                    for k in range(8):
                        ctx.stamp(t.matmul(pv, wb[k][:, 128 * blk:128 * (blk + 1)],
                                           c512(OtT[k], tc),
                                           start=(k == 0), stop=(k == 7)), "t")

                @block.vector
                def _(v, blk=blk, tc=tc, pv=pv):
                    ctx.wait(v, "t")
                    if out_mode == "set":
                        pfree2.append(ctx.stamp(
                            v.tensor_copy(c512(A[blk], tc), pv), "v"))
                    else:
                        pfree2.append(ctx.stamp(
                            v.tensor_tensor(c512(A[blk], tc), c512(A[blk], tc),
                                            pv, ALU.add), "v"))
                rot2 += 1

    @block.sync
    def _(sync):
        ctx.dma(sync, "dma", cosb, io["c_cosg"])
        ctx.dma(sync, "dma", sinb, io["c_sing"])

    rms_norm(n1w, add_A=False)
    project(qT, io["gwq"], wb, 1024, 8, xnT)
    project(latT, io["gkvd"], wb2, 256, 8, xnT)

    @block.gpsimd
    def _(g):
        ctx.wait(g, "t")
        for k in range(2):
            ctx.dma(g, "dw", vupb[k], io["gvup"][128 * k:128 * (k + 1), :])
    ldup = ctx.cnt["dw"]

    @block.tensor
    def _(t, ldup=ldup):
        ctx.wait(t, "dw", ldup)

    project(kT, io["gkup"], kupb, 256, 2, latT)
    build_vnat(vupb, 2, latT)
    rope_qknorm(qT, list(range(8)), rstdq, 16)
    rope_qknorm(kT, [0, 1], rstdk, 4)
    attn_scores_wo("g", io["gwo"], "set")

    @block.sync
    def _(sync):
        ctx.wait(sync, "v")
        ctx.wait(sync, "s")
        ctx.dma(sync, "dma", cosb, io["c_cosl"])
        ctx.dma(sync, "dma", sinb, io["c_sinl"])

    rms_norm(n2w, add_A=True)
    project(qT, io["lwq"], wb, 1024, 8, xnT)
    project(kT, io["lwk"], wb2, 256, 8, xnT)

    @block.gpsimd
    def _(g):
        ctx.wait(g, "t")
        for k in range(8):
            ctx.dma(g, "dw", wb2[k], io["lwv"][128 * k:128 * (k + 1), :])
    ldlv = ctx.cnt["dw"]

    @block.tensor
    def _(t, ldlv=ldlv):
        ctx.wait(t, "dw", ldlv)
        ctx.wait(t, "v")

    build_vnat(wb2, 8, xnT)
    rope_qknorm(qT, list(range(8)), rstdq, 16)
    rope_qknorm(kT, [0, 1], rstdk, 4)
    attn_scores_wo("l", io["lwo"], "add")

    ctx.release()
    ctx.release()




def emit_l1_ffn(ctx, block, io, A, P, Y):
    ctx.mark()
    xTb = [ctx.alloc(f"mx{i}", [128, T], BF16) for i in range(8)]
    w1g = [ctx.alloc(f"mw1_{i}", [128, T], BF16) for i in range(8)]
    w2g = [ctx.alloc(f"mw2_{i}", [128, T], BF16) for i in range(8)]
    cwT = ctx.alloc("mcw", [64, T], BF16)
    rw = ctx.alloc("mrw", [128, 512], BF16)
    rbc = ctx.alloc("mrbc", [64, 1], F32)
    lognat = ctx.alloc("mlog", [128, 1024], F32)
    logstg = ctx.alloc("mlstg", [64, 512], F32)
    b1t = ctx.alloc("mb1t", [128, 128], F32)
    sb1t = [ctx.alloc(f"msb1_{s}", [128, 16], F32) for s in range(2)]
    b2cb = ctx.alloc("mb2", [64, 1024], BF16)
    sb2b = ctx.alloc("msb2", [2, 1024], BF16)
    ones2 = ctx.alloc("mones2", [2, 512], BF16)
    i64b = ctx.alloc("mi64", [64, 64], BF16)
    idnb = ctx.alloc("midn", [128, 128], BF16)
    idnf = ctx.alloc("midnf", [128, 128], F32)
    prob = ctx.alloc("mprob", [128, 64], F32)
    pmax = ctx.alloc("mpmax", [128, 1], F32)
    pden = ctx.alloc("mpden", [128, 1], F32)
    m8 = ctx.alloc("mm8", [128, 8], F32)
    cwn = ctx.alloc("mcwn", [128, 64], BF16)
    ge01 = ctx.alloc("mge", [128, 64], F32)

    @block.gpsimd
    def _(g):
        for i in range(8):
            ctx.dma(g, "dw", xTb[i], io["xt"][128 * i:128 * (i + 1), :])
        ctx.dma(g, "dw", rw.rearrange("p (k e) -> p k e", e=64),
                io["rwT"].rearrange("(k p) e -> p k e", p=128))
        ctx.dma(g, "dw", b2cb, io["b2c"])
        ctx.dma(g, "dw", sb2b, io["sb2c"])
    ldx = ctx.cnt["dw"]

    @block.sync
    def _(sync):
        for nm, dst in (("rbc", rbc), ("b1t", b1t), ("sb1t0", sb1t[0]),
                        ("sb1t1", sb1t[1]), ("c_idn", idnb), ("idnf", idnf),
                        ("c_i64", i64b)):
            ctx.dma(sync, "dma", dst, io[nm])
    ldc = ctx.cnt["dma"]

    @block.vector
    def _(v):
        ctx.stamp(v.memset(ones2, 1.0), "v")

    for tc in range(4):
        @block.tensor
        def _(t, tc=tc):
            ctx.wait(t, "dw", ldx)
            ctx.wait(t, "dma", ldc)
            ctx.wait(t, "v")
            ctx.wait(t, "s")
            for k in range(8):
                ctx.stamp(t.matmul(P[0:64, 512 * tc:512 * (tc + 1)],
                                   rw[:, 64 * k:64 * (k + 1)],
                                   c512(xTb[k], tc),
                                   start=(k == 0), stop=(k == 7)), "t")

        @block.scalar
        def _(s, tc=tc):
            ctx.wait(s, "t")
            ctx.wait(s, "v")
            ctx.stamp(s.activation(logstg, P[0:64, 512 * tc:512 * (tc + 1)],
                                   AF.Identity, bias=rbc), "s")

        for j in range(4):
            @block.tensor
            def _(t, tc=tc, j=j):
                ctx.wait(t, "s")
                ctx.wait(t, "v")
                ctx.stamp(t.transpose(P[0:128, 1536:1600],
                                      logstg[0:64, 128 * j:128 * (j + 1)],
                                      idnf[0:64, 0:64]), "t")

            @block.vector
            def _(v, tc=tc, j=j):
                ctx.wait(v, "t")
                tt = 4 * tc + j
                ctx.stamp(v.tensor_copy(lognat[:, 64 * tt:64 * (tt + 1)],
                                        P[0:128, 1536:1600]), "v")

    for tt in range(16):
        @block.vector
        def _(v, tt=tt):
            ctx.wait(v, "v")
            ctx.wait(v, "s")
            sl = lognat[:, 64 * tt:64 * (tt + 1)]
            ctx.stamp(v.reduce_max(pmax, sl, axis=AX.X), "v")
            ctx.wait(v, "v")
            ctx.stamp(v.tensor_scalar_mul(pmax, pmax, -1.0), "v")

        @block.scalar
        def _(s, tt=tt):
            ctx.wait(s, "v")
            sl = lognat[:, 64 * tt:64 * (tt + 1)]
            ctx.stamp(s.activation(prob, sl, AF.Exp, bias=pmax, accum_out=pden), "s")

        @block.vector
        def _(v, tt=tt):
            ctx.wait(v, "s")
            ctx.stamp(v.reciprocal(pmax, pden), "v")
            ctx.wait(v, "v")
            ctx.stamp(v.tensor_scalar(prob, prob, pmax, None, ALU.mult), "v")
            ctx.wait(v, "v")
            ctx.stamp(v.max(m8, prob), "v")
            ctx.wait(v, "v")
            ctx.stamp(v.tensor_scalar(ge01, prob, m8[:, 7:8], None, ALU.is_ge), "v")
            ctx.stamp(v.reduce_sum(pden, m8, axis=AX.X), "v")
            ctx.wait(v, "v")
            ctx.stamp(v.reciprocal(pden, pden), "v")
            ctx.stamp(v.tensor_tensor(ge01, ge01, prob, ALU.mult), "v")
            ctx.wait(v, "v")
            ctx.stamp(v.tensor_scalar(cwn, ge01, pden, None, ALU.mult), "v")

        @block.tensor
        def _(t, tt=tt):
            ctx.wait(t, "v")
            ctx.stamp(t.transpose(P[0:64, 1536:1600].bitcast(BF16), cwn, idnb), "t")

        @block.vector
        def _(v, tt=tt):
            ctx.wait(v, "t")
            ctx.stamp(v.tensor_copy(cwT[:, 128 * tt:128 * (tt + 1)],
                                    P[0:64, 1536:1600].bitcast(BF16)), "v")

    router_v = ctx.cnt["v"]
    CAP = 384
    H1 = [(P[:, 0:512], P[:, 512:1024]), (P[:, 1024:1536], P[:, 1536:2048])]
    H1S = [[P[:, 0:CAP], P[:, 512:512 + CAP]],
           [P[:, 1024:1024 + CAP], P[:, 1536:1536 + CAP]]]
    Y2 = [[Y[:, 0:512], Y[:, 512:1024]],
          [Y[:, 1024:1536], Y[:, 1536:2048]]]
    CWBv = [Y[:, 0:512], Y[:, 512:1024]]

    spr = ctx.alloc("spr", [128, 39936], U8)

    def spat(name, shape, dt, delta):
        ctx.uid += 1
        return ctx.nc.alloc_sbuf_tensor_at(
            f"{name}_{ctx.uid}", list(shape), dt,
            offset=ctx.offsets["spr"] + delta).ap()

    xgp = [spat(f"xgp{i}", [128, 16 * CAP], BF16, 12288 * i) for i in range(2)]
    silT = [[spat(f"slt{p}{m}", [128, CAP], BF16, 24576 + 768 * (2 * p + m))
             for m in range(2)] for p in range(2)]
    h2sb = [[spat(f"h2s{q}{c}", [128, 1024], BF16, 27648 + 2048 * (3 * q + c))
             for c in range(3)] for q in range(2)]
    def xat(name, shape, dt, base, delta):
        ctx.uid += 1
        return ctx.nc.alloc_sbuf_tensor_at(
            f"{name}_{ctx.uid}", list(shape), dt,
            offset=ctx.offsets[base] + delta).ap()

    h2sb.append([xat("h2s20", [128, 1024], BF16, "mx0", 0),
                 xat("h2s21", [128, 1024], BF16, "mx0", 2048),
                 xat("h2s22", [128, 1024], BF16, "mx1", 0)])
    h2sb.append([xat("h2s30", [128, 1024], BF16, "mx1", 2048),
                 xat("h2s31", [128, 1024], BF16, "mx2", 0),
                 xat("h2s32", [128, 1024], BF16, "mx2", 2048)])
    xgp.append(xat("xgp2", [128, 16 * CAP], BF16, "mx3", 0))
    stash2 = [[[spat(f"msth{p}_{i}_{m}", [128, 512], BF16,
                     1024 * (16 * p + 2 * i + m)) for m in range(2)]
               for i in range(8)] for p in range(2)]
    silu_par = [[spat(f"mslp{p}_{m}", [128, 512], BF16,
                      32768 + 1024 * (2 * p + m)) for m in range(2)]
                for p in range(2)]

    wA = spat("wA", [64, T], F32, 0)
    wM = spat("wM", [64, T], F32, 8192)
    wP = spat("wP", [64, T], F32, 16384)
    wsel16 = ctx.alias("wsel16", [64, T], I16, "mlog")
    lists_nu = ctx.alloc("listsnu", [64, CAP], U32)
    jsel16 = ctx.alloc("jsel16", [64, T], I16)
    tokv = ctx.alloc("tokv", [64, T], I16)
    iotan_b = ctx.alloc("iotanb", [64, CAP], I16)
    pm16 = ctx.alloc("pm16", [64, CAP], I16)
    lists_w = ctx.alloc("listsw", [64, 512], I16)
    lists_n = ctx.alloc("listsn", [64, CAP], I16)
    cwl = ctx.alloc("cwl", [64, CAP], BF16)
    xgp.append(xat("xgp3", [128, 16 * CAP], BF16, "jsel16", 0))
    cwgF = ctx.alloc("cwgF", [128, 3 * 64], F32)
    idxg = [ctx.alloc(f"idxg{i}", [128, 48], I16) for i in range(4)]
    iu32 = [[ctx.alloc(f"iu{i}_{c}", [128, 1], U32) for c in range(3)]
            for i in range(16)]
    xnrow = [ctx.alloc(f"xnr{i}", [128, 1024], BF16) for i in range(2)]
    zrow = ctx.alias("zrow", [128, 1024], BF16, "mlstg")

    GROUPS = [("shared", 0), ("shared", 1)]
    dwmark = {}
    silu_marks = []
    stash_marks = []
    h2_end_t = {0: 0, 1: 0}
    flushv_last = {d: 0 for d in range(4)}

    def load_group_w(gi, which, gate_t):
        if gi >= len(GROUPS):
            return
        kind, s_idx = GROUPS[gi]

        @block.gpsimd
        def _(g, gi=gi, which=which, gate_t=gate_t, kind=kind, s_idx=s_idx):
            ctx.wait(g, "t", gate_t)
            for e in range(8):
                if which == "w1":
                    srcap = io["sw1"][s_idx][:, 256 * e:256 * (e + 1)]
                    ctx.dma(g, "dw", w1g[e].rearrange("p (k h) -> p k h", h=256),
                            srcap.rearrange("(k p) h -> p k h", p=128))
                else:
                    srcap = io["sw2"][s_idx][256 * e:256 * (e + 1), :]
                    ctx.dma(g, "dmo", w2g[e].rearrange("p (k d) -> p k d", d=1024),
                            srcap.rearrange("(k p) d -> p k d", p=128))
                dwmark[(gi, e, which)] = ctx.cnt["dmo" if which == "w2" else "dw"]

    def do_group(gi):
        kind, s_idx = GROUPS[gi]
        for tc in range(4):
            sp = tc % 2
            for e in range(8):
                ep = e % 2

                @block.tensor
                def _(t, gi=gi, e=e, tc=tc, ep=ep,
                      sm=(silu_marks[-2] if len(silu_marks) >= 2 else 0)):
                    ctx.wait(t, "dw", dwmark[(gi, e, "w1")])
                    if sm:
                        ctx.wait(t, "s", sm)
                    else:
                        ctx.wait(t, "v", router_v)
                    for m in range(2):
                        for k in range(8):
                            ctx.stamp(t.matmul(
                                H1[ep][m],
                                w1g[e][:, 256 * k + 128 * m:256 * k + 128 * (m + 1)],
                                c512(xTb[k], tc),
                                start=(k == 0), stop=(k == 7)), "t")
                h1t = ctx.cnt["t"]

                @block.scalar
                def _(s, gi=gi, e=e, ep=ep, s_idx=s_idx, h1t=h1t,
                      stm=(stash_marks[-2] if len(stash_marks) >= 2 else 0)):
                    ctx.wait(s, "t", h1t)
                    if stm:
                        ctx.wait(s, "v", stm)
                    bias0 = sb1t[s_idx][:, 2 * e:2 * e + 1]
                    bias1 = sb1t[s_idx][:, 2 * e + 1:2 * e + 2]
                    ctx.stamp(s.activation(silu_par[ep][0], H1[ep][0], AF.Silu,
                                           bias=bias0), "s")
                    ctx.stamp(s.activation(silu_par[ep][1], H1[ep][1], AF.Silu,
                                           bias=bias1), "s")
                silu_marks.append(ctx.cnt["s"])

                @block.vector
                def _(v, e=e, ep=ep, sp=sp, sm=silu_marks[-1],
                      h2t=h2_end_t[sp]):
                    ctx.wait(v, "s", sm)
                    if h2t:
                        ctx.wait(v, "t", h2t)
                    ctx.stamp(v.tensor_copy(stash2[sp][e][0], silu_par[ep][0]), "v")
                    ctx.stamp(v.tensor_copy(stash2[sp][e][1], silu_par[ep][1]), "v")
                stash_marks.append(ctx.cnt["v"])
            h1done_v = ctx.cnt["v"]
            if tc == 3:
                load_group_w(gi + 1, "w1", ctx.cnt["t"])
                if gi == 1:
                    g1_tc3_t[0] = ctx.cnt["t"]

            for rnd in range(2):
                h2mm = {}
                for dsub in range(4):
                    dout = 4 * rnd + dsub
                    yv = c512(Y, dsub)

                    @block.tensor
                    def _(t, gi=gi, tc=tc, dout=dout, yv=yv, s_idx=s_idx,
                          h1v=h1done_v, sp=sp, fv=flushv_last[dsub],
                          dwm=dwmark[(gi, 7, "w2")]):
                        ctx.wait(t, "dmo", dwm)
                        ctx.wait(t, "v", h1v)
                        if fv:
                            ctx.wait(t, "v", fv)
                        mms = [("w", e, m) for e in range(8) for m in range(2)]
                        if s_idx == 0:
                            mms.append(("sb2",))
                        else:
                            mms.append(("b2",))
                        for i_m, mm in enumerate(mms):
                            st = (i_m == 0)
                            spp = (i_m == len(mms) - 1)
                            if mm[0] == "w":
                                _, e, m = mm
                                ctx.stamp(t.matmul(
                                    yv,
                                    w2g[e][:, 1024 * m + 128 * dout:
                                           1024 * m + 128 * (dout + 1)],
                                    stash2[sp][e][m],
                                    start=st, stop=spp), "t")
                            elif mm[0] == "b2":
                                ctx.stamp(t.matmul(
                                    yv, b2cb[:, 128 * dout:128 * (dout + 1)],
                                    cwT[:, 512 * tc:512 * (tc + 1)],
                                    start=st, stop=spp), "t")
                            else:
                                ctx.stamp(t.matmul(
                                    yv, sb2b[:, 128 * dout:128 * (dout + 1)],
                                    ones2, start=st, stop=spp), "t")
                    h2mm[dsub] = ctx.cnt["t"]

                for dsub in range(4):
                    dout = 4 * rnd + dsub
                    yv = c512(Y, dsub)

                    @block.vector
                    def _(v, tc=tc, dout=dout, yv=yv, first=(gi == 0),
                          hm=h2mm[dsub]):
                        ctx.wait(v, "t", hm)
                        if first:
                            ctx.stamp(v.tensor_copy(c512(A[dout], tc), yv), "v")
                        else:
                            ctx.stamp(v.tensor_tensor(c512(A[dout], tc),
                                                      c512(A[dout], tc), yv,
                                                      ALU.add), "v")
                    flushv_last[dsub] = ctx.cnt["v"]
            h2_end_t[sp] = ctx.cnt["t"]
        load_group_w(gi + 1, "w2", ctx.cnt["t"])

    g1_tc3_t = [0]
    load_group_w(0, "w1", 0)
    load_group_w(0, "w2", 0)

    @block.vector
    def _(v):
        ctx.stamp(v.memset(zrow, 0.0), "v")
    zr_mark = ctx.cnt["v"]

    @block.sync
    def _(sync):
        ctx.wait(sync, "v", zr_mark)
        for r in range(16, 19):
            ctx.dma(sync, "dma", io["xnatb"][128 * r:128 * (r + 1), :], zrow)
        for b in range(2):
            acc = io["acc0" if b == 0 else "acc1"]
            for r in range(19):
                ctx.dma(sync, "dma", acc[128 * r:128 * (r + 1), :], zrow)
        ctx.dma(sync, "dma", tokv, io["iotam"])
        ctx.dma(sync, "dma", iotan_b, io["iotan"])
    zeroed_mark = ctx.cnt["dma"]

    xnat_vm = {}
    xnat_dmam = {}
    for r in range(16):
        rb = r % 2

        @block.tensor
        def _(t, r=r, rb=rb, wv=xnat_vm.get(r - 2, 0)):
            ctx.wait(t, "dw", ldx)
            ctx.wait(t, "dma", ldc)
            if wv:
                ctx.wait(t, "v", wv)
            for k in range(8):
                ctx.stamp(t.transpose(
                    Y[:, 512 * rb + 64 * k:512 * rb + 64 * (k + 1)].bitcast(BF16),
                    xTb[k][:, 128 * r:128 * (r + 1)], idnb), "t")
        tm = ctx.cnt["t"]

        @block.vector
        def _(v, rb=rb, tm=tm, dm=xnat_dmam.get(r - 2, 0)):
            ctx.wait(v, "t", tm)
            if dm:
                ctx.wait(v, "dma", dm)
            ctx.stamp(v.tensor_copy(
                xnrow[rb], Y[:, 512 * rb:512 * (rb + 1)].bitcast(BF16)), "v")
        xnat_vm[r] = ctx.cnt["v"]

        @block.sync
        def _(sync, r=r, rb=rb, vm=xnat_vm[r]):
            ctx.wait(sync, "v", vm)
            ctx.dma(sync, "dma", io["xnatb"][128 * r:128 * (r + 1), :], xnrow[rb])
        xnat_dmam[r] = ctx.cnt["dma"]
    xnat_done = ctx.cnt["dma"]

    @block.vector
    def _(v):
        ctx.stamp(v.memset(wA, 1.0), "v")
        ctx.stamp(v.tensor_scalar(wM, cwT, 1e-8, None, ALU.is_ge), "v")
        ctx.wait(v, "v")
        ctx.wait(v, "v")
        ctx.stamp(v.tensor_tensor_scan(wP, wA, wM, 0.0, ALU.mult, ALU.add), "v")
        ctx.wait(v, "v")
        ctx.stamp(v.tensor_tensor(wP, wP, wM, ALU.subtract), "v")
        ctx.wait(v, "v")
        ctx.stamp(v.tensor_scalar(wA, wP, float(CAP) - 0.5, None, ALU.is_le), "v")
        ctx.wait(v, "v")
        ctx.stamp(v.tensor_tensor(wM, wM, wA, ALU.mult), "v")
        ctx.wait(v, "v")
        ctx.stamp(v.tensor_scalar(wA, wP, 1.0, None, ALU.add), "v")
        ctx.wait(v, "v")
        ctx.stamp(v.tensor_tensor(wA, wA, wM, ALU.mult), "v")
        ctx.wait(v, "v")
        ctx.stamp(v.tensor_scalar(wA, wA, -1.0, None, ALU.add), "v")
        ctx.wait(v, "v")
        ctx.stamp(v.tensor_copy(jsel16, wA), "v")
        ctx.wait(v, "v")
        ctx.stamp(v.tensor_copy(wsel16, wP), "v")
        ctx.wait(v, "v")
        ctx.stamp(v.tensor_scalar(wsel16, wsel16, 15, None,
                                  ALU.bitwise_and), "v")
        ctx.wait(v, "v")
        ctx.stamp(v.tensor_copy(wA, wsel16), "v")
        ctx.wait(v, "v")
        ctx.stamp(v.tensor_tensor(wP, wP, wA, ALU.subtract), "v")
        ctx.wait(v, "v")
        ctx.stamp(v.tensor_scalar(wP, wP, 0.0625, None, ALU.mult), "v")
        ctx.wait(v, "v")
        ctx.stamp(v.tensor_scalar(wA, wA, 32.0, None, ALU.mult), "v")
        ctx.wait(v, "v")
        ctx.stamp(v.tensor_tensor(wP, wP, wA, ALU.add), "v")
        ctx.wait(v, "v")
        ctx.stamp(v.tensor_scalar(wP, wP, 1.0, None, ALU.add), "v")
        ctx.wait(v, "v")
        ctx.stamp(v.tensor_tensor(wP, wP, wM, ALU.mult), "v")
        ctx.wait(v, "v")
        ctx.stamp(v.tensor_scalar(wP, wP, -1.0, None, ALU.add), "v")
        ctx.wait(v, "v")
        ctx.stamp(v.tensor_copy(wsel16, wP), "v")
    sel_mark = ctx.cnt["v"]

    @block.gpsimd
    def _(g):
        ctx.wait(g, "v", sel_mark)
        ctx.wait(g, "dma", zeroed_mark)
        g.load_library(library_config.local_scatter)
        ctx.stamp(g.local_scatter(lists_w, tokv, wsel16, 64, 512, T), "g")
        ctx.stamp(g.local_scatter(lists_n, tokv, jsel16, 64, CAP, T), "g")
        ctx.stamp(g.local_scatter(cwl, cwT, jsel16, 64, CAP, T), "g")
        g.load_library(library_config.mlp)
    ls_mark = ctx.cnt["g"]

    @block.vector
    def _(v):
        ctx.wait(v, "g", ls_mark)
        ctx.stamp(v.tensor_scalar(lists_w, lists_w, 2048.0, None, ALU.add), "v")
        ctx.stamp(v.tensor_scalar(pm16, lists_n, 0.0, None, ALU.is_ge), "v")
        ctx.wait(v, "v")
        ctx.stamp(v.tensor_tensor(pm16, pm16, iotan_b, ALU.mult), "v")
        ctx.wait(v, "v")
        ctx.stamp(v.tensor_tensor(lists_n, lists_n, pm16, ALU.add), "v")
        ctx.wait(v, "v")
        ctx.stamp(v.tensor_scalar(lists_n, lists_n, 2048.0, None, ALU.add), "v")
        ctx.wait(v, "v")
        ctx.stamp(v.tensor_copy(lists_nu, lists_n), "v")
    lfix_mark = ctx.cnt["v"]

    @block.sync
    def _(sync):
        ctx.wait(sync, "v", lfix_mark)
        ctx.dma(sync, "dma", io["wld"], lists_w)
        ctx.dma(sync, "dma", io["nld"], lists_nu)
    lists_dram = ctx.cnt["dma"]

    do_group(0)
    do_group(1)
    shared_end_t = ctx.cnt["t"]
    shared_end_s = ctx.cnt["s"]
    shared_end_v = ctx.cnt["v"]

    @block.tensor
    def _(t):
        ctx.wait(t, "g", ls_mark)
        ctx.wait(t, "s", shared_end_s)
        for c in range(3):
            ctx.stamp(t.transpose(P[:, 32 * c:32 * (c + 1)].bitcast(BF16),
                                  cwl[:, 128 * c:128 * (c + 1)],
                                  idnb[0:64, 0:64]), "t")
    cwg_t = ctx.cnt["t"]

    @block.vector
    def _(v):
        ctx.wait(v, "t", cwg_t)
        for c in range(3):
            ctx.stamp(v.tensor_copy(cwgF[:, 64 * c:64 * (c + 1)],
                                    P[:, 32 * c:32 * (c + 1)].bitcast(BF16)), "v")
    cwg_mark = ctx.cnt["v"]

    exp_dw = {}

    exp_dw2 = {}

    def load_expert_w(e, gate_t, gate_t2=None):
        @block.gpsimd
        def _(g, e=e, gate_t=gate_t, gate_t2=gate_t2):
            if gate_t:
                ctx.wait(g, "t", gate_t)
            ctx.dma(g, "dw", w1g[e % 8].rearrange("p (k h) -> p k h", h=256),
                    io["w1"][e].rearrange("(k p) h -> p k h", p=128))
            if gate_t2:
                ctx.wait(g, "t", gate_t2)
            ctx.dma(g, "dmo", w2g[e % 8].rearrange("p (k d) -> p k d", d=1024),
                    io["w2"][e].rearrange("(k p) d -> p k d", p=128))
        exp_dw[e] = ctx.cnt["dw"]
        exp_dw2[e] = ctx.cnt["dmo"]

    idx_dmam = {}

    def load_pair_idx(q, war_gt):
        @block.sync
        def _(sync, q=q, war_gt=war_gt):
            if war_gt is not None:
                ctx.wait(sync, war_gt[0], war_gt[1])
            for h in range(2):
                ctx.dma(sync, "dma", idxg[q % 4][:, 24 * h:24 * (h + 1)],
                        io["wld"][2 * q + h:2 * q + h + 1, :]
                        .rearrange("o (p c) -> o p c", p=16)[:, :, 0:24]
                        .broadcast_to([8, 16, 24]))
        idx_dmam[q] = ctx.cnt["dma"]

    iu_dmam = {}

    def load_iu(e, war_sc):
        @block.sync
        def _(sync, e=e, war_sc=war_sc):
            if war_sc is not None:
                ctx.wait(sync, war_sc[0], war_sc[1])
            for c in range(3):
                ctx.dma(sync, "dma", iu32[e % 16][c],
                        io["nld"][e:e + 1, 128 * c:128 * (c + 1)]
                        .rearrange("o (p one) -> (o p) one", p=128))
        iu_dmam[e] = ctx.cnt["dma"]

    gcnt = {0: 0, 1: 0, 2: 0, 3: 0}
    GKEYS = ("gt", "gu", "gv", "gw")

    def issue_gather(q, extra_t=0):
        key = GKEYS[q % 4]

        @block.gpsimd
        def _(g, q=q, key=key, extra_t=extra_t, prior=gcnt[q % 4]):
            if prior > 0:
                ctx.wait(g, key, 16 * prior)
            ctx.wait(g, "dma", idx_dmam[q])
            if q < 4:
                ctx.wait(g, "t", shared_end_t)
            if q == 3:
                ctx.wait(g, "t", cwg_t)
            if extra_t:
                ctx.wait(g, "t", extra_t)
            g.dma_gather(xgp[q % 4].rearrange("p (c i) -> p c i", c=8),
                         io["xnatb"], idxg[q % 4], 2 * CAP, 2 * CAP, 1024,
                         transpose=True).then_inc(ctx.sem[key], 16)
        gcnt[q % 4] += 1
        ctx.cnt[key] += 16

    scnt = {0: 0, 1: 0}
    evac_mark = {}

    def issue_scatters(e):
        key = "sc" if e % 2 == 0 else "sd"

        @block.gpsimd
        def _(g, e=e, key=key, prior=scnt[e % 2], em=evac_mark[(e, 2)]):
            ctx.wait(g, "dma", iu_dmam[e])
            ctx.wait(g, "v", em)
            acc = io["acc0"] if e % 2 == 0 else io["acc1"]
            for c in range(3):
                if SIM_SERIAL and 48 * prior + 16 * c > 0:
                    ctx.wait(g, key, 48 * prior + 16 * c)
                elif not SIM_SERIAL and prior > 0 and c == 0:
                    ctx.wait(g, key, 48 * prior)
                g.indirect_dma_start(
                    acc, bass.IndirectOffsetOnAxis(ap=iu32[e % 16][c], axis=0),
                    h2sb[e % 4][c], None,
                    compute_op=ALU.add).then_inc(ctx.sem[key], 16)
        scnt[e % 2] += 1
        ctx.cnt[key] += 48

    for e in range(8):
        load_expert_w(e, g1_tc3_t[0] if e == 0 else 0,
                      gate_t2=shared_end_t if e == 0 else 0)
    for q in range(4):
        load_pair_idx(q, None)
    for e in range(8):
        load_iu(e, None)
    issue_gather(0)
    issue_gather(1)
    issue_gather(2)
    issue_gather(3)
    load_pair_idx(4, ("gt", 16))

    h1_t = {}
    h2_t = {}
    silu_s_mark = {}
    YS = [Y[:, 0:1024], Y[:, 1024:2048]]
    YSh = [[Y[:, 0:512], Y[:, 512:1024]], [Y[:, 1024:1536], Y[:, 1536:2048]]]
    evac_g = {}

    for e in range(64):
        par = e % 2
        q = e // 2

        @block.tensor
        def _(t, e=e, par=par, q=q):
            key = GKEYS[q % 4]
            ctx.wait(t, key, 16 * (q // 4 + 1))
            ctx.wait(t, "dw", exp_dw[e])
            if e >= 2:
                ctx.wait(t, "s", silu_s_mark[e - 2])
            else:
                ctx.wait(t, "s", shared_end_s)
                ctx.wait(t, "v", cwg_mark)
            for m in range(2):
                for k in range(8):
                    ctx.stamp(t.matmul(
                        H1S[par][m],
                        w1g[e % 8][:, 256 * k + 128 * m:256 * k + 128 * (m + 1)],
                        xgp[q % 4][:, 768 * k + CAP * par:
                                   768 * k + CAP * (par + 1)],
                        start=(k == 0), stop=(k == 7)), "t")
        h1_t[e] = ctx.cnt["t"]

        @block.scalar
        def _(s, e=e, par=par, h1m=h1_t[e]):
            ctx.wait(s, "t", h1m)
            if e >= 2:
                ctx.wait(s, "t", h2_t[e - 2])
            for m in range(2):
                ctx.stamp(s.activation(
                    silT[par][m], H1S[par][m], AF.Silu,
                    bias=b1t[:, 64 * m + e:64 * m + e + 1]), "s")
        silu_s_mark[e] = ctx.cnt["s"]

        for c in range(3):
            g_idx = 3 * e + c
            slot = g_idx % 2

            @block.tensor
            def _(t, e=e, c=c, slot=slot, sm=silu_s_mark[e],
                  emp=evac_g.get(g_idx - 2, 0)):
                ctx.wait(t, "s", sm)
                if c == 0:
                    ctx.wait(t, "dmo", exp_dw2[e])
                if emp:
                    ctx.wait(t, "v", emp)
                else:
                    ctx.wait(t, "v", shared_end_v)
                for half in range(2):
                    for kk in range(2):
                        ctx.stamp(t.matmul(
                            YSh[slot][half],
                            silT[e % 2][kk][:, 128 * c:128 * (c + 1)],
                            w2g[e % 8][:, 1024 * kk + 512 * half:
                                       1024 * kk + 512 * (half + 1)],
                            start=(kk == 0), stop=(kk == 1)), "t")
            h2c = ctx.cnt["t"]
            if c == 2:
                h2_t[e] = h2c

            @block.vector
            def _(v, e=e, par=par, c=c, slot=slot, h2c=h2c):
                ctx.wait(v, "t", h2c)
                if e < 2:
                    ctx.wait(v, "v", cwg_mark)
                if e >= 4:
                    ctx.wait(v, "sc" if par == 0 else "sd",
                             48 * (e // 2 - 1))
                ctx.stamp(v.tensor_scalar(
                    h2sb[e % 4][c], YS[slot],
                    cwgF[:, 64 * c + e:64 * c + e + 1], None, ALU.mult), "v")
            evac_mark[(e, c)] = ctx.cnt["v"]
            evac_g[g_idx] = ctx.cnt["v"]

        if par == 0 and e >= 2 and q + 3 < 32:
            issue_gather(q + 3, extra_t=h1_t[e - 1])
        if e >= 1:
            issue_scatters(e - 1)
        if e + 8 < 64:
            load_expert_w(e + 8, h2_t[e])
        if par == 1 and q + 5 < 32:
            qq = q + 5
            load_pair_idx(qq, (GKEYS[qq % 4], 16 * ((qq - 4) // 4 + 1)))
        if e + 8 < 64:
            load_iu(e + 8, ("sc" if e % 2 == 0 else "sd",
                            48 * ((e - 8) // 2 + 1)) if e >= 8 else None)

    issue_scatters(63)

    ctx.release()


def emit_acc_merge(ctx, block, io, zfull, P, Y):
    ctx.mark()
    acr = [[ctx.alloc(f"acr{b}{i}", [128, 1024], BF16) for i in range(2)]
           for b in range(2)]
    acs = [ctx.alloc(f"acs{i}", [128, 1024], BF16) for i in range(2)]
    idnm = ctx.alloc("idnm", [128, 128], BF16)

    @block.sync
    def _(sync):
        ctx.dma(sync, "dma", idnm, io["c_idn"])

    @block.tensor
    def _(t):
        ctx.wait(t, "s")
        ctx.wait(t, "dma")

    vmadd = {}
    vmsum = {}
    dmam = {}
    for r in range(16):
        rb = r % 2

        @block.sync
        def _(sync, r=r, rb=rb, wv=vmsum.get(r - 2, 0)):
            if wv:
                ctx.wait(sync, "v", wv)
            ctx.dma(sync, "dma", acr[0][rb], io["acc0"][128 * r:128 * (r + 1), :])
            ctx.dma(sync, "dma", acr[1][rb], io["acc1"][128 * r:128 * (r + 1), :])
        dmam[r] = ctx.cnt["dma"]

        @block.vector
        def _(v, rb=rb, dm=dmam[r]):
            ctx.wait(v, "dma", dm)
            ctx.stamp(v.tensor_tensor(acs[rb], acr[0][rb], acr[1][rb],
                                      ALU.add), "v")
        vmsum[r] = ctx.cnt["v"]

        @block.tensor
        def _(t, rb=rb, vm=vmsum[r], wv=vmadd.get(r - 2, 0)):
            ctx.wait(t, "v", vm)
            if wv:
                ctx.wait(t, "v", wv)
            for k in range(8):
                ctx.stamp(t.transpose(
                    Y[:, 512 * rb + 64 * k:512 * rb + 64 * (k + 1)].bitcast(BF16),
                    acs[rb][:, 128 * k:128 * (k + 1)], idnm), "t")
        tm = ctx.cnt["t"]

        @block.vector
        def _(v, r=r, rb=rb, tm=tm):
            ctx.wait(v, "t", tm)
            for k in range(8):
                sl = zfull[k][:, 128 * r:128 * (r + 1)]
                ctx.stamp(v.tensor_tensor(
                    sl, sl,
                    Y[:, 512 * rb + 64 * k:512 * rb + 64 * (k + 1)].bitcast(BF16),
                    ALU.add), "v")
        vmadd[r] = ctx.cnt["v"]
    ctx.release()




def build_p0():
    nc = bass.Bass("TRN2", target_bir_lowering=False)
    shapes = {
        "xt": (D, T), "dtwT": (D, D), "BwT": (D, 32), "CwT": (D, 32),
        "dtb": (128, 8), "Bb": (32, 1), "Cb": (32, 1), "Dsk": (128, 8),
        "logAcol": (128, 256),
    }
    io = {}
    for name, shp in shapes.items():
        io[name] = nc.dram_tensor(name, list(shp), F32, kind="ExternalInput").ap()
    for name, arr in _CONSTS.items():
        io[name] = nc.dram_tensor(name, list(arr.shape), BF16, kind="ExternalInput").ap()
    out = nc.dram_tensor("xt_out", [D, T], F32, kind="ExternalOutput").ap()
    ctx = Ctx(nc)
    P = nc.alloc_psum_tensor("P", [128, T], F32).ap()
    Y = nc.alloc_psum_tensor("Y", [128, T], F32).ap()
    if True:
        block = DeferBlock()
        ctx.register_consts(block, [1.0, 1e-5])
        ctx.mark()
        A = [ctx.alloc(f"A{i}", [128, T], BF16) for i in range(8)]
        emit_l0_attn_core(ctx, block, io, A, P, Y)
        ctx.mark()
        zfull = [ctx.alloc(f"zf{i}", [128, T], F32) for i in range(8)]

        @block.sync
        def _(sync):
            ctx.wait(sync, "v")
            ctx.wait(sync, "t")
            ctx.wait(sync, "s")
            for i in range(8):
                ctx.dma(sync, "dma", zfull[i], io["xt"][128 * i:128 * (i + 1), :])

        @block.vector
        def _(v):
            ctx.wait(v, "dma")

        emit_ln(ctx, block, zfull, A, P, Y, 1.0, out)
        ctx.release()
        ctx.release()

        @block.gpsimd
        def _(g):
            ctx.wait(g, "dmo")
        block.replay(nc)
    mybir.codegen_inst_isa_subclasses(nc)
    return ctx, nc


def build_p1():
    nc = bass.Bass("TRN2", target_bir_lowering=False)
    io = {}
    io["xt"] = nc.dram_tensor("xt", [D, T], F32, kind="ExternalInput").ap()
    io["wblk"] = nc.dram_tensor("wblk", [D, D], F32, kind="ExternalInput").ap()
    io["wr8"] = nc.dram_tensor("wr8", [128, 8], F32, kind="ExternalInput").ap()
    for name, arr in _CONSTS.items():
        io[name] = nc.dram_tensor(name, list(arr.shape), BF16, kind="ExternalInput").ap()
    out = nc.dram_tensor("xt_out", [D, T], F32, kind="ExternalOutput").ap()
    ctx = Ctx(nc)
    P = nc.alloc_psum_tensor("P", [128, T], F32).ap()
    Y = nc.alloc_psum_tensor("Y", [128, T], F32).ap()
    if True:
        block = DeferBlock()
        ctx.register_consts(block, [1.0, 1e-5])
        ctx.mark()
        A = [ctx.alloc(f"A{i}", [128, T], BF16) for i in range(8)]
        zfull = [ctx.alloc(f"zf{i}", [128, T], F32) for i in range(8)]

        @block.sync
        def _(sync):
            for i in range(8):
                ctx.dma(sync, "dma", zfull[i], io["xt"][128 * i:128 * (i + 1), :])

        @block.vector
        def _(v):
            ctx.wait(v, "dma")
            ctx.stamp(v.engine_nop(), "v")

        emit_l0_ffn(ctx, block, io, A, zfull, P, Y)
        emit_ln(ctx, block, zfull, A, P, Y, 1.0, out)
        ctx.release()

        @block.gpsimd
        def _(g):
            ctx.wait(g, "dmo")
        block.replay(nc)
    mybir.codegen_inst_isa_subclasses(nc)
    return ctx, nc


def build_p2():
    nc = bass.Bass("TRN2", target_bir_lowering=False)
    io = {}
    io["xt"] = nc.dram_tensor("xt", [D, T], F32, kind="ExternalInput").ap()
    for nm, shp in (("n1w", (128, 8)), ("n2w", (128, 8)),
                    ("gwq", (D, D)), ("gkvd", (D, 256)), ("gkup", (256, 256)),
                    ("gvup", (256, 256)), ("gwo", (D, D)), ("lwq", (D, D)),
                    ("lwk", (D, 256)), ("lwv", (D, 256)), ("lwo", (D, D))):
        io[nm] = nc.dram_tensor(nm, list(shp), F32, kind="ExternalInput").ap()
    for name, arr in _CONSTS.items():
        io[name] = nc.dram_tensor(name, list(arr.shape), BF16, kind="ExternalInput").ap()
    out = nc.dram_tensor("xt_out", [D, T], F32, kind="ExternalOutput").ap()
    ctx = Ctx(nc)
    P = nc.alloc_psum_tensor("P", [128, T], F32).ap()
    Y = nc.alloc_psum_tensor("Y", [128, T], F32).ap()
    if True:
        block = DeferBlock()
        ctx.register_consts(block, [1.0, 1e-5, 1e-6])
        ctx.mark()
        A = [ctx.alloc(f"A{i}", [128, T], BF16) for i in range(8)]
        emit_l1_attn(ctx, block, io, A, P, Y)
        ctx.mark()
        zfull = [ctx.alloc(f"zf{i}", [128, T], F32) for i in range(8)]

        @block.sync
        def _(sync):
            ctx.wait(sync, "v")
            ctx.wait(sync, "t")
            ctx.wait(sync, "s")
            for i in range(8):
                ctx.dma(sync, "dma", zfull[i], io["xt"][128 * i:128 * (i + 1), :])

        @block.vector
        def _(v):
            ctx.wait(v, "dma")
            ctx.stamp(v.engine_nop(), "v")

        emit_ln(ctx, block, zfull, A, P, Y, 2.0, out)
        ctx.release()
        ctx.release()

        @block.gpsimd
        def _(g):
            ctx.wait(g, "dmo")
        block.replay(nc)
    mybir.codegen_inst_isa_subclasses(nc)
    return ctx, nc


def build_p3():
    nc = bass.Bass("TRN2", target_bir_lowering=False)
    io = {}
    io["xt"] = nc.dram_tensor("xt", [D, T], F32, kind="ExternalInput").ap()
    for nm, shp in (("rwT", (D, 64)), ("rbc", (64, 1)), ("w1", (64, D, 256)),
                    ("w2", (64, 256, D)), ("b1t", (128, 128)), ("b2c", (64, D)),
                    ("sw1", (2, D, 2048)), ("sw2", (2, 2048, D)),
                    ("sb1t0", (128, 16)), ("sb1t1", (128, 16)),
                    ("sb2c", (2, D)), ("idnf", (128, 128))):
        io[nm] = nc.dram_tensor(nm, list(shp), F32, kind="ExternalInput").ap()
    for name, arr in _CONSTS.items():
        io[name] = nc.dram_tensor(name, list(arr.shape), BF16,
                                  kind="ExternalInput").ap()
    io["iotam"] = nc.dram_tensor("iotam", [64, T], I16,
                                 kind="ExternalInput").ap()
    io["iotan"] = nc.dram_tensor("iotan", [64, 384], I16,
                                 kind="ExternalInput").ap()
    io["xnatb"] = nc.dram_tensor("xnatb", [2432, 1024], BF16,
                                 kind="Internal").ap()
    io["acc0"] = nc.dram_tensor("acc0", [2432, 1024], BF16,
                                kind="Internal").ap()
    io["acc1"] = nc.dram_tensor("acc1", [2432, 1024], BF16,
                                kind="Internal").ap()
    io["wld"] = nc.dram_tensor("wld", [64, 512], I16, kind="Internal").ap()
    io["nld"] = nc.dram_tensor("nld", [64, 384], U32, kind="Internal").ap()
    out = nc.dram_tensor("xt_out", [D, T], F32, kind="ExternalOutput").ap()
    ctx = Ctx(nc)
    P = nc.alloc_psum_tensor("P", [128, T], F32).ap()
    Y = nc.alloc_psum_tensor("Y", [128, T], F32).ap()
    if True:
        block = DeferBlock()
        ctx.register_consts(block, [1.0, 1e-5, 1e-6])
        ctx.mark()
        A = [ctx.alloc(f"A{i}", [128, T], BF16) for i in range(8)]
        emit_l1_ffn(ctx, block, io, A, P, Y)
        ctx.mark()
        zfull = [ctx.alloc(f"zf{i}", [128, T], F32) for i in range(8)]

        @block.sync
        def _(sync):
            ctx.wait(sync, "v")
            ctx.wait(sync, "t")
            ctx.wait(sync, "s")
            ctx.wait(sync, "sc")
            ctx.wait(sync, "sd")
            ctx.wait(sync, "gt")
            ctx.wait(sync, "gu")
            ctx.wait(sync, "gv")
            ctx.wait(sync, "gw")
            for i in range(8):
                ctx.dma(sync, "dma", zfull[i], io["xt"][128 * i:128 * (i + 1), :])

        @block.vector
        def _(v):
            ctx.wait(v, "dma")
            ctx.stamp(v.engine_nop(), "v")

        emit_acc_merge(ctx, block, io, zfull, P, Y)
        emit_ln(ctx, block, zfull, A, P, Y, 1.0, out)
        ctx.release()
        ctx.release()

        @block.gpsimd
        def _(g):
            ctx.wait(g, "dmo")
        block.replay(nc)
    mybir.codegen_inst_isa_subclasses(nc)
    return ctx, nc


def barrier(ctx, block):
    marks = {k: ctx.cnt[k] for k in ("t", "v", "s", "dma", "dw", "dmo")}

    @block.tensor
    def _(t):
        for k in ("v", "s", "dma", "dw", "dmo"):
            ctx.wait(t, k, marks[k])

    @block.vector
    def _(v):
        for k in ("t", "s", "dma", "dw", "dmo"):
            ctx.wait(v, k, marks[k])

    @block.scalar
    def _(s):
        for k in ("t", "v", "dma", "dw", "dmo"):
            ctx.wait(s, k, marks[k])

    @block.sync
    def _(sync):
        for k in ("t", "v", "s", "dw", "dmo"):
            ctx.wait(sync, k, marks[k])

    @block.gpsimd
    def _(g):
        for k in ("t", "v", "s", "dma", "dmo"):
            ctx.wait(g, k, marks[k])


def build_fused():
    nc = bass.Bass("TRN2", target_bir_lowering=False)
    io = {}
    shapes = {
        "xt": (D, T), "dtwT": (D, D), "BwT": (D, 32), "CwT": (D, 32),
        "dtb": (128, 8), "Bb": (32, 1), "Cb": (32, 1), "Dsk": (128, 8),
        "logAcol": (128, 256),
        "wblk": (D, D), "wr8": (128, 8),
        "n1w": (128, 8), "n2w": (128, 8), "gwq": (D, D), "gkvd": (D, 256),
        "gkup": (256, 256), "gvup": (256, 256), "gwo": (D, D), "lwq": (D, D),
        "lwk": (D, 256), "lwv": (D, 256), "lwo": (D, D),
        "rwT": (D, 64), "rbc": (64, 1), "w1": (64, D, 256), "w2": (64, 256, D),
        "b1t": (128, 128), "b2c": (64, D), "sw1": (2, D, 2048),
        "sw2": (2, 2048, D), "sb1t0": (128, 16), "sb1t1": (128, 16),
        "sb2c": (2, D), "idnf": (128, 128),
    }
    for nm, shp in shapes.items():
        io[nm] = nc.dram_tensor(nm, list(shp), F32, kind="ExternalInput").ap()
    for name, arr in _CONSTS.items():
        io[name] = nc.dram_tensor(name, list(arr.shape), BF16,
                                  kind="ExternalInput").ap()
    io["iotam"] = nc.dram_tensor("iotam", [64, T], I16,
                                 kind="ExternalInput").ap()
    io["iotan"] = nc.dram_tensor("iotan", [64, 384], I16,
                                 kind="ExternalInput").ap()
    io["xnatb"] = nc.dram_tensor("xnatb", [2432, 1024], BF16,
                                 kind="Internal").ap()
    io["acc0"] = nc.dram_tensor("acc0", [2432, 1024], BF16,
                                kind="Internal").ap()
    io["acc1"] = nc.dram_tensor("acc1", [2432, 1024], BF16,
                                kind="Internal").ap()
    io["wld"] = nc.dram_tensor("wld", [64, 512], I16, kind="Internal").ap()
    io["nld"] = nc.dram_tensor("nld", [64, 384], U32, kind="Internal").ap()
    out = nc.dram_tensor("xt_out", [D, T], F32, kind="ExternalOutput").ap()
    xmir = nc.dram_tensor("xmir", [D, T], F32, kind="Internal").ap()

    ctx = Ctx(nc)
    P = nc.alloc_psum_tensor("P", [128, T], F32).ap()
    Y = nc.alloc_psum_tensor("Y", [128, T], F32).ap()

    def ln_stage(io_x, A, coef, out_dram):
        ctx.mark()
        zfull = [ctx.alloc(f"zf{i}", [128, T], F32) for i in range(8)]

        @block.sync
        def _(sync):
            ctx.wait(sync, "v")
            ctx.wait(sync, "t")
            ctx.wait(sync, "s")
            for i in range(8):
                ctx.dma(sync, "dma", zfull[i], io_x[128 * i:128 * (i + 1), :])

        @block.vector
        def _(v):
            ctx.wait(v, "dma")
            ctx.stamp(v.engine_nop(), "v")

        emit_ln(ctx, block, zfull, A, P, Y, coef, out_dram)
        ctx.release()

    if True:
        block = DeferBlock()
        ctx.register_consts(block, [1.0, 1e-5, 1e-6])

        ctx.mark()
        A = [ctx.alloc(f"A0_{i}", [128, T], BF16) for i in range(8)]
        emit_l0_attn_core(ctx, block, io, A, P, Y)
        ln_stage(io["xt"], A, 1.0, xmir)
        ctx.release()
        barrier(ctx, block)

        io1 = dict(io)
        io1["xt"] = xmir
        ctx.mark()
        A = [ctx.alloc(f"A1_{i}", [128, T], BF16) for i in range(8)]
        zfull = [ctx.alloc(f"z1_{i}", [128, T], F32) for i in range(8)]

        @block.sync
        def _(sync):
            for i in range(8):
                ctx.dma(sync, "dma", zfull[i], xmir[128 * i:128 * (i + 1), :])

        @block.vector
        def _(v):
            ctx.wait(v, "dma")
            ctx.stamp(v.engine_nop(), "v")

        emit_l0_ffn(ctx, block, io1, A, zfull, P, Y)
        emit_ln(ctx, block, zfull, A, P, Y, 1.0, xmir)
        ctx.release()
        barrier(ctx, block)

        ctx.mark()
        A = [ctx.alloc(f"A2_{i}", [128, T], BF16) for i in range(8)]
        emit_l1_attn(ctx, block, io1, A, P, Y)
        ln_stage(xmir, A, 2.0, xmir)
        ctx.release()
        barrier(ctx, block)

        ctx.mark()
        A = [ctx.alloc(f"A3_{i}", [128, T], BF16) for i in range(8)]
        emit_l1_ffn(ctx, block, io1, A, P, Y)
        ctx.mark()
        zfull = [ctx.alloc(f"zf3_{i}", [128, T], F32) for i in range(8)]

        @block.sync
        def _(sync):
            ctx.wait(sync, "v")
            ctx.wait(sync, "t")
            ctx.wait(sync, "s")
            ctx.wait(sync, "sc")
            ctx.wait(sync, "sd")
            ctx.wait(sync, "gt")
            ctx.wait(sync, "gu")
            ctx.wait(sync, "gv")
            ctx.wait(sync, "gw")
            for i in range(8):
                ctx.dma(sync, "dma", zfull[i], xmir[128 * i:128 * (i + 1), :])

        @block.vector
        def _(v):
            ctx.wait(v, "dma")
            ctx.stamp(v.engine_nop(), "v")

        emit_acc_merge(ctx, block, io1, zfull, P, Y)
        emit_ln(ctx, block, zfull, A, P, Y, 1.0, out)
        ctx.release()
        ctx.release()

        @block.gpsimd
        def _(g):
            ctx.wait(g, "dmo")
        block.replay(nc)
    mybir.codegen_inst_isa_subclasses(nc)
    return ctx, nc


def kernel(**inputs):
    n = 8
    base = {}
    base.update(prep_l0ffn(inputs, 0))
    base.update(prep_l1attn(inputs, 0))
    base.update(prep_l1ffn(inputs, 0))
    feeds = []
    for b in range(n):
        f = dict(base)
        f.update(prep_l0attn(inputs, b))
        feeds.append(f)
    ctx, nc = build_fused()
    res = run_bass_kernel_spmd(nc, feeds, core_ids=list(range(n)))
    out = np.stack([res.results[b]["xt_out"].T for b in range(n)])
    return np.ascontiguousarray(out)


# version 09 (final)
# speedup vs baseline: 1.0145x; 1.0145x over previous
import numpy as np
import ml_dtypes
import concourse.bass as bass
import concourse.mybir as mybir
from concourse.bass_utils import run_bass_kernel_spmd

F32 = mybir.dt.float32
BF16 = mybir.dt.bfloat16
U8 = mybir.dt.uint8
I16 = mybir.dt.int16
U32 = mybir.dt.uint32
from concourse import library_config
AF = mybir.ActivationFunctionType
ALU = mybir.AluOpType
AX = mybir.AxisListType
BF = ml_dtypes.bfloat16

D = 1024
T = 2048
SIM_SERIAL = False



def _host_constants():
    c = {}
    p = np.arange(128)
    seld = np.zeros((128, 512), np.float32)
    for v in range(4):
        seld[32 * v + (p % 32), 128 * v + p] = 1.0
    c["c_seld"] = seld.astype(BF)
    sdy = np.zeros((128, 32), np.float32)
    sdy[p, p % 32] = 1.0
    c["c_seldy"] = sdy.astype(BF)
    selb = np.zeros((32, 1024), np.float32)
    for nb in range(8):
        selb[4 * nb + p // 32, 128 * nb + p] = 1.0
    c["c_selb"] = selb.astype(BF)
    c["c_onec"] = np.ones((128, 1), np.float32).astype(BF)
    c["c_oner"] = np.ones((1, 128), np.float32).astype(BF)
    c["c_idn"] = np.eye(128, dtype=np.float32).astype(BF)
    c["c_i64"] = np.eye(64, dtype=np.float32).astype(BF)
    ohc = np.zeros((16, 16 * 128), np.float32)
    for r in range(16):
        ohc[r, 128 * r:128 * (r + 1)] = 1.0
    c["c_ohc"] = ohc.astype(BF)
    hsA = np.zeros((128, 8 * 16), np.float32)
    for j in range(8):
        for k in range(128):
            hsA[k, 16 * j + 2 * j + k // 64] = 1.0
    c["c_hselA"] = hsA.astype(BF)
    hsB = np.zeros((16, 8 * 128), np.float32)
    for j in range(8):
        for q in range(128):
            hsB[2 * j + q // 64, 128 * j + q] = 1.0
    c["c_hselB"] = hsB.astype(BF)
    tk = np.arange(128)[:, None]
    tq = np.arange(512)[None, :]
    gm = np.concatenate([np.where(tq >= tk + d, 0.0, -1e30).astype(np.float32)
                         for d in (0, 128, 256, 384)], axis=1)
    c["c_gmask"] = gm.astype(BF)
    deltas = (-256, -128, 0, 128, 256, 384)
    lm = np.concatenate([np.where((tq - tk - d >= 0) & (tq - tk - d < 256),
                                  0.0, -1e30).astype(np.float32)
                         for d in deltas], axis=1)
    c["c_lmask"] = lm.astype(BF)
    p = np.arange(128)
    i_p = (p % 64) // 2
    for base, tag in ((1e6, "g"), (1e4, "l")):
        inv = 1.0 / base ** (np.arange(0, 64, 2) / 64.0)
        fr = np.arange(T)[:, None] * inv[None, :]
        cosv = np.cos(fr).astype(np.float32)
        sinv = np.sin(fr).astype(np.float32)
        ct = cosv[:, i_p].T.copy()
        stt = sinv[:, i_p].T.copy()
        sgn = np.where(p % 2 == 0, -1.0, 1.0)[:, None]
        c[f"c_cos{tag}"] = ct.astype(BF)
        c[f"c_sin{tag}"] = (stt * sgn).astype(BF)
    return c


_CONSTS = _host_constants()


def prep_l0attn(i, b):
    d = {}
    d["xt"] = np.ascontiguousarray(i["L0_attn_x"][b].T).astype(np.float32)
    d["dtwT"] = np.ascontiguousarray(i["L0_attn_dt_w"].T)
    d["BwT"] = np.ascontiguousarray(i["L0_attn_B_w"].T)
    d["CwT"] = np.ascontiguousarray(i["L0_attn_C_w"].T)
    d["dtb"] = np.ascontiguousarray(i["L0_attn_dt_b"].reshape(8, 128).T)
    d["Bb"] = np.ascontiguousarray(i["L0_attn_B_b"].reshape(32, 1))
    d["Cb"] = np.ascontiguousarray(i["L0_attn_C_b"].reshape(32, 1))
    d["Dsk"] = np.ascontiguousarray(i["L0_attn_D_skip"].reshape(8, 128).T)
    logA = i["L0_attn_log_A"]
    pp = np.arange(128)
    acc = np.empty((128, 256), np.float32)
    for g in range(32):
        for nb in range(8):
            acc[:, 8 * g + nb] = logA[32 * g + (pp % 32), 4 * nb + pp // 32]
    d["logAcol"] = acc
    d.update(_CONSTS)
    return d


def prep_l0ffn(i, b):
    d = {}
    d["wblk"] = np.ascontiguousarray(i["L0_ffn_w_block"])
    d["wr8"] = np.ascontiguousarray(i["L0_ffn_w_router"].reshape(8, 128).T)
    d.update(_CONSTS)
    return d


def prep_l1attn(i, b):
    d = {}
    d["n1w"] = np.ascontiguousarray(i["L1_attn_norm1_w"].reshape(8, 128).T)
    d["n2w"] = np.ascontiguousarray(i["L1_attn_norm2_w"].reshape(8, 128).T)
    d["gwq"] = np.ascontiguousarray(i["L1_attn_g_wq"])
    d["gkvd"] = np.ascontiguousarray(i["L1_attn_g_kv_down"])
    d["gkup"] = np.ascontiguousarray(i["L1_attn_g_k_up"])
    d["gvup"] = np.ascontiguousarray(i["L1_attn_g_v_up"])
    d["gwo"] = np.ascontiguousarray(i["L1_attn_g_wo"])
    d["lwq"] = np.ascontiguousarray(i["L1_attn_l_wq"])
    d["lwk"] = np.ascontiguousarray(i["L1_attn_l_wk"])
    d["lwv"] = np.ascontiguousarray(i["L1_attn_l_wv"])
    d["lwo"] = np.ascontiguousarray(i["L1_attn_l_wo"])
    d.update(_CONSTS)
    d.pop("l_deltas", None)
    return d


def prep_l1ffn(i, b):
    d = {}
    d["rwT"] = np.ascontiguousarray(i["L1_ffn_router_w"])
    d["rbc"] = np.ascontiguousarray(i["L1_ffn_router_b"].reshape(64, 1))
    d["w1"] = np.ascontiguousarray(i["L1_ffn_w1"])
    d["w2"] = np.ascontiguousarray(i["L1_ffn_w2"])
    b1 = i["L1_ffn_b1"]
    b1t = np.empty((128, 128), np.float32)
    for m in range(2):
        for e in range(64):
            b1t[:, 64 * m + e] = b1[e, 128 * m:128 * (m + 1)]
    d["b1t"] = b1t
    d["b2c"] = np.ascontiguousarray(i["L1_ffn_b2"])
    d["sw1"] = np.ascontiguousarray(i["L1_ffn_sw1"])
    d["sw2"] = np.ascontiguousarray(i["L1_ffn_sw2"])
    sb1 = i["L1_ffn_sb1"]
    sb1t = np.empty((2, 128, 16), np.float32)
    for s in range(2):
        for pe in range(8):
            for m in range(2):
                sb1t[s, :, 2 * pe + m] = sb1[s, 256 * pe + 128 * m:
                                             256 * pe + 128 * (m + 1)]
    d["sb1t0"] = np.ascontiguousarray(sb1t[0])
    d["sb1t1"] = np.ascontiguousarray(sb1t[1])
    d["idnf"] = np.eye(128, dtype=np.float32)
    d["sb2c"] = np.ascontiguousarray(i["L1_ffn_sb2"])
    d["iotam"] = np.ascontiguousarray(np.broadcast_to(
        (np.arange(2048, dtype=np.int32) - 2048).astype(np.int16), (64, 2048)))
    d["iotan"] = np.ascontiguousarray(np.broadcast_to(
        np.arange(384, dtype=np.int16), (64, 384)))
    d.update(_CONSTS)
    return d




class Ctx:
    def __init__(self, nc):
        self.nc = nc
        self.sem = {}
        self.cnt = {}
        self._cm = []
        for name in ("dma", "dw", "dmo", "t", "v", "s", "g", "gt", "gu", "gv", "gw", "sc", "sd", "di"):
            cm = nc.semaphore(name)
            self.sem[name] = cm.__enter__()
            self._cm.append(cm)
            self.cnt[name] = 0
        self.sb_off = 18048
        self.sb_limit = 228992
        self.marks = []
        self.offsets = {}
        self.uid = 0

    def stamp(self, inst, key, inc=1):
        inst.then_inc(self.sem[key], inc)
        self.cnt[key] += inc
        return self.cnt[key]

    def dma(self, q, key, dst, src):
        if self.cnt[key] > 0:
            self.wait(q, key, self.cnt[key])
        return self.stamp(q.dma_start(out=dst, in_=src), key, 16)

    def wait(self, eng, key, val=None):
        if val is None:
            val = self.cnt[key]
        eng.wait_ge(self.sem[key], val)

    def alloc(self, name, shape, dtype):
        bytesz = {F32: 4, BF16: 2, U8: 1, I16: 2, U32: 4}[dtype]
        fb = int(np.prod(shape[1:])) * bytesz
        fb = (fb + 31) // 32 * 32
        off = self.sb_off
        assert off + fb <= self.sb_limit, f"SBUF overflow {name}: {off}+{fb}>{self.sb_limit}"
        self.sb_off = off + fb
        self.uid += 1
        self.offsets[name] = off
        return self.nc.alloc_sbuf_tensor_at(f"{name}_{self.uid}", list(shape), dtype,
                                            offset=off).ap()

    def alias(self, name, shape, dtype, at_name):
        self.uid += 1
        return self.nc.alloc_sbuf_tensor_at(f"{name}_{self.uid}", list(shape), dtype,
                                            offset=self.offsets[at_name]).ap()

    def register_consts(self, block, values):
        cpool = self.alloc("cpool", [128, len(values)], F32)
        ctx = self

        @block.vector
        def _(v):
            for i, val in enumerate(values):
                sl = cpool[:, i:i + 1]
                ctx.stamp(v.memset(sl, float(val)), "v")
                ctx.nc.const_aps.register(F32, float(val), sl)

    def mark(self):
        self.marks.append(self.sb_off)

    def release(self):
        self.sb_off = self.marks.pop()


class _Rec:
    __slots__ = ("name", "args", "kwargs", "incs")

    def __init__(self, name, args, kwargs):
        self.name = name
        self.args = args
        self.kwargs = kwargs
        self.incs = []

    def then_inc(self, sem, n):
        self.incs.append((sem, n))
        return self


class _EngProxy:
    def __init__(self, stream):
        self._stream = stream

    def __getattr__(self, name):
        stream = self._stream

        def call(*a, **kw):
            r = _Rec(name, a, kw)
            stream.append(r)
            return r
        return call


class DeferBlock:

    ENGINES = ("tensor", "vector", "scalar", "sync", "gpsimd")

    def __init__(self):
        self.streams = {k: [] for k in self.ENGINES}

    def _dec(self, key):
        def deco(f):
            f(_EngProxy(self.streams[key]))
        return deco

    @property
    def tensor(self):
        return self._dec("tensor")

    @property
    def vector(self):
        return self._dec("vector")

    @property
    def scalar(self):
        return self._dec("scalar")

    @property
    def sync(self):
        return self._dec("sync")

    @property
    def gpsimd(self):
        return self._dec("gpsimd")

    def replay(self, nc, chunk=1200):
        with nc.Block() as block:
            for key in self.ENGINES:
                stream = self.streams[key]
                for lo in range(0, max(len(stream), 1), chunk):
                    seg = stream[lo:lo + chunk]

                    def emit(eng, seg=seg):
                        for r in seg:
                            inst = getattr(eng, r.name)(*r.args, **r.kwargs)
                            for sem, n in r.incs:
                                inst.then_inc(sem, n)
                    getattr(block, key)(emit)


def c512(ap, c):
    return ap[:, 512 * c:512 * (c + 1)]




def emit_ln(ctx, block, zfull, A, P, Y, coef, out_dram):
    ctx.mark()
    zb = [ctx.alloc(f"zb{i}", [128, T], BF16) for i in range(8)]
    z2 = [ctx.alloc(f"z2{i}", [128, T], BF16) for i in range(2)]
    onec = ctx.alloc("ln_onec", [128, 1], BF16)
    oner = ctx.alloc("ln_oner", [1, 128], BF16)
    mrow = ctx.alloc("mrow", [1, T], F32)
    vrow = ctx.alloc("vrow", [1, T], F32)
    msq = ctx.alloc("msq", [1, T], F32)
    rrow = ctx.alloc("rrow", [1, T], F32)
    rrow_b = ctx.alloc("rrow_b", [1, T], BF16)
    mrs_b = ctx.alloc("mrs_b", [1, T], BF16)

    @block.vector
    def _(v):
        ctx.stamp(v.memset(onec, 1.0), "v")
        ctx.stamp(v.memset(oner, 1.0), "v")

    S1 = Y[0:1, :]
    S2 = Y[64:65, :]

    for blk in range(8):
        @block.vector
        def _(v, blk=blk):
            ctx.wait(v, "v")
            ctx.wait(v, "t")
            if A is not None:
                ctx.stamp(v.scalar_tensor_tensor(
                    zfull[blk], zfull[blk], float(coef), A[blk],
                    ALU.mult, ALU.add), "v")
                ctx.wait(v, "v")
            elif coef != 1.0:
                ctx.stamp(v.tensor_scalar_mul(zfull[blk], zfull[blk], float(coef)), "v")
                ctx.wait(v, "v")
            ctx.stamp(v.tensor_copy(zb[blk], zfull[blk]), "v")
            ctx.wait(v, "v")
            ctx.stamp(v.tensor_tensor(z2[blk % 2], zb[blk], zb[blk], ALU.mult), "v")

        @block.tensor
        def _(t, blk=blk):
            ctx.wait(t, "v")
            for c in range(4):
                ctx.stamp(t.matmul(c512(S1, c), onec, c512(zb[blk], c),
                                   start=(blk == 0), stop=(blk == 7)), "t")
                ctx.stamp(t.matmul(c512(S2, c), onec, c512(z2[blk % 2], c),
                                   start=(blk == 0), stop=(blk == 7)), "t")

    @block.vector
    def _(v):
        ctx.wait(v, "t")
        ctx.stamp(v.tensor_scalar_mul(mrow, S1, 1.0 / D), "v")
        ctx.stamp(v.tensor_scalar_mul(vrow, S2, 1.0 / D), "v")
        ctx.wait(v, "v")
        ctx.stamp(v.tensor_tensor(msq, mrow, mrow, ALU.mult), "v")
        ctx.wait(v, "v")
        ctx.stamp(v.tensor_tensor(vrow, vrow, msq, ALU.subtract), "v")

    @block.scalar
    def _(s):
        ctx.wait(s, "v")
        ctx.stamp(s.activation(rrow, vrow, AF.Ln, bias=1e-5), "s")
        ctx.wait(s, "s")
        ctx.stamp(s.activation(rrow, rrow, AF.Exp, scale=-0.5), "s")

    @block.vector
    def _(v):
        ctx.wait(v, "s")
        ctx.stamp(v.tensor_copy(rrow_b, rrow), "v")
        ctx.stamp(v.tensor_tensor(msq, mrow, rrow, ALU.mult), "v")
        ctx.wait(v, "v")
        ctx.stamp(v.tensor_copy(mrs_b, msq), "v")

    for c in range(4):
        @block.tensor
        def _(t, c=c):
            ctx.wait(t, "v")
            ctx.stamp(t.matmul(P[:, 0:512], oner, rrow_b[0:1, 512 * c:512 * (c + 1)],
                               start=True, stop=True), "t")
            ctx.stamp(t.matmul(P[:, 512:1024], oner, mrs_b[0:1, 512 * c:512 * (c + 1)],
                               start=True, stop=True), "t")
        for blk in range(8):
            @block.vector
            def _(v, c=c, blk=blk):
                ctx.wait(v, "t")
                sl = slice(512 * c, 512 * (c + 1))
                ctx.stamp(v.tensor_tensor(zfull[blk][:, sl], zfull[blk][:, sl],
                                          P[:, 0:512], ALU.mult), "v")
                ctx.wait(v, "v")
                ctx.stamp(v.tensor_tensor(zfull[blk][:, sl], zfull[blk][:, sl],
                                          P[:, 512:1024], ALU.subtract), "v")

    if out_dram is not None:
        @block.gpsimd
        def _(g):
            ctx.wait(g, "v")
            for blk in range(8):
                ctx.dma(g, "dmo", out_dram[128 * blk:128 * (blk + 1), :], zfull[blk])
    ctx.release()




def emit_l0_attn_core(ctx, block, io, A, P, Y):
    nc = ctx.nc
    ctx.mark()
    dtT = [ctx.alloc(f"dtT{i}", [128, T], BF16) for i in range(8)]
    dtxT = [ctx.alloc(f"dtxT{i}", [128, T], BF16) for i in range(8)]
    bT = ctx.alloc("bT", [32, T], BF16)
    cT = ctx.alloc("cT", [32, T], BF16)
    nacol = ctx.alloc("nacol", [128, 256], F32)
    dsk_sb = ctx.alloc("dsk_sb", [128, 8], F32)
    seld = ctx.alloc("seld", [128, 512], BF16)
    seldy = ctx.alloc("seldy", [128, 32], BF16)
    selb = ctx.alloc("selb", [32, 1024], BF16)

    ctx.mark()
    xTb = [ctx.alloc(f"xTb{i}", [128, T], BF16) for i in range(8)]
    wbufb = [ctx.alloc(f"dtwb{i}", [128, D], BF16) for i in range(8)]
    bcwb = ctx.alloc("bcwb", [128, 512], BF16)
    sp = [[ctx.alloc(f"sp{i}_{p}", [128, 512], F32) for i in range(2)]
          for p in range(2)]
    dtb_sb = ctx.alloc("dtb_sb", [128, 8], F32)
    bb_sb = ctx.alloc("bb_sb", [32, 1], F32)
    cb_sb = ctx.alloc("cb_sb", [32, 1], F32)
    acol = ctx.alloc("acol", [128, 256], F32)

    @block.sync
    def _(sync):
        for nm, dst in (("dtb", dtb_sb), ("Bb", bb_sb), ("Cb", cb_sb),
                        ("Dsk", dsk_sb), ("logAcol", acol),
                        ("c_seld", seld), ("c_seldy", seldy), ("c_selb", selb)):
            ctx.dma(sync, "dma", dst, io[nm])
    ld0 = ctx.cnt["dma"]

    @block.scalar
    def _(s):
        ctx.wait(s, "dma", ld0)
        ctx.stamp(s.activation(nacol, acol, AF.Exp), "s")

    @block.vector
    def _(v):
        ctx.wait(v, "s")
        ctx.stamp(v.tensor_scalar_mul(nacol, nacol, -1.0), "v")

    @block.gpsimd
    def _(g):
        for i in range(8):
            ctx.dma(g, "dw", xTb[i], io["xt"][128 * i:128 * (i + 1), :])
        ctx.dma(g, "dw", bcwb[:, 0:256].rearrange("p (k n) -> p k n", n=32),
                io["BwT"].rearrange("(k p) n -> p k n", p=128))
        ctx.dma(g, "dw", bcwb[:, 256:512].rearrange("p (k n) -> p k n", n=32),
                io["CwT"].rearrange("(k p) n -> p k n", p=128))
        for k in range(8):
            ctx.dma(g, "dw", wbufb[k], io["dtwT"][128 * k:128 * (k + 1), :])
    ldw = ctx.cnt["dw"]

    pviews = [c512(P, c) for c in range(4)] + [c512(Y, c) for c in range(4)]
    pfree = []
    dt_add = []
    rot = 0
    for ec in range(8):
        for tc in range(4):
            pv = pviews[rot % 8]

            @block.tensor
            def _(t, ec=ec, tc=tc, pv=pv, rot=rot):
                ctx.wait(t, "dw", ldw)
                ctx.wait(t, "v")
                if rot >= 8:
                    ctx.wait(t, "s", pfree[rot - 8])
                for k in range(8):
                    ctx.stamp(t.matmul(pv, wbufb[k][:, 128 * ec:128 * (ec + 1)],
                                       c512(xTb[k], tc),
                                       start=(k == 0), stop=(k == 7)), "t")

            spp = sp[rot % 2]

            @block.scalar
            def _(s, ec=ec, pv=pv, spp=spp, dtv=(dt_add[rot - 2] if rot >= 2 else 0)):
                ctx.wait(s, "t")
                if dtv:
                    ctx.wait(s, "v", dtv)
                bias = dtb_sb[:, ec:ec + 1]
                ctx.stamp(s.activation(spp[0], pv, AF.Relu, bias=bias), "s")
                pfree.append(ctx.stamp(s.activation(spp[1], pv, AF.Abs, bias=bias), "s"))
                ctx.wait(s, "s")
                ctx.stamp(s.activation(spp[1], spp[1], AF.Exp, scale=-1.0), "s")
                ctx.wait(s, "s")
                ctx.stamp(s.activation(spp[1], spp[1], AF.Ln, bias=1.0), "s")

            @block.vector
            def _(v, ec=ec, tc=tc, spp=spp):
                ctx.wait(v, "s")
                ctx.stamp(v.tensor_tensor(c512(dtT[ec], tc), spp[0], spp[1],
                                          ALU.add), "v")
            dt_add.append(ctx.cnt["v"])
            rot += 1

    for wcol0, bias_sb, dst in ((0, bb_sb, bT), (256, cb_sb, cT)):
        for tc in range(4):
            pv = pviews[rot % 8][0:32, :]

            @block.tensor
            def _(t, wcol0=wcol0, tc=tc, pv=pv, rot=rot):
                if rot >= 8:
                    ctx.wait(t, "s", pfree[rot - 8])
                for k in range(8):
                    ctx.stamp(t.matmul(pv, bcwb[:, wcol0 + 32 * k:wcol0 + 32 * (k + 1)],
                                       c512(xTb[k], tc),
                                       start=(k == 0), stop=(k == 7)), "t")

            @block.scalar
            def _(s, tc=tc, pv=pv, bias_sb=bias_sb, dst=dst):
                ctx.wait(s, "t")
                pfree.append(ctx.stamp(
                    s.activation(c512(dst, tc), pv, AF.Identity, bias=bias_sb), "s"))
            rot += 1

    for i in range(8):
        @block.vector
        def _(v, i=i):
            ctx.wait(v, "s")
            ctx.wait(v, "v")
            ctx.stamp(v.tensor_tensor(dtxT[i], dtT[i], xTb[i], ALU.mult), "v")

    ctx.release()
    ctx.mark()
    breps = [ctx.alloc(f"brep{i}", [128, T], BF16) for i in range(8)]
    creps = [ctx.alloc(f"crep{i}", [128, T], BF16) for i in range(8)]
    abar = [ctx.alloc(f"abar{i}", [128, T], BF16) for i in range(2)]
    ubuf = ctx.alloc("ubuf", [128, T], BF16)
    hbuf = ctx.alloc("hbuf", [128, T], BF16)
    ytbuf2 = [ctx.alloc(f"ytbuf{i}", [128, T], BF16) for i in range(2)]
    xflush = ctx.alloc("xflush", [128, T], F32)

    @block.tensor
    def _(t):
        ctx.wait(t, "s")
        ctx.wait(t, "v")
    for nb in range(8):
        for src_, dst in ((bT, breps), (cT, creps)):
            @block.tensor
            def _(t, nb=nb, src_=src_):
                ctx.wait(t, "v")
                for c in range(4):
                    ctx.stamp(t.matmul(c512(P, c), selb[:, 128 * nb:128 * (nb + 1)],
                                       c512(src_, c), start=True, stop=True), "t")

            @block.vector
            def _(v, nb=nb, dst=dst):
                ctx.wait(v, "t")
                for c in range(4):
                    ctx.stamp(v.tensor_copy(c512(dst[nb], c), c512(P, c)), "v")

    exp_done = {0: ctx.cnt["s"], 1: ctx.cnt["s"]}
    u_done = {0: ctx.cnt["v"], 1: ctx.cnt["v"]}
    scan_mark = [ctx.cnt["v"]]
    pending = [None]
    flush_mark = [ctx.cnt["v"]]

    ymm_guard = {0: 0, 1: 0}

    def emit_pending_ymm():
        if pending[0] is None:
            return
        g4p, nbp, ypar, vmark, fm = pending[0]
        pending[0] = None

        @block.tensor
        def _(t, g4=g4p, nb=nbp, ypar=ypar, vmark=vmark, fm=fm):
            ctx.wait(t, "g", vmark)
            if nb == 0:
                ctx.wait(t, "v", fm)
            tp = (0, 96) if g4 == 3 else None
            for c in range(4):
                ctx.stamp(t.matmul(
                    Y[32 * g4:32 * (g4 + 1), 512 * c:512 * (c + 1)],
                    seldy, c512(ytbuf2[ypar], c),
                    start=(nb == 0), stop=(nb == 7), tile_position=tp), "t")
        ymm_guard[ypar] = ctx.cnt["t"]
    tseq = 0
    for gb in range(8):
        for g4 in range(4):
            for nb in range(8):
                par = (g4 * 8 + nb) % 2
                ypar = tseq % 2
                tile_idx = 8 * (4 * gb + g4) + nb
                for th in range(2):
                    PA = P[:, 0:1024] if th == 0 else P[:, 1024:2048]
                    PB = P[:, 1024:2048] if th == 0 else P[:, 0:1024]

                    @block.tensor
                    def _(t, gb=gb, g4=g4, th=th, PA=PA, PB=PB,
                          ed=exp_done[th], ud=u_done[th],
                          ed2=exp_done[1 - th], ud2=u_done[1 - th]):
                        ctx.wait(t, "s", ed)
                        ctx.wait(t, "v", ud2)
                        for c in range(2):
                            ctx.stamp(t.matmul(
                                c512(PA, c), seld[:, 128 * g4:128 * (g4 + 1)],
                                dtT[gb][:, 1024 * th + 512 * c:1024 * th + 512 * (c + 1)],
                                start=True, stop=True), "t")
                        ctx.wait(t, "s", ed2)
                        ctx.wait(t, "v", ud)
                        for c in range(2):
                            ctx.stamp(t.matmul(
                                c512(PB, c), seld[:, 128 * g4:128 * (g4 + 1)],
                                dtxT[gb][:, 1024 * th + 512 * c:1024 * th + 512 * (c + 1)],
                                start=True, stop=True), "t")
                    tmark = ctx.cnt["t"]

                    @block.scalar
                    def _(s, th=th, par=par, tile_idx=tile_idx, tmark=tmark, PA=PA):
                        ctx.wait(s, "t", tmark - 2)
                        for c in range(2):
                            ctx.stamp(s.activation(
                                abar[par][:, 1024 * th + 512 * c:1024 * th + 512 * (c + 1)],
                                c512(PA, c), AF.Exp,
                                scale=nacol[:, tile_idx:tile_idx + 1]), "s")
                    exp_done[th] = ctx.cnt["s"]

                    @block.vector
                    def _(v, th=th, nb=nb, tmark=tmark, PB=PB, scm=scan_mark[0]):
                        ctx.wait(v, "t", tmark)
                        if th == 0:
                            ctx.wait(v, "v", scm)
                        ctx.stamp(v.tensor_tensor(
                            ubuf[:, 1024 * th:1024 * (th + 1)], PB,
                            breps[nb][:, 1024 * th:1024 * (th + 1)],
                            ALU.mult), "v")
                    u_done[th] = ctx.cnt["v"]

                emit_pending_ymm()

                @block.vector
                def _(v, par=par, nb=nb, gg=ctx.cnt["g"]):
                    ctx.wait(v, "s", ctx.cnt["s"])
                    ctx.wait(v, "v")
                    ctx.wait(v, "g", gg)
                    ctx.stamp(v.tensor_tensor_scan(hbuf, abar[par], ubuf,
                                                   0.0, ALU.mult, ALU.add), "v")
                scan_mark[0] = ctx.cnt["v"]
                smark = ctx.cnt["v"]

                @block.gpsimd
                def _(g, nb=nb, ypar=ypar, yg=ymm_guard[tseq % 2], sm=smark):
                    ctx.wait(g, "v", sm)
                    ctx.wait(g, "t", yg)
                    ctx.stamp(g.tensor_tensor(ytbuf2[ypar], hbuf, creps[nb],
                                              ALU.mult), "g")
                vmark = ctx.cnt["g"]
                pending[0] = (g4, nb, ypar, vmark, flush_mark[0])
                tseq += 1
        emit_pending_ymm()

        @block.sync
        def _(sync, gb=gb):
            if gb > 0:
                ctx.wait(sync, "v")
            ctx.dma(sync, "dma", xflush, io["xt"][128 * gb:128 * (gb + 1), :])

        @block.vector
        def _(v, gb=gb):
            ctx.wait(v, "dma")
            ctx.wait(v, "t")
            for c in range(4):
                ctx.stamp(v.scalar_tensor_tensor(
                    c512(A[gb], c), c512(xflush, c), dsk_sb[:, gb:gb + 1],
                    c512(Y, c), ALU.mult, ALU.add), "v")
        flush_v = ctx.cnt["v"]
        flush_mark[0] = flush_v

        @block.tensor
        def _(t, fv=flush_v):
            ctx.wait(t, "v", fv)

    ctx.release()
    ctx.release()




def emit_l0_ffn(ctx, block, io, A, zfull, P, Y):
    K_TOP = 1536
    ctx.mark()
    xTb = [ctx.alloc(f"fx{i}", [128, T], BF16) for i in range(8)]
    wblkb = [ctx.alloc(f"fw{i}", [128, D], BF16) for i in range(8)]
    wr8 = ctx.alloc("wr8", [128, 8], F32)
    idnb = ctx.alloc("idnb", [128, 128], BF16)
    ohc = ctx.alloc("ohc", [16, T], BF16)
    onescb = ctx.alloc("onescb", [128, 1], BF16)
    onesrf = ctx.alloc("onesrf", [1, 128], F32)
    snat = ctx.alloc("snat", [128, 16], F32)
    ssq = ctx.alloc("ssq", [128, 8], F32)
    mT = ctx.alloc("mT", [128, 16], BF16)
    mt16 = ctx.alloc("mt16", [16, 128], BF16)
    geb = ctx.alloc("geb", [128, 16], BF16)
    gered = ctx.alloc("gered", [128, 1], BF16)
    wncol = ctx.alloc("wncol", [128, 1], BF16)
    lo = ctx.alloc("lo", [1, 1], F32)
    hi = ctx.alloc("hi", [1, 1], F32)
    mid = ctx.alloc("mid", [1, 1], F32)
    tnum = ctx.alloc("tnum", [1, 1], F32)
    c01 = ctx.alloc("c01", [1, 1], F32)
    cntf = ctx.alloc("cntf", [128, 1], F32)
    sq = ctx.alloc("fsq", [128, 512], F32)
    cub = ctx.alloc("fcub", [128, 512], F32)
    tnh = ctx.alloc("ftnh", [128, 512], F32)
    gmx = ctx.alloc("gmx", [128, 512], F32)

    @block.sync
    def _(sync):
        ctx.dma(sync, "dma", wr8, io["wr8"])
        ctx.dma(sync, "dma", idnb, io["c_idn"])
        ctx.dma(sync, "dma", ohc, io["c_ohc"])
    ldc = ctx.cnt["dma"]

    @block.gpsimd
    def _(g):
        for i in range(8):
            ctx.dma(g, "dw", xTb[i], io["xt"][128 * i:128 * (i + 1), :])
        for k in range(8):
            ctx.dma(g, "dw", wblkb[k], io["wblk"][128 * k:128 * (k + 1), :])
    ldw = ctx.cnt["dw"]

    @block.vector
    def _(v):
        ctx.stamp(v.memset(onescb, 1.0), "v")
        ctx.stamp(v.memset(onesrf, 1.0), "v")

    zready = ctx.cnt["v"]

    Pv16 = P[:, 0:16]

    @block.tensor
    def _(t):
        ctx.wait(t, "dma", ldc)
        ctx.wait(t, "v", zready)
        for c in range(16):
            for k in range(8):
                ctx.stamp(t.matmul(Pv16[:, c:c + 1],
                                   zfull[k][:, 128 * c:128 * (c + 1)],
                                   wr8[:, k:k + 1],
                                   start=(k == 0), stop=(k == 7)), "t")

    @block.vector
    def _(v):
        ctx.wait(v, "t")
        ctx.stamp(v.tensor_copy(snat, Pv16), "v")
        ctx.stamp(v.tensor_tensor(ssq, wr8, wr8, ALU.mult), "v")
        ctx.wait(v, "v")
        ctx.stamp(v.reduce_sum(cntf, ssq, axis=AX.X), "v")
        ctx.wait(v, "v")
        ctx.stamp(v.tensor_copy(wncol, cntf), "v")

    @block.tensor
    def _(t):
        ctx.wait(t, "v")
        ctx.stamp(t.matmul(Y[0:1, 0:1], wncol, onescb, start=True, stop=True), "t")

    @block.scalar
    def _(s):
        ctx.wait(s, "t")
        ctx.stamp(s.activation(hi, Y[0:1, 0:1], AF.Sqrt, scale=1024.0), "s")

    @block.vector
    def _(v):
        ctx.wait(v, "s")
        ctx.stamp(v.tensor_scalar_add(hi, hi, 1.0), "v")
        ctx.wait(v, "v")
        ctx.stamp(v.tensor_scalar_mul(lo, hi, -1.0), "v")


    for it in range(34):
        @block.tensor
        def _(t, it=it):
            ctx.wait(t, "v")
        @block.vector
        def _(v, it=it):
            ctx.wait(v, "v")
            ctx.wait(v, "t")
            ctx.stamp(v.tensor_tensor(tnum, lo, hi, ALU.add), "v")
            ctx.wait(v, "v")
            ctx.stamp(v.tensor_scalar_mul(mid, tnum, 0.5), "v")

        @block.tensor
        def _(t, it=it):
            ctx.wait(t, "v")
            ctx.stamp(t.matmul(Y[0:128, 16:17], onesrf, mid, start=True, stop=True), "t")

        @block.vector
        def _(v, it=it):
            ctx.wait(v, "t")
            ctx.stamp(v.tensor_scalar(geb, snat, Y[0:128, 16:17], None, ALU.is_ge), "v")
            ctx.wait(v, "v")
            ctx.stamp(v.reduce_sum(cntf, geb, axis=AX.X), "v")
            ctx.wait(v, "v")
            ctx.stamp(v.tensor_copy(gered, cntf), "v")

        @block.tensor
        def _(t, it=it):
            ctx.wait(t, "v")
            ctx.stamp(t.matmul(Y[0:1, 17:18], gered, onescb, start=True, stop=True), "t")

        @block.vector
        def _(v, it=it):
            ctx.wait(v, "t")
            ctx.stamp(v.tensor_scalar(c01, Y[0:1, 17:18], float(K_TOP) - 0.5, None,
                                      ALU.is_ge), "v")
            ctx.wait(v, "v")
            ctx.stamp(v.tensor_tensor(tnum, mid, lo, ALU.subtract), "v")
            ctx.wait(v, "v")
            ctx.stamp(v.tensor_tensor(tnum, c01, tnum, ALU.mult), "v")
            ctx.wait(v, "v")
            ctx.stamp(v.tensor_tensor(lo, lo, tnum, ALU.add), "v")
            ctx.wait(v, "v")
            ctx.stamp(v.tensor_tensor(tnum, mid, hi, ALU.subtract), "v")
            ctx.wait(v, "v")
            ctx.stamp(v.tensor_tensor(c01, c01, tnum, ALU.mult), "v")
            ctx.wait(v, "v")
            ctx.stamp(v.tensor_tensor(tnum, tnum, c01, ALU.subtract), "v")
            ctx.wait(v, "v")
            ctx.stamp(v.tensor_tensor(hi, hi, tnum, ALU.add), "v")

    @block.tensor
    def _(t):
        ctx.wait(t, "v")
        ctx.stamp(t.matmul(Y[0:128, 16:17], onesrf, lo, start=True, stop=True), "t")

    @block.vector
    def _(v):
        ctx.wait(v, "t")
        ctx.stamp(v.tensor_scalar(mT, snat, Y[0:128, 16:17], None, ALU.is_ge), "v")

    @block.tensor
    def _(t):
        ctx.wait(t, "v")
        ctx.stamp(t.transpose(P[0:16, 0:64].bitcast(BF16), mT, idnb), "t")

    @block.vector
    def _(v):
        ctx.wait(v, "t")
        ctx.stamp(v.tensor_copy(mt16, P[0:16, 0:64].bitcast(BF16)), "v")
    mready_v = ctx.cnt["v"]

    @block.tensor
    def _(t):
        ctx.wait(t, "v", mready_v)
        for r in range(16):
            ctx.stamp(t.matmul(Y[:, 128 * r:128 * (r + 1)],
                               ohc[:, 128 * r:128 * (r + 1)], mt16,
                               start=True, stop=True), "t")
    mbc_t = ctx.cnt["t"]

    pviews = [c512(P, c) for c in range(4)]
    pfree = []
    rot = 0
    for ec in range(8):
        for tc in range(4):
            pv = pviews[rot % 4]

            @block.tensor
            def _(t, ec=ec, tc=tc, pv=pv, rot=rot):
                ctx.wait(t, "dw", ldw)
                if rot >= 4:
                    ctx.wait(t, "v", pfree[rot - 4])
                for k in range(8):
                    ctx.stamp(t.matmul(pv, wblkb[k][:, 128 * ec:128 * (ec + 1)],
                                       c512(xTb[k], tc),
                                       start=(k == 0), stop=(k == 7)), "t")
            mm_t = ctx.cnt["t"]

            @block.scalar
            def _(s, pv=pv, mm_t=mm_t):
                ctx.wait(s, "t", mm_t)
                ctx.stamp(s.activation(sq, pv, AF.Square), "s")
            sq_s = ctx.cnt["s"]

            @block.vector
            def _(v, ec=ec, tc=tc, pv=pv, sq_s=sq_s):
                ctx.wait(v, "s", sq_s)
                ctx.stamp(v.tensor_tensor(cub, sq, pv, ALU.mult), "v")
                ctx.wait(v, "v")
                ctx.stamp(v.scalar_tensor_tensor(cub, cub, 0.044715, pv,
                                                 ALU.mult, ALU.add), "v")

            @block.scalar
            def _(s, pv=pv):
                ctx.wait(s, "v")
                ctx.stamp(s.activation(tnh, cub, AF.Tanh, scale=0.7978845608028654), "s")

            @block.vector
            def _(v, ec=ec, tc=tc, pv=pv):
                ctx.wait(v, "s")
                ctx.stamp(v.tensor_scalar_add(tnh, tnh, 1.0), "v")
                ctx.wait(v, "v")
                ctx.stamp(v.tensor_tensor(gmx, tnh, pv, ALU.mult), "v")
                ctx.wait(v, "v")
                ctx.stamp(v.scalar_tensor_tensor(
                    gmx, gmx, 0.5, zfull[ec][:, 512 * tc:512 * (tc + 1)],
                    ALU.mult, ALU.subtract), "v")
                ctx.wait(v, "v")
                pfree.append(ctx.cnt["v"] + 1)
                ctx.stamp(v.tensor_tensor(gmx, gmx, c512(Y, tc), ALU.mult), "v")
                ctx.wait(v, "v")
                ctx.stamp(v.tensor_tensor(c512(A[ec], tc),
                                          zfull[ec][:, 512 * tc:512 * (tc + 1)],
                                          gmx, ALU.add), "v")
            rot += 1

    ctx.release()




L_DELTAS = (-256, -128, 0, 128, 256, 384)


def emit_l1_attn(ctx, block, io, A, P, Y):
    nc = ctx.nc
    ctx.mark()
    xnT = [ctx.alloc(f"axn{i}", [128, T], BF16) for i in range(8)]
    qT = [ctx.alloc(f"aq{i}", [128, T], BF16) for i in range(8)]
    kT = [ctx.alloc(f"ak{i}", [128, T], BF16) for i in range(2)]
    vnat = [ctx.alloc(f"av{i}", [128, 260], BF16) for i in range(16)]
    cosb = ctx.alloc("cosb", [128, T], BF16)
    sinb = ctx.alloc("sinb", [128, T], BF16)
    gmask = ctx.alloc("gmask", [128, 4 * 512], BF16)
    lmask = ctx.alloc("lmask", [128, 6 * 512], BF16)
    hselA = ctx.alloc("hselA", [128, 128], BF16)
    hselB = ctx.alloc("hselB", [16, 1024], BF16)
    n1w = ctx.alloc("n1w", [128, 8], F32)
    n2w = ctx.alloc("n2w", [128, 8], F32)
    idnb2 = ctx.alloc("idnb2", [128, 128], BF16)
    srow = ctx.alloc("a_srow", [1, T], F32)
    srowb = ctx.alloc("a_srowb", [1, T], BF16)
    rstdq = ctx.alloc("rstdq", [16, T], BF16)
    rstdk = ctx.alloc("rstdk", [16, T], BF16)
    ptile = [ctx.alloc(f"apt{i}", [128, 512], BF16) for i in range(4)]
    sqb = ctx.alloc("asqb", [128, T], BF16)
    swp = ctx.alias("aswp", [128, T], BF16, "asqb")
    rt1 = ctx.alloc("art1", [128, T], BF16)
    kboth = ctx.alloc("akb", [128, T], BF16)
    rzall = ctx.alloc("arz", [16, T], BF16)
    xsc = ctx.alloc("axsc", [128, T], F32)
    rstdqf = xsc[0:16, :]
    zcoll = xsc[32:33, :].bitcast(BF16)[:, 0:T]
    xsc2 = ctx.alias("axsc2", [128, T], F32, "art1")
    onescb = ctx.alloc("a_onec", [128, 1], BF16)
    onerb = ctx.alloc("a_oner", [1, 128], BF16)

    zdram = nc.dram_tensor(f"zd{ctx.uid}", [16, T], BF16, kind="Internal").ap()

    ctx.mark()
    wb = [ctx.alloc(f"awb{i}", [128, D], BF16) for i in range(8)]
    wb2 = [ctx.alloc(f"aw2{i}", [128, 256], BF16) for i in range(8)]
    kupb = [ctx.alloc(f"aku{i}", [128, 256], BF16) for i in range(2)]
    vupb = [ctx.alloc(f"avu{i}", [128, 256], BF16) for i in range(2)]
    latT = [ctx.alias(f"alt{i}", [128, T], BF16, f"awb{4 * i}") for i in range(2)]
    OtT = [ctx.alias(f"aot{i}", [128, T], BF16, f"axn{i}") for i in range(8)]

    SHUF = []
    for i in range(16):
        SHUF += [2 * i + 1, 2 * i]

    @block.sync
    def _(sync):
        for nm, dst in (("c_hselA", hselA), ("c_hselB", hselB),
                        ("c_gmask", gmask), ("c_lmask", lmask),
                        ("n1w", n1w), ("n2w", n2w), ("c_idn", idnb2)):
            ctx.dma(sync, "dma", dst, io[nm])

    @block.vector
    def _(v):
        ctx.stamp(v.memset(onescb, 1.0), "v")
        ctx.stamp(v.memset(onerb, 1.0), "v")
        for i in range(16):
            for h4 in range(4):
                ctx.stamp(v.memset(vnat[i][:, 65 * h4 + 64:65 * h4 + 65], 1.0), "v")

    def rms_norm(nw, add_A):
        xb2 = [xsc, xsc2]
        vm = {}
        sq2 = [sqb[:, 0:1024], sqb[:, 1024:2048]]
        for blk in range(8):
            @block.sync
            def _(sync, blk=blk, w=vm.get(blk - 2, 0)):
                if w:
                    ctx.wait(sync, "v", w)
                ctx.dma(sync, "dma", xb2[blk % 2], io["xt"][128 * blk:128 * (blk + 1), :])
            dm = ctx.cnt["dma"]

            @block.vector
            def _(v, blk=blk, dm=dm, tm=(vm.get(blk - 2, 0))):
                ctx.wait(v, "dma", dm)
                ctx.wait(v, "t")
                if add_A:
                    ctx.stamp(v.tensor_tensor(xb2[blk % 2], xb2[blk % 2],
                                              A[blk], ALU.add), "v")
                    ctx.wait(v, "v")
                ctx.stamp(v.tensor_copy(sqb, xb2[blk % 2]), "v")
                ctx.wait(v, "v")
                ctx.stamp(v.tensor_tensor(sqb, sqb, sqb, ALU.mult), "v")
            vm[blk] = ctx.cnt["v"]

            @block.tensor
            def _(t, blk=blk, w=vm[blk]):
                ctx.wait(t, "v", w)
                for c in range(4):
                    ctx.stamp(t.matmul(c512(Y[0:1, :], c), onescb, c512(sqb, c),
                                       start=(blk == 0), stop=(blk == 7)), "t")

        @block.vector
        def _(v):
            ctx.wait(v, "t")
            ctx.stamp(v.tensor_scalar_mul(srow, Y[0:1, :], 1.0 / D), "v")

        @block.scalar
        def _(s):
            ctx.wait(s, "v")
            ctx.stamp(s.activation(srow, srow, AF.Ln, bias=1e-6), "s")
            ctx.wait(s, "s")
            ctx.stamp(s.activation(srow, srow, AF.Exp, scale=-0.5), "s")

        @block.vector
        def _(v):
            ctx.wait(v, "s")
            ctx.stamp(v.tensor_copy(srowb, srow), "v")
        rdy = ctx.cnt["v"]

        @block.tensor
        def _(t, rdy=rdy):
            ctx.wait(t, "v", rdy)
            ctx.wait(t, "v")
            ctx.wait(t, "s")
            for c in range(4):
                ctx.stamp(t.matmul(c512(P, c), onerb, c512(srowb, c),
                                   start=True, stop=True), "t")
        bmark = ctx.cnt["t"]

        vm2 = {-2: vm[6], -1: vm[7]}
        for blk in range(8):
            @block.sync
            def _(sync, blk=blk, w=vm2.get(blk - 2, 0)):
                if w:
                    ctx.wait(sync, "v", w)
                ctx.dma(sync, "dma", xb2[blk % 2], io["xt"][128 * blk:128 * (blk + 1), :])
            dm = ctx.cnt["dma"]

            @block.vector
            def _(v, blk=blk, nw=nw, dm=dm, bmark=bmark):
                ctx.wait(v, "dma", dm)
                ctx.wait(v, "t", bmark)
                if add_A:
                    ctx.stamp(v.tensor_tensor(xb2[blk % 2], xb2[blk % 2],
                                              A[blk], ALU.add), "v")
                    ctx.wait(v, "v")
                for c in range(4):
                    ctx.stamp(v.scalar_tensor_tensor(
                        c512(xnT[blk], c), c512(xb2[blk % 2], c), nw[:, blk:blk + 1],
                        c512(P, c), ALU.mult, ALU.mult), "v")
            vm2[blk] = ctx.cnt["v"]
        @block.tensor
        def _(t):
            ctx.wait(t, "v")

    def project(dstT, w_io, wbuf, wwidth, kchunks, src_tiles):
        @block.gpsimd
        def _(g):
            ctx.wait(g, "t")
            for k in range(kchunks):
                ctx.dma(g, "dw", wbuf[k][:, 0:wwidth],
                        w_io[128 * k:128 * (k + 1), :])
        ldw = ctx.cnt["dw"]
        rot = 0
        pfree = []
        ncol = wwidth // 128
        for m in range(ncol):
            for tc in range(4):
                pv = c512(P, rot % 4)

                @block.tensor
                def _(t, m=m, tc=tc, pv=pv, rot=rot, ldw=ldw):
                    ctx.wait(t, "dw", ldw)
                    ctx.wait(t, "v")
                    if rot >= 4:
                        ctx.wait(t, "v", pfree[rot - 4])
                    for k in range(kchunks):
                        ctx.stamp(t.matmul(pv, wbuf[k][:, 128 * m:128 * (m + 1)],
                                           c512(src_tiles[k], tc),
                                           start=(k == 0), stop=(k == kchunks - 1)), "t")

                @block.vector
                def _(v, m=m, tc=tc, pv=pv):
                    ctx.wait(v, "t")
                    pfree.append(ctx.stamp(
                        v.tensor_copy(c512(dstT[m], tc), pv), "v"))
                rot += 1

    def build_vnat(wbuf, kchunks, src_tiles):
        vp = Y[0:128, 512:1024]
        for tt in range(16):
            for h4 in range(4):
                @block.tensor
                def _(t, tt=tt, h4=h4):
                    ctx.wait(t, "v")
                    for k in range(kchunks):
                        ctx.stamp(t.matmul(vp[:, 0:64],
                                           src_tiles[k][:, 128 * tt:128 * (tt + 1)],
                                           wbuf[k][:, 64 * h4:64 * (h4 + 1)],
                                           start=(k == 0), stop=(k == kchunks - 1)), "t")

                @block.vector
                def _(v, tt=tt, h4=h4):
                    ctx.wait(v, "t")
                    ctx.stamp(v.tensor_copy(vnat[tt][:, 65 * h4:65 * h4 + 64],
                                            vp[:, 0:64]), "v")

    def rope_qknorm(tiles, js, rstd_dst, nrows):
        for idx, tl in enumerate(tiles):
            @block.vector
            def _(v, tl=tl):
                ctx.wait(v, "v")
                ctx.wait(v, "t")
                ctx.stamp(v.tensor_tensor(sqb, tl, tl, ALU.mult), "v")

            @block.tensor
            def _(t, idx=idx, j=js[idx], last=(idx == len(tiles) - 1)):
                ctx.wait(t, "v")
                for c in range(4):
                    ctx.stamp(t.matmul(c512(Y[0:16, :], c),
                                       hselA[:, 16 * j:16 * (j + 1)],
                                       c512(sqb, c),
                                       start=(idx == 0), stop=last), "t")

        @block.vector
        def _(v):
            ctx.wait(v, "t")
            ctx.stamp(v.tensor_scalar_mul(rstdqf[0:nrows, :], Y[0:nrows, :],
                                          1.0 / 64.0), "v")

        @block.scalar
        def _(s):
            ctx.wait(s, "v")
            ctx.stamp(s.activation(rstdqf[0:nrows, :], rstdqf[0:nrows, :],
                                   AF.Ln, bias=1e-6), "s")
            ctx.wait(s, "s")
            ctx.stamp(s.activation(rstdqf[0:nrows, :], rstdqf[0:nrows, :],
                                   AF.Exp, scale=-0.5), "s")

        @block.vector
        def _(v):
            ctx.wait(v, "s")
            ctx.stamp(v.tensor_copy(rstd_dst[0:nrows, :], rstdqf[0:nrows, :]), "v")
        rdy = ctx.cnt["v"]

        for idx, tl in enumerate(tiles):
            j = js[idx]

            @block.vector
            def _(v, tl=tl, rdy=rdy):
                ctx.wait(v, "v", rdy)
                ctx.wait(v, "v")
                ctx.stamp(v.stream_shuffle(swp, tl, SHUF), "v")
                ctx.wait(v, "v")
                ctx.stamp(v.tensor_tensor(rt1, tl, cosb, ALU.mult), "v")
                ctx.wait(v, "v")
                ctx.stamp(v.tensor_tensor(swp, swp, sinb, ALU.mult), "v")
                ctx.wait(v, "v")
                ctx.stamp(v.tensor_tensor(rt1, rt1, swp, ALU.add), "v")
            rr = ctx.cnt["v"]

            @block.tensor
            def _(t, j=j, rr=rr):
                ctx.wait(t, "v", rr)
                for c in range(4):
                    ctx.stamp(t.matmul(c512(P, c),
                                       hselB[0:nrows, 128 * j:128 * (j + 1)],
                                       c512(rstd_dst[0:nrows, :], c),
                                       start=True, stop=True), "t")

            @block.vector
            def _(v, tl=tl):
                ctx.wait(v, "t")
                for c in range(4):
                    ctx.stamp(v.tensor_tensor(c512(tl, c), c512(rt1, c),
                                              c512(P, c), ALU.mult), "v")

    def attn_scores_wo(kind, wo_io, out_mode):
        causal = (kind == "g")
        OTp2 = [Y[0:65, 0:512], Y[0:65, 512:1024]]
        ev_marks = {0: (0, 0), 1: (0, 0)}

        @block.tensor
        def _(t):
            ctx.wait(t, "v")
            ctx.wait(t, "s")
        rot = [0]
        pm_exp = {}
        pm_mask = {}
        pm_ot = {}
        cur_kvh = [-1]
        kb_mark = [0]
        for h in range(16):
            kvh = h // 4
            if kvh != cur_kvh[0]:
                cur_kvh[0] = kvh

                @block.vector
                def _(v, kvh=kvh):
                    ctx.wait(v, "v")
                    ctx.wait(v, "t")
                    src_sl = kT[kvh // 2][64 * (kvh % 2):64 * (kvh % 2) + 64, :]
                    ctx.stamp(v.tensor_copy(kboth[0:64, :], src_sl), "v")
                    ctx.stamp(v.tensor_copy(kboth[64:128, :], src_sl), "v")
                kb_mark[0] = ctx.cnt["v"]
            qtile = qT[h // 2]
            qrow0 = 64 * (h % 2)
            for cq in range(4):
                gpar = (4 * h + cq) % 2
                OTp = OTp2[gpar]
                iks = list(range(0, 4 * cq + 4)) if causal else \
                    list(range(max(0, 4 * cq - 2), 4 * cq + 4))
                for n_i, ik in enumerate(iks):
                    rr = rot[0]
                    r = rr % 4
                    pr = rr % 4
                    pv = c512(P, r)
                    dlt = 128 * ik - 512 * cq
                    if causal:
                        masked = dlt >= 0
                        mvar = dlt // 128 if masked else None
                        msrc = gmask
                    else:
                        masked = True
                        mvar = L_DELTAS.index(dlt)
                        msrc = lmask

                    @block.tensor
                    def _(t, pv=pv, ik=ik, cq=cq, qtile=qtile, qrow0=qrow0, rr=rr,
                          kbm=kb_mark[0], masked=masked, mvar=mvar, msrc=msrc):
                        ctx.wait(t, "v", kbm)
                        if rr - 4 in pm_exp:
                            ctx.wait(t, "s", pm_exp[rr - 4])
                        ctx.stamp(t.matmul(
                            pv, kboth[qrow0:qrow0 + 64, 128 * ik:128 * (ik + 1)],
                            qtile[qrow0:qrow0 + 64, 512 * cq:512 * (cq + 1)],
                            start=True, stop=not masked), "t")
                        if masked:
                            ctx.stamp(t.matmul(
                                pv, idnb2,
                                msrc[:, 512 * mvar:512 * (mvar + 1)],
                                start=False, stop=True), "t")
                    mmt = ctx.cnt["t"]

                    @block.scalar
                    def _(s, pv=pv, pr=pr, mmt=mmt, rr=rr):
                        ctx.wait(s, "t", mmt)
                        if rr - 4 in pm_ot:
                            ctx.wait(s, "t", pm_ot[rr - 4])
                        ctx.stamp(s.activation(ptile[pr], pv, AF.Exp, scale=0.125), "s")
                    pm_exp[rr] = ctx.cnt["s"]

                    @block.tensor
                    def _(t, pr=pr, ik=ik, kvh=kvh, n_i=n_i, OTp=OTp,
                          last=(n_i == len(iks) - 1), rr=rr,
                          evm=(ev_marks[gpar][1] if n_i == 0 else 0)):
                        if n_i == 0 and evm:
                            ctx.wait(t, "v", evm)
                        ctx.wait(t, "s", pm_exp[rr])
                        ctx.stamp(t.matmul(
                            OTp, vnat[ik][:, 65 * kvh:65 * (kvh + 1)], ptile[pr],
                            start=(n_i == 0), stop=last), "t")
                    pm_ot[rr] = ctx.cnt["t"]
                    rot[0] += 1

                otm = ctx.cnt["t"]

                @block.vector
                def _(v, h=h, cq=cq, otm=otm, OTp=OTp, zdm=ctx.cnt["dmo"]):
                    ctx.wait(v, "t", otm)
                    ctx.wait(v, "dmo", zdm)
                    ctx.stamp(v.tensor_copy(
                        OtT[h // 2][64 * (h % 2):64 * (h % 2) + 64,
                                    512 * cq:512 * (cq + 1)],
                        OTp[0:64, :]), "v")
                    ctx.stamp(v.tensor_copy(zcoll[0:1, 512 * cq:512 * (cq + 1)],
                                            OTp[64:65, :]), "v")
                ev_marks[gpar] = (ctx.cnt["s"], ctx.cnt["v"])

            @block.gpsimd
            def _(g, h=h):
                ctx.wait(g, "v")
                ctx.dma(g, "dmo", zdram[h:h + 1, :], zcoll)

        @block.sync
        def _(sync):
            ctx.wait(sync, "dmo", ctx.cnt["dmo"])
            ctx.dma(sync, "dma", rzall, zdram)

        @block.vector
        def _(v):
            ctx.wait(v, "dma")
            ctx.stamp(v.reciprocal(rstdqf, rzall), "v")
            ctx.wait(v, "v")
            ctx.stamp(v.tensor_copy(rzall, rstdqf), "v")
        rzrdy = ctx.cnt["v"]

        for j in range(8):
            @block.tensor
            def _(t, j=j, rzrdy=rzrdy):
                ctx.wait(t, "v", rzrdy)
                ctx.wait(t, "v")
                for c in range(4):
                    ctx.stamp(t.matmul(c512(P, c), hselB[:, 128 * j:128 * (j + 1)],
                                       c512(rzall, c), start=True, stop=True), "t")

            @block.vector
            def _(v, j=j):
                ctx.wait(v, "t")
                for c in range(4):
                    ctx.stamp(v.tensor_tensor(c512(OtT[j], c), c512(OtT[j], c),
                                              c512(P, c), ALU.mult), "v")

        @block.gpsimd
        def _(g):
            ctx.wait(g, "t")
            for k in range(8):
                ctx.dma(g, "dw", wb[k], wo_io[128 * k:128 * (k + 1), :])
        ldwo = ctx.cnt["dw"]
        rot2 = 0
        pfree2 = []
        for blk in range(8):
            for tc in range(4):
                pv = c512(P, rot2 % 4)

                @block.tensor
                def _(t, blk=blk, tc=tc, pv=pv, rot2=rot2, ldwo=ldwo):
                    ctx.wait(t, "dw", ldwo)
                    ctx.wait(t, "v")
                    if rot2 >= 4:
                        ctx.wait(t, "v", pfree2[rot2 - 4])
                    for k in range(8):
                        ctx.stamp(t.matmul(pv, wb[k][:, 128 * blk:128 * (blk + 1)],
                                           c512(OtT[k], tc),
                                           start=(k == 0), stop=(k == 7)), "t")

                @block.vector
                def _(v, blk=blk, tc=tc, pv=pv):
                    ctx.wait(v, "t")
                    if out_mode == "set":
                        pfree2.append(ctx.stamp(
                            v.tensor_copy(c512(A[blk], tc), pv), "v"))
                    else:
                        pfree2.append(ctx.stamp(
                            v.tensor_tensor(c512(A[blk], tc), c512(A[blk], tc),
                                            pv, ALU.add), "v"))
                rot2 += 1

    @block.sync
    def _(sync):
        ctx.dma(sync, "dma", cosb, io["c_cosg"])
        ctx.dma(sync, "dma", sinb, io["c_sing"])

    rms_norm(n1w, add_A=False)
    project(qT, io["gwq"], wb, 1024, 8, xnT)
    project(latT, io["gkvd"], wb2, 256, 8, xnT)

    @block.gpsimd
    def _(g):
        ctx.wait(g, "t")
        for k in range(2):
            ctx.dma(g, "dw", vupb[k], io["gvup"][128 * k:128 * (k + 1), :])
    ldup = ctx.cnt["dw"]

    @block.tensor
    def _(t, ldup=ldup):
        ctx.wait(t, "dw", ldup)

    project(kT, io["gkup"], kupb, 256, 2, latT)
    build_vnat(vupb, 2, latT)
    rope_qknorm(qT, list(range(8)), rstdq, 16)
    rope_qknorm(kT, [0, 1], rstdk, 4)
    attn_scores_wo("g", io["gwo"], "set")

    @block.sync
    def _(sync):
        ctx.wait(sync, "v")
        ctx.wait(sync, "s")
        ctx.dma(sync, "dma", cosb, io["c_cosl"])
        ctx.dma(sync, "dma", sinb, io["c_sinl"])

    rms_norm(n2w, add_A=True)
    project(qT, io["lwq"], wb, 1024, 8, xnT)
    project(kT, io["lwk"], wb2, 256, 8, xnT)

    @block.gpsimd
    def _(g):
        ctx.wait(g, "t")
        for k in range(8):
            ctx.dma(g, "dw", wb2[k], io["lwv"][128 * k:128 * (k + 1), :])
    ldlv = ctx.cnt["dw"]

    @block.tensor
    def _(t, ldlv=ldlv):
        ctx.wait(t, "dw", ldlv)
        ctx.wait(t, "v")

    build_vnat(wb2, 8, xnT)
    rope_qknorm(qT, list(range(8)), rstdq, 16)
    rope_qknorm(kT, [0, 1], rstdk, 4)
    attn_scores_wo("l", io["lwo"], "add")

    ctx.release()
    ctx.release()




def emit_l1_ffn(ctx, block, io, A, P, Y):
    ctx.mark()
    xTb = [ctx.alloc(f"mx{i}", [128, T], BF16) for i in range(8)]
    w1g = [ctx.alloc(f"mw1_{i}", [128, T], BF16) for i in range(8)]
    w2g = [ctx.alloc(f"mw2_{i}", [128, T], BF16) for i in range(8)]
    cwT = ctx.alloc("mcw", [64, T], BF16)
    rw = ctx.alloc("mrw", [128, 512], BF16)
    rbc = ctx.alloc("mrbc", [64, 1], F32)
    lognat = ctx.alloc("mlog", [128, 1024], F32)
    logstg = ctx.alloc("mlstg", [64, 512], F32)
    b1t = ctx.alloc("mb1t", [128, 128], F32)
    sb1t = [ctx.alloc(f"msb1_{s}", [128, 16], F32) for s in range(2)]
    b2cb = ctx.alloc("mb2", [64, 1024], BF16)
    sb2b = ctx.alloc("msb2", [2, 1024], BF16)
    ones2 = ctx.alloc("mones2", [2, 512], BF16)
    i64b = ctx.alloc("mi64", [64, 64], BF16)
    idnb = ctx.alloc("midn", [128, 128], BF16)
    idnf = ctx.alloc("midnf", [128, 128], F32)
    prob = ctx.alloc("mprob", [128, 64], F32)
    pmax = ctx.alloc("mpmax", [128, 1], F32)
    pden = ctx.alloc("mpden", [128, 1], F32)
    m8 = ctx.alloc("mm8", [128, 8], F32)
    cwn = ctx.alloc("mcwn", [128, 64], BF16)
    ge01 = ctx.alloc("mge", [128, 64], F32)

    @block.gpsimd
    def _(g):
        for i in range(8):
            ctx.dma(g, "dw", xTb[i], io["xt"][128 * i:128 * (i + 1), :])
        ctx.dma(g, "dw", rw.rearrange("p (k e) -> p k e", e=64),
                io["rwT"].rearrange("(k p) e -> p k e", p=128))
        ctx.dma(g, "dw", b2cb, io["b2c"])
        ctx.dma(g, "dw", sb2b, io["sb2c"])
    ldx = ctx.cnt["dw"]

    @block.sync
    def _(sync):
        for nm, dst in (("rbc", rbc), ("b1t", b1t), ("sb1t0", sb1t[0]),
                        ("sb1t1", sb1t[1]), ("c_idn", idnb), ("idnf", idnf),
                        ("c_i64", i64b)):
            ctx.dma(sync, "dma", dst, io[nm])
    ldc = ctx.cnt["dma"]

    @block.vector
    def _(v):
        ctx.stamp(v.memset(ones2, 1.0), "v")

    for tc in range(4):
        @block.tensor
        def _(t, tc=tc):
            ctx.wait(t, "dw", ldx)
            ctx.wait(t, "dma", ldc)
            ctx.wait(t, "v")
            ctx.wait(t, "s")
            for k in range(8):
                ctx.stamp(t.matmul(P[0:64, 512 * tc:512 * (tc + 1)],
                                   rw[:, 64 * k:64 * (k + 1)],
                                   c512(xTb[k], tc),
                                   start=(k == 0), stop=(k == 7)), "t")

        @block.scalar
        def _(s, tc=tc):
            ctx.wait(s, "t")
            ctx.wait(s, "v")
            ctx.stamp(s.activation(logstg, P[0:64, 512 * tc:512 * (tc + 1)],
                                   AF.Identity, bias=rbc), "s")

        for j in range(4):
            @block.tensor
            def _(t, tc=tc, j=j):
                ctx.wait(t, "s")
                ctx.wait(t, "v")
                ctx.stamp(t.transpose(P[0:128, 1536:1600],
                                      logstg[0:64, 128 * j:128 * (j + 1)],
                                      idnf[0:64, 0:64]), "t")

            @block.vector
            def _(v, tc=tc, j=j):
                ctx.wait(v, "t")
                tt = 4 * tc + j
                ctx.stamp(v.tensor_copy(lognat[:, 64 * tt:64 * (tt + 1)],
                                        P[0:128, 1536:1600]), "v")

    for tt in range(16):
        @block.vector
        def _(v, tt=tt):
            ctx.wait(v, "v")
            ctx.wait(v, "s")
            sl = lognat[:, 64 * tt:64 * (tt + 1)]
            ctx.stamp(v.reduce_max(pmax, sl, axis=AX.X), "v")
            ctx.wait(v, "v")
            ctx.stamp(v.tensor_scalar_mul(pmax, pmax, -1.0), "v")

        @block.scalar
        def _(s, tt=tt):
            ctx.wait(s, "v")
            sl = lognat[:, 64 * tt:64 * (tt + 1)]
            ctx.stamp(s.activation(prob, sl, AF.Exp, bias=pmax, accum_out=pden), "s")

        @block.vector
        def _(v, tt=tt):
            ctx.wait(v, "s")
            ctx.stamp(v.reciprocal(pmax, pden), "v")
            ctx.wait(v, "v")
            ctx.stamp(v.tensor_scalar(prob, prob, pmax, None, ALU.mult), "v")
            ctx.wait(v, "v")
            ctx.stamp(v.max(m8, prob), "v")
            ctx.wait(v, "v")
            ctx.stamp(v.tensor_scalar(ge01, prob, m8[:, 7:8], None, ALU.is_ge), "v")
            ctx.stamp(v.reduce_sum(pden, m8, axis=AX.X), "v")
            ctx.wait(v, "v")
            ctx.stamp(v.reciprocal(pden, pden), "v")
            ctx.stamp(v.tensor_tensor(ge01, ge01, prob, ALU.mult), "v")
            ctx.wait(v, "v")
            ctx.stamp(v.tensor_scalar(cwn, ge01, pden, None, ALU.mult), "v")

        @block.tensor
        def _(t, tt=tt):
            ctx.wait(t, "v")
            ctx.stamp(t.transpose(P[0:64, 1536:1600].bitcast(BF16), cwn, idnb), "t")

        @block.vector
        def _(v, tt=tt):
            ctx.wait(v, "t")
            ctx.stamp(v.tensor_copy(cwT[:, 128 * tt:128 * (tt + 1)],
                                    P[0:64, 1536:1600].bitcast(BF16)), "v")

    router_v = ctx.cnt["v"]
    CAP = 384
    H1 = [(P[:, 0:512], P[:, 512:1024]), (P[:, 1024:1536], P[:, 1536:2048])]
    H1S = [[P[:, 0:CAP], P[:, 512:512 + CAP]],
           [P[:, 1024:1024 + CAP], P[:, 1536:1536 + CAP]]]
    Y2 = [[Y[:, 0:512], Y[:, 512:1024]],
          [Y[:, 1024:1536], Y[:, 1536:2048]]]
    CWBv = [Y[:, 0:512], Y[:, 512:1024]]

    spr = ctx.alloc("spr", [128, 39936], U8)

    def spat(name, shape, dt, delta):
        ctx.uid += 1
        return ctx.nc.alloc_sbuf_tensor_at(
            f"{name}_{ctx.uid}", list(shape), dt,
            offset=ctx.offsets["spr"] + delta).ap()

    xgp = [spat(f"xgp{i}", [128, 16 * CAP], BF16, 12288 * i) for i in range(2)]
    silT = [[spat(f"slt{p}{m}", [128, CAP], BF16, 24576 + 768 * (2 * p + m))
             for m in range(2)] for p in range(2)]
    h2sb = [[spat(f"h2s{q}{c}", [128, 1024], BF16, 27648 + 2048 * (3 * q + c))
             for c in range(3)] for q in range(2)]
    def xat(name, shape, dt, base, delta):
        ctx.uid += 1
        return ctx.nc.alloc_sbuf_tensor_at(
            f"{name}_{ctx.uid}", list(shape), dt,
            offset=ctx.offsets[base] + delta).ap()

    h2sb.append([xat("h2s20", [128, 1024], BF16, "mx0", 0),
                 xat("h2s21", [128, 1024], BF16, "mx0", 2048),
                 xat("h2s22", [128, 1024], BF16, "mx1", 0)])
    h2sb.append([xat("h2s30", [128, 1024], BF16, "mx1", 2048),
                 xat("h2s31", [128, 1024], BF16, "mx2", 0),
                 xat("h2s32", [128, 1024], BF16, "mx2", 2048)])
    xgp.append(xat("xgp2", [128, 16 * CAP], BF16, "mx3", 0))
    stash2 = [[[spat(f"msth{p}_{i}_{m}", [128, 512], BF16,
                     1024 * (16 * p + 2 * i + m)) for m in range(2)]
               for i in range(8)] for p in range(2)]
    silu_par = [[spat(f"mslp{p}_{m}", [128, 512], BF16,
                      32768 + 1024 * (2 * p + m)) for m in range(2)]
                for p in range(2)]

    wA = spat("wA", [64, T], F32, 0)
    wM = spat("wM", [64, T], F32, 8192)
    wP = spat("wP", [64, T], F32, 16384)
    wsel16 = ctx.alias("wsel16", [64, T], I16, "mlog")
    lists_nu = ctx.alloc("listsnu", [64, CAP], U32)
    jsel16 = ctx.alloc("jsel16", [64, T], I16)
    tokv = ctx.alloc("tokv", [64, T], I16)
    iotan_b = ctx.alloc("iotanb", [64, CAP], I16)
    pm16 = ctx.alloc("pm16", [64, CAP], I16)
    lists_w = ctx.alloc("listsw", [64, 512], I16)
    lists_n = ctx.alloc("listsn", [64, CAP], I16)
    cwl = ctx.alloc("cwl", [64, CAP], BF16)
    xgp.append(xat("xgp3", [128, 16 * CAP], BF16, "jsel16", 0))
    cwgF = ctx.alloc("cwgF", [128, 3 * 64], F32)
    idxg = [ctx.alloc(f"idxg{i}", [128, 48], I16) for i in range(4)]
    iu32 = [[ctx.alloc(f"iu{i}_{c}", [128, 1], U32) for c in range(3)]
            for i in range(16)]
    xnrow = [ctx.alloc(f"xnr{i}", [128, 1024], BF16) for i in range(2)]
    zrow = ctx.alias("zrow", [128, 1024], BF16, "mlstg")

    GROUPS = [("shared", 0), ("shared", 1)]
    dwmark = {}
    silu_marks = []
    stash_marks = []
    h2_end_t = {0: 0, 1: 0}
    flushv_last = {d: 0 for d in range(4)}

    def load_group_w(gi, which, gate_t):
        if gi >= len(GROUPS):
            return
        kind, s_idx = GROUPS[gi]

        @block.gpsimd
        def _(g, gi=gi, which=which, gate_t=gate_t, kind=kind, s_idx=s_idx):
            ctx.wait(g, "t", gate_t)
            for e in range(8):
                if which == "w1":
                    srcap = io["sw1"][s_idx][:, 256 * e:256 * (e + 1)]
                    ctx.dma(g, "dw", w1g[e].rearrange("p (k h) -> p k h", h=256),
                            srcap.rearrange("(k p) h -> p k h", p=128))
                else:
                    srcap = io["sw2"][s_idx][256 * e:256 * (e + 1), :]
                    ctx.dma(g, "dmo", w2g[e].rearrange("p (k d) -> p k d", d=1024),
                            srcap.rearrange("(k p) d -> p k d", p=128))
                dwmark[(gi, e, which)] = ctx.cnt["dmo" if which == "w2" else "dw"]

    def do_group(gi):
        kind, s_idx = GROUPS[gi]
        for tc in range(4):
            sp = tc % 2
            for e in range(8):
                ep = e % 2

                @block.tensor
                def _(t, gi=gi, e=e, tc=tc, ep=ep,
                      sm=(silu_marks[-2] if len(silu_marks) >= 2 else 0)):
                    ctx.wait(t, "dw", dwmark[(gi, e, "w1")])
                    if sm:
                        ctx.wait(t, "s", sm)
                    else:
                        ctx.wait(t, "v", router_v)
                    for m in range(2):
                        for k in range(8):
                            ctx.stamp(t.matmul(
                                H1[ep][m],
                                w1g[e][:, 256 * k + 128 * m:256 * k + 128 * (m + 1)],
                                c512(xTb[k], tc),
                                start=(k == 0), stop=(k == 7)), "t")
                h1t = ctx.cnt["t"]

                @block.scalar
                def _(s, gi=gi, e=e, ep=ep, s_idx=s_idx, h1t=h1t,
                      stm=(stash_marks[-2] if len(stash_marks) >= 2 else 0)):
                    ctx.wait(s, "t", h1t)
                    if stm:
                        ctx.wait(s, "v", stm)
                    bias0 = sb1t[s_idx][:, 2 * e:2 * e + 1]
                    bias1 = sb1t[s_idx][:, 2 * e + 1:2 * e + 2]
                    ctx.stamp(s.activation(silu_par[ep][0], H1[ep][0], AF.Silu,
                                           bias=bias0), "s")
                    ctx.stamp(s.activation(silu_par[ep][1], H1[ep][1], AF.Silu,
                                           bias=bias1), "s")
                silu_marks.append(ctx.cnt["s"])

                @block.vector
                def _(v, e=e, ep=ep, sp=sp, sm=silu_marks[-1],
                      h2t=h2_end_t[sp]):
                    ctx.wait(v, "s", sm)
                    if h2t:
                        ctx.wait(v, "t", h2t)
                    ctx.stamp(v.tensor_copy(stash2[sp][e][0], silu_par[ep][0]), "v")
                    ctx.stamp(v.tensor_copy(stash2[sp][e][1], silu_par[ep][1]), "v")
                stash_marks.append(ctx.cnt["v"])
            h1done_v = ctx.cnt["v"]
            if tc == 3:
                load_group_w(gi + 1, "w1", ctx.cnt["t"])
                if gi == 1:
                    g1_tc3_t[0] = ctx.cnt["t"]

            for rnd in range(2):
                h2mm = {}
                for dsub in range(4):
                    dout = 4 * rnd + dsub
                    yv = c512(Y, dsub)

                    @block.tensor
                    def _(t, gi=gi, tc=tc, dout=dout, yv=yv, s_idx=s_idx,
                          h1v=h1done_v, sp=sp, fv=flushv_last[dsub],
                          dwm=dwmark[(gi, 7, "w2")]):
                        ctx.wait(t, "dmo", dwm)
                        ctx.wait(t, "v", h1v)
                        if fv:
                            ctx.wait(t, "v", fv)
                        mms = [("w", e, m) for e in range(8) for m in range(2)]
                        if s_idx == 0:
                            mms.append(("sb2",))
                        else:
                            mms.append(("b2",))
                        for i_m, mm in enumerate(mms):
                            st = (i_m == 0)
                            spp = (i_m == len(mms) - 1)
                            if mm[0] == "w":
                                _, e, m = mm
                                ctx.stamp(t.matmul(
                                    yv,
                                    w2g[e][:, 1024 * m + 128 * dout:
                                           1024 * m + 128 * (dout + 1)],
                                    stash2[sp][e][m],
                                    start=st, stop=spp), "t")
                            elif mm[0] == "b2":
                                ctx.stamp(t.matmul(
                                    yv, b2cb[:, 128 * dout:128 * (dout + 1)],
                                    cwT[:, 512 * tc:512 * (tc + 1)],
                                    start=st, stop=spp), "t")
                            else:
                                ctx.stamp(t.matmul(
                                    yv, sb2b[:, 128 * dout:128 * (dout + 1)],
                                    ones2, start=st, stop=spp), "t")
                    h2mm[dsub] = ctx.cnt["t"]

                for dsub in range(4):
                    dout = 4 * rnd + dsub
                    yv = c512(Y, dsub)

                    @block.vector
                    def _(v, tc=tc, dout=dout, yv=yv, first=(gi == 0),
                          hm=h2mm[dsub]):
                        ctx.wait(v, "t", hm)
                        if first:
                            ctx.stamp(v.tensor_copy(c512(A[dout], tc), yv), "v")
                        else:
                            ctx.stamp(v.tensor_tensor(c512(A[dout], tc),
                                                      c512(A[dout], tc), yv,
                                                      ALU.add), "v")
                    flushv_last[dsub] = ctx.cnt["v"]
            h2_end_t[sp] = ctx.cnt["t"]
        load_group_w(gi + 1, "w2", ctx.cnt["t"])

    g1_tc3_t = [0]
    load_group_w(0, "w1", 0)
    load_group_w(0, "w2", 0)

    @block.vector
    def _(v):
        ctx.stamp(v.memset(zrow, 0.0), "v")
    zr_mark = ctx.cnt["v"]

    @block.sync
    def _(sync):
        ctx.wait(sync, "v", zr_mark)
        for r in range(16, 19):
            ctx.dma(sync, "dma", io["xnatb"][128 * r:128 * (r + 1), :], zrow)
        for b in range(2):
            acc = io["acc0" if b == 0 else "acc1"]
            for r in range(19):
                ctx.dma(sync, "dma", acc[128 * r:128 * (r + 1), :], zrow)
        ctx.dma(sync, "dma", tokv, io["iotam"])
        ctx.dma(sync, "dma", iotan_b, io["iotan"])
    zeroed_mark = ctx.cnt["dma"]

    xnat_vm = {}
    xnat_dmam = {}
    for r in range(16):
        rb = r % 2

        @block.tensor
        def _(t, r=r, rb=rb, wv=xnat_vm.get(r - 2, 0)):
            ctx.wait(t, "dw", ldx)
            ctx.wait(t, "dma", ldc)
            if wv:
                ctx.wait(t, "v", wv)
            for k in range(8):
                ctx.stamp(t.transpose(
                    Y[:, 512 * rb + 64 * k:512 * rb + 64 * (k + 1)].bitcast(BF16),
                    xTb[k][:, 128 * r:128 * (r + 1)], idnb), "t")
        tm = ctx.cnt["t"]

        @block.vector
        def _(v, rb=rb, tm=tm, dm=xnat_dmam.get(r - 2, 0)):
            ctx.wait(v, "t", tm)
            if dm:
                ctx.wait(v, "dma", dm)
            ctx.stamp(v.tensor_copy(
                xnrow[rb], Y[:, 512 * rb:512 * (rb + 1)].bitcast(BF16)), "v")
        xnat_vm[r] = ctx.cnt["v"]

        @block.sync
        def _(sync, r=r, rb=rb, vm=xnat_vm[r]):
            ctx.wait(sync, "v", vm)
            ctx.dma(sync, "dma", io["xnatb"][128 * r:128 * (r + 1), :], xnrow[rb])
        xnat_dmam[r] = ctx.cnt["dma"]
    xnat_done = ctx.cnt["dma"]

    @block.vector
    def _(v):
        ctx.stamp(v.memset(wA, 1.0), "v")
        ctx.stamp(v.tensor_scalar(wM, cwT, 1e-8, None, ALU.is_ge), "v")
        ctx.wait(v, "v")
        ctx.wait(v, "v")
        ctx.stamp(v.tensor_tensor_scan(wP, wA, wM, 0.0, ALU.mult, ALU.add), "v")
        ctx.wait(v, "v")
        ctx.stamp(v.tensor_tensor(wP, wP, wM, ALU.subtract), "v")
        ctx.wait(v, "v")
        ctx.stamp(v.tensor_scalar(wA, wP, float(CAP) - 0.5, None, ALU.is_le), "v")
        ctx.wait(v, "v")
        ctx.stamp(v.tensor_tensor(wM, wM, wA, ALU.mult), "v")
        ctx.wait(v, "v")
        ctx.stamp(v.tensor_scalar(wA, wP, 1.0, None, ALU.add), "v")
        ctx.wait(v, "v")
        ctx.stamp(v.tensor_tensor(wA, wA, wM, ALU.mult), "v")
        ctx.wait(v, "v")
        ctx.stamp(v.tensor_scalar(wA, wA, -1.0, None, ALU.add), "v")
        ctx.wait(v, "v")
        ctx.stamp(v.tensor_copy(jsel16, wA), "v")
        ctx.wait(v, "v")
        ctx.stamp(v.tensor_copy(wsel16, wP), "v")
        ctx.wait(v, "v")
        ctx.stamp(v.tensor_scalar(wsel16, wsel16, 15, None,
                                  ALU.bitwise_and), "v")
        ctx.wait(v, "v")
        ctx.stamp(v.tensor_copy(wA, wsel16), "v")
        ctx.wait(v, "v")
        ctx.stamp(v.tensor_tensor(wP, wP, wA, ALU.subtract), "v")
        ctx.wait(v, "v")
        ctx.stamp(v.tensor_scalar(wP, wP, 0.0625, None, ALU.mult), "v")
        ctx.wait(v, "v")
        ctx.stamp(v.tensor_scalar(wA, wA, 32.0, None, ALU.mult), "v")
        ctx.wait(v, "v")
        ctx.stamp(v.tensor_tensor(wP, wP, wA, ALU.add), "v")
        ctx.wait(v, "v")
        ctx.stamp(v.tensor_scalar(wP, wP, 1.0, None, ALU.add), "v")
        ctx.wait(v, "v")
        ctx.stamp(v.tensor_tensor(wP, wP, wM, ALU.mult), "v")
        ctx.wait(v, "v")
        ctx.stamp(v.tensor_scalar(wP, wP, -1.0, None, ALU.add), "v")
        ctx.wait(v, "v")
        ctx.stamp(v.tensor_copy(wsel16, wP), "v")
    sel_mark = ctx.cnt["v"]

    @block.gpsimd
    def _(g):
        ctx.wait(g, "v", sel_mark)
        ctx.wait(g, "dma", zeroed_mark)
        g.load_library(library_config.local_scatter)
        ctx.stamp(g.local_scatter(lists_w, tokv, wsel16, 64, 512, T), "g")
        ctx.stamp(g.local_scatter(lists_n, tokv, jsel16, 64, CAP, T), "g")
        ctx.stamp(g.local_scatter(cwl, cwT, jsel16, 64, CAP, T), "g")
        g.load_library(library_config.mlp)
    ls_mark = ctx.cnt["g"]

    @block.vector
    def _(v):
        ctx.wait(v, "g", ls_mark)
        ctx.stamp(v.tensor_scalar(lists_w, lists_w, 2048.0, None, ALU.add), "v")
        ctx.stamp(v.tensor_scalar(pm16, lists_n, 0.0, None, ALU.is_ge), "v")
        ctx.wait(v, "v")
        ctx.stamp(v.tensor_tensor(pm16, pm16, iotan_b, ALU.mult), "v")
        ctx.wait(v, "v")
        ctx.stamp(v.tensor_tensor(lists_n, lists_n, pm16, ALU.add), "v")
        ctx.wait(v, "v")
        ctx.stamp(v.tensor_scalar(lists_n, lists_n, 2048.0, None, ALU.add), "v")
        ctx.wait(v, "v")
        ctx.stamp(v.tensor_copy(lists_nu, lists_n), "v")
    lfix_mark = ctx.cnt["v"]

    @block.sync
    def _(sync):
        ctx.wait(sync, "v", lfix_mark)
        ctx.dma(sync, "dma", io["wld"], lists_w)
        ctx.dma(sync, "dma", io["nld"], lists_nu)
    lists_dram = ctx.cnt["dma"]

    do_group(0)
    do_group(1)
    shared_end_t = ctx.cnt["t"]
    shared_end_s = ctx.cnt["s"]
    shared_end_v = ctx.cnt["v"]

    @block.tensor
    def _(t):
        ctx.wait(t, "g", ls_mark)
        ctx.wait(t, "s", shared_end_s)
        for c in range(3):
            ctx.stamp(t.transpose(P[:, 32 * c:32 * (c + 1)].bitcast(BF16),
                                  cwl[:, 128 * c:128 * (c + 1)],
                                  idnb[0:64, 0:64]), "t")
    cwg_t = ctx.cnt["t"]

    @block.vector
    def _(v):
        ctx.wait(v, "t", cwg_t)
        for c in range(3):
            ctx.stamp(v.tensor_copy(cwgF[:, 64 * c:64 * (c + 1)],
                                    P[:, 32 * c:32 * (c + 1)].bitcast(BF16)), "v")
    cwg_mark = ctx.cnt["v"]

    exp_dw = {}

    exp_dw2 = {}

    def load_expert_w(e, gate_t, gate_t2=None):
        @block.gpsimd
        def _(g, e=e, gate_t=gate_t, gate_t2=gate_t2):
            if gate_t:
                ctx.wait(g, "t", gate_t)
            ctx.dma(g, "dw", w1g[e % 8].rearrange("p (k h) -> p k h", h=256),
                    io["w1"][e].rearrange("(k p) h -> p k h", p=128))
            if gate_t2:
                ctx.wait(g, "t", gate_t2)
            ctx.dma(g, "dmo", w2g[e % 8].rearrange("p (k d) -> p k d", d=1024),
                    io["w2"][e].rearrange("(k p) d -> p k d", p=128))
        exp_dw[e] = ctx.cnt["dw"]
        exp_dw2[e] = ctx.cnt["dmo"]

    idx_dmam = {}

    def load_pair_idx(q, war_gt):
        @block.sync
        def _(sync, q=q, war_gt=war_gt):
            if q == 0:
                ctx.wait(sync, "dma", lists_dram)
            if war_gt is not None:
                ctx.wait(sync, war_gt[0], war_gt[1])
            for h in range(2):
                ctx.dma(sync, "di", idxg[q % 4][:, 24 * h:24 * (h + 1)],
                        io["wld"][2 * q + h:2 * q + h + 1, :]
                        .rearrange("o (p c) -> o p c", p=16)[:, :, 0:24]
                        .broadcast_to([8, 16, 24]))
        idx_dmam[q] = ctx.cnt["di"]

    iu_dmam = {}

    def load_iu(e, war_sc):
        @block.sync
        def _(sync, e=e, war_sc=war_sc):
            if war_sc is not None:
                ctx.wait(sync, war_sc[0], war_sc[1])
            for c in range(3):
                ctx.dma(sync, "dma", iu32[e % 16][c],
                        io["nld"][e:e + 1, 128 * c:128 * (c + 1)]
                        .rearrange("o (p one) -> (o p) one", p=128))
        iu_dmam[e] = ctx.cnt["dma"]

    gcnt = {0: 0, 1: 0, 2: 0, 3: 0}
    GKEYS = ("gt", "gu", "gv", "gw")

    def issue_gather(q, extra_t=0):
        key = GKEYS[q % 4]

        @block.gpsimd
        def _(g, q=q, key=key, extra_t=extra_t, prior=gcnt[q % 4]):
            if prior > 0:
                ctx.wait(g, key, 16 * prior)
            ctx.wait(g, "di", idx_dmam[q])
            if q < 4:
                ctx.wait(g, "t", shared_end_t)
            if q == 3:
                ctx.wait(g, "t", cwg_t)
            if extra_t:
                ctx.wait(g, "t", extra_t)
            g.dma_gather(xgp[q % 4].rearrange("p (c i) -> p c i", c=8),
                         io["xnatb"], idxg[q % 4], 2 * CAP, 2 * CAP, 1024,
                         transpose=True).then_inc(ctx.sem[key], 16)
        gcnt[q % 4] += 1
        ctx.cnt[key] += 16

    scnt = {0: 0, 1: 0}
    evac_mark = {}

    def issue_scatters(e):
        key = "sc" if e % 2 == 0 else "sd"

        @block.gpsimd
        def _(g, e=e, key=key, prior=scnt[e % 2], em=evac_mark[(e, 2)]):
            ctx.wait(g, "dma", iu_dmam[e])
            ctx.wait(g, "v", em)
            acc = io["acc0"] if e % 2 == 0 else io["acc1"]
            for c in range(3):
                if SIM_SERIAL and 48 * prior + 16 * c > 0:
                    ctx.wait(g, key, 48 * prior + 16 * c)
                elif not SIM_SERIAL and prior > 0 and c == 0:
                    ctx.wait(g, key, 48 * prior)
                g.indirect_dma_start(
                    acc, bass.IndirectOffsetOnAxis(ap=iu32[e % 16][c], axis=0),
                    h2sb[e % 4][c], None,
                    compute_op=ALU.add).then_inc(ctx.sem[key], 16)
        scnt[e % 2] += 1
        ctx.cnt[key] += 48

    for e in range(8):
        load_expert_w(e, g1_tc3_t[0] if e == 0 else 0,
                      gate_t2=shared_end_t if e == 0 else 0)
    for q in range(4):
        load_pair_idx(q, None)
    for e in range(8):
        load_iu(e, None)
    issue_gather(0)
    issue_gather(1)
    issue_gather(2)
    issue_gather(3)
    load_pair_idx(4, ("gt", 16))

    h1_t = {}
    h2_t = {}
    silu_s_mark = {}
    YS = [Y[:, 0:1024], Y[:, 1024:2048]]
    YSh = [[Y[:, 0:512], Y[:, 512:1024]], [Y[:, 1024:1536], Y[:, 1536:2048]]]
    evac_g = {}

    for e in range(64):
        par = e % 2
        q = e // 2

        @block.tensor
        def _(t, e=e, par=par, q=q):
            key = GKEYS[q % 4]
            ctx.wait(t, key, 16 * (q // 4 + 1))
            ctx.wait(t, "dw", exp_dw[e])
            if e >= 2:
                ctx.wait(t, "s", silu_s_mark[e - 2])
            else:
                ctx.wait(t, "s", shared_end_s)
                ctx.wait(t, "v", cwg_mark)
            for m in range(2):
                for k in range(8):
                    ctx.stamp(t.matmul(
                        H1S[par][m],
                        w1g[e % 8][:, 256 * k + 128 * m:256 * k + 128 * (m + 1)],
                        xgp[q % 4][:, 768 * k + CAP * par:
                                   768 * k + CAP * (par + 1)],
                        start=(k == 0), stop=(k == 7)), "t")
        h1_t[e] = ctx.cnt["t"]

        @block.scalar
        def _(s, e=e, par=par, h1m=h1_t[e]):
            ctx.wait(s, "t", h1m)
            if e >= 2:
                ctx.wait(s, "t", h2_t[e - 2])
            for m in range(2):
                ctx.stamp(s.activation(
                    silT[par][m], H1S[par][m], AF.Silu,
                    bias=b1t[:, 64 * m + e:64 * m + e + 1]), "s")
        silu_s_mark[e] = ctx.cnt["s"]

        for c in range(3):
            g_idx = 3 * e + c
            slot = g_idx % 2

            @block.tensor
            def _(t, e=e, c=c, slot=slot, sm=silu_s_mark[e],
                  emp=evac_g.get(g_idx - 2, 0)):
                ctx.wait(t, "s", sm)
                if c == 0:
                    ctx.wait(t, "dmo", exp_dw2[e])
                if emp:
                    ctx.wait(t, "v", emp)
                else:
                    ctx.wait(t, "v", shared_end_v)
                for half in range(2):
                    for kk in range(2):
                        ctx.stamp(t.matmul(
                            YSh[slot][half],
                            silT[e % 2][kk][:, 128 * c:128 * (c + 1)],
                            w2g[e % 8][:, 1024 * kk + 512 * half:
                                       1024 * kk + 512 * (half + 1)],
                            start=(kk == 0), stop=(kk == 1)), "t")
            h2c = ctx.cnt["t"]
            if c == 2:
                h2_t[e] = h2c

            @block.vector
            def _(v, e=e, par=par, c=c, slot=slot, h2c=h2c):
                ctx.wait(v, "t", h2c)
                if e < 2:
                    ctx.wait(v, "v", cwg_mark)
                if e >= 4:
                    ctx.wait(v, "sc" if par == 0 else "sd",
                             48 * (e // 2 - 1))
                ctx.stamp(v.tensor_scalar(
                    h2sb[e % 4][c], YS[slot],
                    cwgF[:, 64 * c + e:64 * c + e + 1], None, ALU.mult), "v")
            evac_mark[(e, c)] = ctx.cnt["v"]
            evac_g[g_idx] = ctx.cnt["v"]

        if par == 0 and e >= 2 and q + 3 < 32:
            issue_gather(q + 3, extra_t=h1_t[e - 1])
        if e >= 1:
            issue_scatters(e - 1)
        if e + 8 < 64:
            load_expert_w(e + 8, h2_t[e])
        if par == 1 and q + 5 < 32:
            qq = q + 5
            load_pair_idx(qq, (GKEYS[qq % 4], 16 * ((qq - 4) // 4 + 1)))
        if e + 8 < 64:
            load_iu(e + 8, ("sc" if e % 2 == 0 else "sd",
                            48 * ((e - 8) // 2 + 1)) if e >= 8 else None)

    issue_scatters(63)

    ctx.release()


def emit_acc_merge(ctx, block, io, zfull, P, Y):
    ctx.mark()
    acr = [[ctx.alloc(f"acr{b}{i}", [128, 1024], BF16) for i in range(2)]
           for b in range(2)]
    acs = [ctx.alloc(f"acs{i}", [128, 1024], BF16) for i in range(2)]
    idnm = ctx.alloc("idnm", [128, 128], BF16)

    @block.sync
    def _(sync):
        ctx.dma(sync, "dma", idnm, io["c_idn"])

    @block.tensor
    def _(t):
        ctx.wait(t, "s")
        ctx.wait(t, "dma")

    vmadd = {}
    vmsum = {}
    dmam = {}
    for r in range(16):
        rb = r % 2

        @block.sync
        def _(sync, r=r, rb=rb, wv=vmsum.get(r - 2, 0)):
            if wv:
                ctx.wait(sync, "v", wv)
            ctx.dma(sync, "dma", acr[0][rb], io["acc0"][128 * r:128 * (r + 1), :])
            ctx.dma(sync, "dma", acr[1][rb], io["acc1"][128 * r:128 * (r + 1), :])
        dmam[r] = ctx.cnt["dma"]

        @block.vector
        def _(v, rb=rb, dm=dmam[r]):
            ctx.wait(v, "dma", dm)
            ctx.stamp(v.tensor_tensor(acs[rb], acr[0][rb], acr[1][rb],
                                      ALU.add), "v")
        vmsum[r] = ctx.cnt["v"]

        @block.tensor
        def _(t, rb=rb, vm=vmsum[r], wv=vmadd.get(r - 2, 0)):
            ctx.wait(t, "v", vm)
            if wv:
                ctx.wait(t, "v", wv)
            for k in range(8):
                ctx.stamp(t.transpose(
                    Y[:, 512 * rb + 64 * k:512 * rb + 64 * (k + 1)].bitcast(BF16),
                    acs[rb][:, 128 * k:128 * (k + 1)], idnm), "t")
        tm = ctx.cnt["t"]

        @block.vector
        def _(v, r=r, rb=rb, tm=tm):
            ctx.wait(v, "t", tm)
            for k in range(8):
                sl = zfull[k][:, 128 * r:128 * (r + 1)]
                ctx.stamp(v.tensor_tensor(
                    sl, sl,
                    Y[:, 512 * rb + 64 * k:512 * rb + 64 * (k + 1)].bitcast(BF16),
                    ALU.add), "v")
        vmadd[r] = ctx.cnt["v"]
    ctx.release()




def build_p0():
    nc = bass.Bass("TRN2", target_bir_lowering=False)
    shapes = {
        "xt": (D, T), "dtwT": (D, D), "BwT": (D, 32), "CwT": (D, 32),
        "dtb": (128, 8), "Bb": (32, 1), "Cb": (32, 1), "Dsk": (128, 8),
        "logAcol": (128, 256),
    }
    io = {}
    for name, shp in shapes.items():
        io[name] = nc.dram_tensor(name, list(shp), F32, kind="ExternalInput").ap()
    for name, arr in _CONSTS.items():
        io[name] = nc.dram_tensor(name, list(arr.shape), BF16, kind="ExternalInput").ap()
    out = nc.dram_tensor("xt_out", [D, T], F32, kind="ExternalOutput").ap()
    ctx = Ctx(nc)
    P = nc.alloc_psum_tensor("P", [128, T], F32).ap()
    Y = nc.alloc_psum_tensor("Y", [128, T], F32).ap()
    if True:
        block = DeferBlock()
        ctx.register_consts(block, [1.0, 1e-5])
        ctx.mark()
        A = [ctx.alloc(f"A{i}", [128, T], BF16) for i in range(8)]
        emit_l0_attn_core(ctx, block, io, A, P, Y)
        ctx.mark()
        zfull = [ctx.alloc(f"zf{i}", [128, T], F32) for i in range(8)]

        @block.sync
        def _(sync):
            ctx.wait(sync, "v")
            ctx.wait(sync, "t")
            ctx.wait(sync, "s")
            for i in range(8):
                ctx.dma(sync, "dma", zfull[i], io["xt"][128 * i:128 * (i + 1), :])

        @block.vector
        def _(v):
            ctx.wait(v, "dma")

        emit_ln(ctx, block, zfull, A, P, Y, 1.0, out)
        ctx.release()
        ctx.release()

        @block.gpsimd
        def _(g):
            ctx.wait(g, "dmo")
        block.replay(nc)
    mybir.codegen_inst_isa_subclasses(nc)
    return ctx, nc


def build_p1():
    nc = bass.Bass("TRN2", target_bir_lowering=False)
    io = {}
    io["xt"] = nc.dram_tensor("xt", [D, T], F32, kind="ExternalInput").ap()
    io["wblk"] = nc.dram_tensor("wblk", [D, D], F32, kind="ExternalInput").ap()
    io["wr8"] = nc.dram_tensor("wr8", [128, 8], F32, kind="ExternalInput").ap()
    for name, arr in _CONSTS.items():
        io[name] = nc.dram_tensor(name, list(arr.shape), BF16, kind="ExternalInput").ap()
    out = nc.dram_tensor("xt_out", [D, T], F32, kind="ExternalOutput").ap()
    ctx = Ctx(nc)
    P = nc.alloc_psum_tensor("P", [128, T], F32).ap()
    Y = nc.alloc_psum_tensor("Y", [128, T], F32).ap()
    if True:
        block = DeferBlock()
        ctx.register_consts(block, [1.0, 1e-5])
        ctx.mark()
        A = [ctx.alloc(f"A{i}", [128, T], BF16) for i in range(8)]
        zfull = [ctx.alloc(f"zf{i}", [128, T], F32) for i in range(8)]

        @block.sync
        def _(sync):
            for i in range(8):
                ctx.dma(sync, "dma", zfull[i], io["xt"][128 * i:128 * (i + 1), :])

        @block.vector
        def _(v):
            ctx.wait(v, "dma")
            ctx.stamp(v.engine_nop(), "v")

        emit_l0_ffn(ctx, block, io, A, zfull, P, Y)
        emit_ln(ctx, block, zfull, A, P, Y, 1.0, out)
        ctx.release()

        @block.gpsimd
        def _(g):
            ctx.wait(g, "dmo")
        block.replay(nc)
    mybir.codegen_inst_isa_subclasses(nc)
    return ctx, nc


def build_p2():
    nc = bass.Bass("TRN2", target_bir_lowering=False)
    io = {}
    io["xt"] = nc.dram_tensor("xt", [D, T], F32, kind="ExternalInput").ap()
    for nm, shp in (("n1w", (128, 8)), ("n2w", (128, 8)),
                    ("gwq", (D, D)), ("gkvd", (D, 256)), ("gkup", (256, 256)),
                    ("gvup", (256, 256)), ("gwo", (D, D)), ("lwq", (D, D)),
                    ("lwk", (D, 256)), ("lwv", (D, 256)), ("lwo", (D, D))):
        io[nm] = nc.dram_tensor(nm, list(shp), F32, kind="ExternalInput").ap()
    for name, arr in _CONSTS.items():
        io[name] = nc.dram_tensor(name, list(arr.shape), BF16, kind="ExternalInput").ap()
    out = nc.dram_tensor("xt_out", [D, T], F32, kind="ExternalOutput").ap()
    ctx = Ctx(nc)
    P = nc.alloc_psum_tensor("P", [128, T], F32).ap()
    Y = nc.alloc_psum_tensor("Y", [128, T], F32).ap()
    if True:
        block = DeferBlock()
        ctx.register_consts(block, [1.0, 1e-5, 1e-6])
        ctx.mark()
        A = [ctx.alloc(f"A{i}", [128, T], BF16) for i in range(8)]
        emit_l1_attn(ctx, block, io, A, P, Y)
        ctx.mark()
        zfull = [ctx.alloc(f"zf{i}", [128, T], F32) for i in range(8)]

        @block.sync
        def _(sync):
            ctx.wait(sync, "v")
            ctx.wait(sync, "t")
            ctx.wait(sync, "s")
            for i in range(8):
                ctx.dma(sync, "dma", zfull[i], io["xt"][128 * i:128 * (i + 1), :])

        @block.vector
        def _(v):
            ctx.wait(v, "dma")
            ctx.stamp(v.engine_nop(), "v")

        emit_ln(ctx, block, zfull, A, P, Y, 2.0, out)
        ctx.release()
        ctx.release()

        @block.gpsimd
        def _(g):
            ctx.wait(g, "dmo")
        block.replay(nc)
    mybir.codegen_inst_isa_subclasses(nc)
    return ctx, nc


def build_p3():
    nc = bass.Bass("TRN2", target_bir_lowering=False)
    io = {}
    io["xt"] = nc.dram_tensor("xt", [D, T], F32, kind="ExternalInput").ap()
    for nm, shp in (("rwT", (D, 64)), ("rbc", (64, 1)), ("w1", (64, D, 256)),
                    ("w2", (64, 256, D)), ("b1t", (128, 128)), ("b2c", (64, D)),
                    ("sw1", (2, D, 2048)), ("sw2", (2, 2048, D)),
                    ("sb1t0", (128, 16)), ("sb1t1", (128, 16)),
                    ("sb2c", (2, D)), ("idnf", (128, 128))):
        io[nm] = nc.dram_tensor(nm, list(shp), F32, kind="ExternalInput").ap()
    for name, arr in _CONSTS.items():
        io[name] = nc.dram_tensor(name, list(arr.shape), BF16,
                                  kind="ExternalInput").ap()
    io["iotam"] = nc.dram_tensor("iotam", [64, T], I16,
                                 kind="ExternalInput").ap()
    io["iotan"] = nc.dram_tensor("iotan", [64, 384], I16,
                                 kind="ExternalInput").ap()
    io["xnatb"] = nc.dram_tensor("xnatb", [2432, 1024], BF16,
                                 kind="Internal").ap()
    io["acc0"] = nc.dram_tensor("acc0", [2432, 1024], BF16,
                                kind="Internal").ap()
    io["acc1"] = nc.dram_tensor("acc1", [2432, 1024], BF16,
                                kind="Internal").ap()
    io["wld"] = nc.dram_tensor("wld", [64, 512], I16, kind="Internal").ap()
    io["nld"] = nc.dram_tensor("nld", [64, 384], U32, kind="Internal").ap()
    out = nc.dram_tensor("xt_out", [D, T], F32, kind="ExternalOutput").ap()
    ctx = Ctx(nc)
    P = nc.alloc_psum_tensor("P", [128, T], F32).ap()
    Y = nc.alloc_psum_tensor("Y", [128, T], F32).ap()
    if True:
        block = DeferBlock()
        ctx.register_consts(block, [1.0, 1e-5, 1e-6])
        ctx.mark()
        A = [ctx.alloc(f"A{i}", [128, T], BF16) for i in range(8)]
        emit_l1_ffn(ctx, block, io, A, P, Y)
        ctx.mark()
        zfull = [ctx.alloc(f"zf{i}", [128, T], F32) for i in range(8)]

        @block.sync
        def _(sync):
            ctx.wait(sync, "v")
            ctx.wait(sync, "t")
            ctx.wait(sync, "s")
            ctx.wait(sync, "sc")
            ctx.wait(sync, "sd")
            ctx.wait(sync, "gt")
            ctx.wait(sync, "gu")
            ctx.wait(sync, "gv")
            ctx.wait(sync, "gw")
            for i in range(8):
                ctx.dma(sync, "dma", zfull[i], io["xt"][128 * i:128 * (i + 1), :])

        @block.vector
        def _(v):
            ctx.wait(v, "dma")
            ctx.stamp(v.engine_nop(), "v")

        emit_acc_merge(ctx, block, io, zfull, P, Y)
        emit_ln(ctx, block, zfull, A, P, Y, 1.0, out)
        ctx.release()
        ctx.release()

        @block.gpsimd
        def _(g):
            ctx.wait(g, "dmo")
        block.replay(nc)
    mybir.codegen_inst_isa_subclasses(nc)
    return ctx, nc


def barrier(ctx, block):
    marks = {k: ctx.cnt[k] for k in ("t", "v", "s", "dma", "dw", "dmo")}

    @block.tensor
    def _(t):
        for k in ("v", "s", "dma", "dw", "dmo"):
            ctx.wait(t, k, marks[k])

    @block.vector
    def _(v):
        for k in ("t", "s", "dma", "dw", "dmo"):
            ctx.wait(v, k, marks[k])

    @block.scalar
    def _(s):
        for k in ("t", "v", "dma", "dw", "dmo"):
            ctx.wait(s, k, marks[k])

    @block.sync
    def _(sync):
        for k in ("t", "v", "s", "dw", "dmo"):
            ctx.wait(sync, k, marks[k])

    @block.gpsimd
    def _(g):
        for k in ("t", "v", "s", "dma", "dmo"):
            ctx.wait(g, k, marks[k])


def build_fused():
    nc = bass.Bass("TRN2", target_bir_lowering=False)
    io = {}
    shapes = {
        "xt": (D, T), "dtwT": (D, D), "BwT": (D, 32), "CwT": (D, 32),
        "dtb": (128, 8), "Bb": (32, 1), "Cb": (32, 1), "Dsk": (128, 8),
        "logAcol": (128, 256),
        "wblk": (D, D), "wr8": (128, 8),
        "n1w": (128, 8), "n2w": (128, 8), "gwq": (D, D), "gkvd": (D, 256),
        "gkup": (256, 256), "gvup": (256, 256), "gwo": (D, D), "lwq": (D, D),
        "lwk": (D, 256), "lwv": (D, 256), "lwo": (D, D),
        "rwT": (D, 64), "rbc": (64, 1), "w1": (64, D, 256), "w2": (64, 256, D),
        "b1t": (128, 128), "b2c": (64, D), "sw1": (2, D, 2048),
        "sw2": (2, 2048, D), "sb1t0": (128, 16), "sb1t1": (128, 16),
        "sb2c": (2, D), "idnf": (128, 128),
    }
    for nm, shp in shapes.items():
        io[nm] = nc.dram_tensor(nm, list(shp), F32, kind="ExternalInput").ap()
    for name, arr in _CONSTS.items():
        io[name] = nc.dram_tensor(name, list(arr.shape), BF16,
                                  kind="ExternalInput").ap()
    io["iotam"] = nc.dram_tensor("iotam", [64, T], I16,
                                 kind="ExternalInput").ap()
    io["iotan"] = nc.dram_tensor("iotan", [64, 384], I16,
                                 kind="ExternalInput").ap()
    io["xnatb"] = nc.dram_tensor("xnatb", [2432, 1024], BF16,
                                 kind="Internal").ap()
    io["acc0"] = nc.dram_tensor("acc0", [2432, 1024], BF16,
                                kind="Internal").ap()
    io["acc1"] = nc.dram_tensor("acc1", [2432, 1024], BF16,
                                kind="Internal").ap()
    io["wld"] = nc.dram_tensor("wld", [64, 512], I16, kind="Internal").ap()
    io["nld"] = nc.dram_tensor("nld", [64, 384], U32, kind="Internal").ap()
    out = nc.dram_tensor("xt_out", [D, T], F32, kind="ExternalOutput").ap()
    xmir = nc.dram_tensor("xmir", [D, T], F32, kind="Internal").ap()

    ctx = Ctx(nc)
    P = nc.alloc_psum_tensor("P", [128, T], F32).ap()
    Y = nc.alloc_psum_tensor("Y", [128, T], F32).ap()

    def ln_stage(io_x, A, coef, out_dram):
        ctx.mark()
        zfull = [ctx.alloc(f"zf{i}", [128, T], F32) for i in range(8)]

        @block.sync
        def _(sync):
            ctx.wait(sync, "v")
            ctx.wait(sync, "t")
            ctx.wait(sync, "s")
            for i in range(8):
                ctx.dma(sync, "dma", zfull[i], io_x[128 * i:128 * (i + 1), :])

        @block.vector
        def _(v):
            ctx.wait(v, "dma")
            ctx.stamp(v.engine_nop(), "v")

        emit_ln(ctx, block, zfull, A, P, Y, coef, out_dram)
        ctx.release()

    if True:
        block = DeferBlock()
        ctx.register_consts(block, [1.0, 1e-5, 1e-6])

        ctx.mark()
        A = [ctx.alloc(f"A0_{i}", [128, T], BF16) for i in range(8)]
        emit_l0_attn_core(ctx, block, io, A, P, Y)
        ln_stage(io["xt"], A, 1.0, xmir)
        ctx.release()
        barrier(ctx, block)

        io1 = dict(io)
        io1["xt"] = xmir
        ctx.mark()
        A = [ctx.alloc(f"A1_{i}", [128, T], BF16) for i in range(8)]
        zfull = [ctx.alloc(f"z1_{i}", [128, T], F32) for i in range(8)]

        @block.sync
        def _(sync):
            for i in range(8):
                ctx.dma(sync, "dma", zfull[i], xmir[128 * i:128 * (i + 1), :])

        @block.vector
        def _(v):
            ctx.wait(v, "dma")
            ctx.stamp(v.engine_nop(), "v")

        emit_l0_ffn(ctx, block, io1, A, zfull, P, Y)
        emit_ln(ctx, block, zfull, A, P, Y, 1.0, xmir)
        ctx.release()
        barrier(ctx, block)

        ctx.mark()
        A = [ctx.alloc(f"A2_{i}", [128, T], BF16) for i in range(8)]
        emit_l1_attn(ctx, block, io1, A, P, Y)
        ln_stage(xmir, A, 2.0, xmir)
        ctx.release()
        barrier(ctx, block)

        ctx.mark()
        A = [ctx.alloc(f"A3_{i}", [128, T], BF16) for i in range(8)]
        emit_l1_ffn(ctx, block, io1, A, P, Y)
        ctx.mark()
        zfull = [ctx.alloc(f"zf3_{i}", [128, T], F32) for i in range(8)]

        @block.sync
        def _(sync):
            ctx.wait(sync, "v")
            ctx.wait(sync, "t")
            ctx.wait(sync, "s")
            ctx.wait(sync, "sc")
            ctx.wait(sync, "sd")
            ctx.wait(sync, "gt")
            ctx.wait(sync, "gu")
            ctx.wait(sync, "gv")
            ctx.wait(sync, "gw")
            for i in range(8):
                ctx.dma(sync, "dma", zfull[i], xmir[128 * i:128 * (i + 1), :])

        @block.vector
        def _(v):
            ctx.wait(v, "dma")
            ctx.stamp(v.engine_nop(), "v")

        emit_acc_merge(ctx, block, io1, zfull, P, Y)
        emit_ln(ctx, block, zfull, A, P, Y, 1.0, out)
        ctx.release()
        ctx.release()

        @block.gpsimd
        def _(g):
            ctx.wait(g, "dmo")
        block.replay(nc)
    mybir.codegen_inst_isa_subclasses(nc)
    return ctx, nc


def kernel(**inputs):
    n = 8
    base = {}
    base.update(prep_l0ffn(inputs, 0))
    base.update(prep_l1attn(inputs, 0))
    base.update(prep_l1ffn(inputs, 0))
    feeds = []
    for b in range(n):
        f = dict(base)
        f.update(prep_l0attn(inputs, b))
        feeds.append(f)
    ctx, nc = build_fused()
    res = run_bass_kernel_spmd(nc, feeds, core_ids=list(range(n)))
    out = np.stack([res.results[b]["xt_out"].T for b in range(n)])
    return np.ascontiguousarray(out)
